# version 01
# baseline (speedup 1.0000x reference)
import contextlib
import numpy as np
import ml_dtypes
import concourse.bass as bass
import concourse.mybir as mybir
from concourse.bass_utils import run_bass_kernel_spmd

F32 = mybir.dt.float32
BF16 = mybir.dt.bfloat16
U32 = mybir.dt.uint32
OP = mybir.AluOpType
AF = mybir.ActivationFunctionType

NCORES = 8
BPC = 4
T = 2048
TOK = BPC * T
DM = 1024
NT = TOK // 128
H = 16
DH = 64
NHP = H // 2
NQT = T // 128
NE = 16
CAPL = 1024
CG = 4096
DFF = 4096
PADL, PADM, PADR = 127, 129, 191
PADW = PADL + PADM + PADR
DISP_ROWS = NE * CAPL
DISP_ROWS_PAD = DISP_ROWS + 128
EPS = 1e-5


class DSem2:
    def __init__(self, es, nc, name):
        self.sems = [es.enter_context(nc.semaphore(f"{name}{i}")) for i in range(2)]
        self.cnt = [0, 0]

    def inc(self, inst, t):
        inst.then_inc(self.sems[t % 2], 16)
        self.cnt[t % 2] += 16

    def wait_tile(self, eng, t, per=1):
        eng.wait_ge(self.sems[t % 2], 16 * per * (t // 2 + 1))

    def wait_all(self, eng):
        eng.wait_ge(self.sems[0], self.cnt[0])
        eng.wait_ge(self.sems[1], self.cnt[1])


class Sem:
    def __init__(self, es, nc, name):
        self.sem = es.enter_context(nc.semaphore(name))
        self.n = 0

    def inc(self, inst, k=1):
        inst.then_inc(self.sem, k)
        self.n += k
        return self.n

    def dinc(self, inst):
        inst.then_inc(self.sem, 16)
        self.n += 16
        return self.n

    def wait(self, eng, n=None):
        eng.wait_ge(self.sem, self.n if n is None else n)


def build_nc(debug=False, max_phase=9, nt_lim=None, ln_parts=15, ln_sk=0,
             bpc_lim=None, nhp_lim=None, ne_lim=None):
    NTL = NT if nt_lim is None else nt_lim
    BPCL = BPC if bpc_lim is None else bpc_lim
    NHPL = NHP if nhp_lim is None else nhp_lim
    NEL = NE if ne_lim is None else ne_lim
    nc = bass.Bass("TRN2", target_bir_lowering=False, debug=False)

    xq = nc.dram_tensor("xq", [TOK, DM], F32, kind="ExternalInput")
    wqkvT = nc.dram_tensor("wqkvT", [DM, 3 * DM], F32, kind="ExternalInput")
    bqkv = nc.dram_tensor("bqkv", [3 * DM], F32, kind="ExternalInput")
    outwT = nc.dram_tensor("outwT", [DM, DM], F32, kind="ExternalInput")
    outb = nc.dram_tensor("outb", [DM], F32, kind="ExternalInput")
    peT = nc.dram_tensor("peT", [DH, 129], F32, kind="ExternalInput")
    wg = nc.dram_tensor("wg", [DM, NE], F32, kind="ExternalInput")
    w1 = nc.dram_tensor("w1", [NE, DM, DFF], F32, kind="ExternalInput")
    w2 = nc.dram_tensor("w2", [NE, DFF, DM], F32, kind="ExternalInput")
    identb = nc.dram_tensor("identb", [128, 128], BF16, kind="ExternalInput")
    identf = nc.dram_tensor("identf", [128, 128], F32, kind="ExternalInput")
    triu = nc.dram_tensor("triu", [64, 64], F32, kind="ExternalInput")
    corehot = nc.dram_tensor("corehot", [1, 8], F32, kind="ExternalInput")
    prefmask = nc.dram_tensor("prefmask", [8, 1], F32, kind="ExternalInput")
    sel2 = nc.dram_tensor("sel2", [128, 2], BF16, kind="ExternalInput")

    y = nc.dram_tensor("y", [TOK, DM], F32, kind="ExternalOutput")

    xnbH = nc.dram_tensor("xnbH", [TOK, DM], BF16, kind="Internal")
    xnTH = nc.dram_tensor("xnTH", [8, 128, TOK], BF16, kind="Internal")
    qkH = nc.dram_tensor("qkH", [16, 128, T], BF16, kind="Internal")
    padH = nc.dram_tensor("padH", [2, 2, NQT, 128, PADW + 1], BF16, kind="Internal")
    dispH = nc.dram_tensor("dispH", [DISP_ROWS_PAD, DM], BF16, kind="Internal")
    out2H = nc.dram_tensor("out2H", [DISP_ROWS, DM], F32, kind="Internal")
    attnH = nc.dram_tensor("attnH", [TOK, DM], F32, kind="Internal")
    OTH = nc.dram_tensor("OTH", [8, 128, T], BF16, kind="Internal")
    vH = nc.dram_tensor("vH", [16, 128, DM], BF16, kind="Internal")
    arinH = nc.dram_tensor("arinH", [8, NE], F32, kind="Internal")
    aroutH = nc.dram_tensor("aroutH", [8, NE], F32, kind="Internal")

    dbg = {}
    if debug:
        for name, shape, dt in [
            ("dbg_xnb", [TOK, DM], BF16),
            ("dbg_idx", [128, NT], F32),
            ("dbg_gate", [128, NT], F32),
            ("dbg_loc", [128, NT], F32),
            ("dbg_disp", [512, DM], BF16),
            ("dbg_xnT", [128, TOK], BF16),
            ("dbg_attn", [T, DM], F32),
        ]:
            dbg[name] = nc.dram_tensor(name, shape, dt, kind="ExternalOutput")

    es = contextlib.ExitStack()
    with es:
        s_identb = es.enter_context(nc.sbuf_tensor("s_identb", [128, 128], BF16))
        s_identf = es.enter_context(nc.sbuf_tensor("s_identf", [128, 128], F32))
        s_sel2 = es.enter_context(nc.sbuf_tensor("s_sel2", [128, 2], BF16))
        s_triu = es.enter_context(nc.sbuf_tensor("s_triu", [64, 64], F32))
        s_corehot = es.enter_context(nc.sbuf_tensor("s_corehot", [1, 8], F32))
        s_prefmask = es.enter_context(nc.sbuf_tensor("s_prefmask", [8, 1], F32))
        s_wg = es.enter_context(nc.sbuf_tensor("s_wg", [128, 8, NE], F32))
        s_onesb = es.enter_context(nc.sbuf_tensor("s_onesb", [128, 512], BF16))
        s_onesf = es.enter_context(nc.sbuf_tensor("s_onesf", [128, 8], F32))
        s_onesfr = es.enter_context(nc.sbuf_tensor("s_onesfr", [1, 128], F32))
        s_biasrow = es.enter_context(nc.sbuf_tensor("s_biasrow", [1, 3 * DM], BF16))
        s_outbrow = es.enter_context(nc.sbuf_tensor("s_outbrow", [1, DM], BF16))
        s_peT2 = es.enter_context(nc.sbuf_tensor("s_peT2", [128, 129], BF16))
        swqkv = es.enter_context(nc.sbuf_tensor("swqkv", [128, 8, 3 * DM], BF16))
        soutw = es.enter_context(nc.sbuf_tensor("soutw", [128, 8, DM], BF16))
        s_eps = es.enter_context(nc.sbuf_tensor("s_eps", [128, 1], F32))
        s_idxc = es.enter_context(nc.sbuf_tensor("s_idxc", [128, NT], F32))
        s_gatec = es.enter_context(nc.sbuf_tensor("s_gatec", [128, NT], F32))
        s_locc = es.enter_context(nc.sbuf_tensor("s_locc", [128, NT], F32))
        s_dc = es.enter_context(nc.sbuf_tensor("s_dc", [128, NT], U32))
        s_gB = es.enter_context(nc.sbuf_tensor("s_gB", [128, NT], U32))
        s_offs = es.enter_context(nc.sbuf_tensor("s_offs", [1, NE], F32))

        block = es.enter_context(nc.Block())

        dG0 = Sem(es, nc, "dG0")
        dS0 = Sem(es, nc, "dS0")
        T0 = Sem(es, nc, "T0")
        V0 = Sem(es, nc, "V0")
        S0 = Sem(es, nc, "S0")
        G0 = Sem(es, nc, "G0")

        @block.gpsimd
        def _(g):
            g.dma_start(out=s_biasrow[:], in_=bass.AP(bqkv, 0, [[3 * DM, 1], [1, 3 * DM]])).then_inc(dG0.sem, 16)
            g.dma_start(out=s_outbrow[:], in_=bass.AP(outb, 0, [[DM, 1], [1, DM]])).then_inc(dG0.sem, 16)
            g.dma_start(out=s_peT2[0:64, :], in_=peT.ap()).then_inc(dG0.sem, 16)
            g.dma_start(out=s_peT2[64:128, :], in_=peT.ap()).then_inc(dG0.sem, 16)
            g.dma_start(out=swqkv[:], in_=bass.AP(wqkvT, 0, [[3 * DM, 128], [128 * 3 * DM, 8], [1, 3 * DM]])).then_inc(dG0.sem, 16)
            g.dma_start(out=soutw[:], in_=bass.AP(outwT, 0, [[DM, 128], [128 * DM, 8], [1, DM]])).then_inc(dG0.sem, 16)
            dG0.n += 96

        @block.sync
        def _(s):
            s.dma_start(out=s_identb[:], in_=identb.ap()).then_inc(dS0.sem, 16)
            s.dma_start(out=s_identf[:], in_=identf.ap()).then_inc(dS0.sem, 16)
            s.dma_start(out=s_sel2[:], in_=sel2.ap()).then_inc(dS0.sem, 16)
            s.dma_start(out=s_triu[:], in_=triu.ap()).then_inc(dS0.sem, 16)
            s.dma_start(out=s_corehot[:], in_=corehot.ap()).then_inc(dS0.sem, 16)
            s.dma_start(out=s_prefmask[:], in_=prefmask.ap()).then_inc(dS0.sem, 16)
            s.dma_start(out=s_wg[:], in_=bass.AP(wg, 0, [[NE, 128], [128 * NE, 8], [1, NE]])).then_inc(dS0.sem, 16)
            dS0.n += 7 * 16


        @block.vector
        def _(v):
            v.memset(s_onesb[:], 1.0).then_inc(V0.sem, 1)
            v.memset(s_onesf[:], 1.0).then_inc(V0.sem, 1)
            v.memset(s_eps[:], EPS).then_inc(V0.sem, 1)
            v.memset(s_onesfr[:], 1.0).then_inc(V0.sem, 1)
            V0.n += 4

        @block.scalar
        def _(sc):
            pass




        dX = DSem2(es, nc, "dX")
        dWb = DSem2(es, nc, "dWb")
        dWT = DSem2(es, nc, "dWT")
        V1 = Sem(es, nc, "V1")
        S1 = Sem(es, nc, "S1")
        T1 = Sem(es, nc, "T1")

        if max_phase >= 1:
         with contextlib.ExitStack() as ph1:
            sx = [ph1.enter_context(nc.sbuf_tensor(f"sx{i}", [128, DM], F32)) for i in range(2)]
            sxn = ph1.enter_context(nc.sbuf_tensor("sxn", [128, DM], F32))
            sbn = ph1.enter_context(nc.sbuf_tensor("sbn", [128, 12], F32))
            smv = ph1.enter_context(nc.sbuf_tensor("smv", [128, 2], F32))
            ssd = ph1.enter_context(nc.sbuf_tensor("ssd", [128, 1], F32))
            srstd = ph1.enter_context(nc.sbuf_tensor("srstd", [128, 1], F32))
            sxnTf = ph1.enter_context(nc.sbuf_tensor("sxnTf", [128, DM], F32))
            sxnTb = [ph1.enter_context(nc.sbuf_tensor(f"sxnTb{i}", [128, DM], BF16)) for i in range(2)]
            sxnb = [ph1.enter_context(nc.sbuf_tensor(f"sxnb{i}", [128, DM], BF16)) for i in range(2)]
            slg = ph1.enter_context(nc.sbuf_tensor("slg", [128, NE], F32))
            sge = ph1.enter_context(nc.sbuf_tensor("sge", [128, NE], F32))
            sgex = ph1.enter_context(nc.sbuf_tensor("sgex", [128, NE], F32))
            sgsum = ph1.enter_context(nc.sbuf_tensor("sgsum", [128, 1], F32))
            smax8 = ph1.enter_context(nc.sbuf_tensor("smax8", [128, 8], F32))
            sidx8 = ph1.enter_context(nc.sbuf_tensor("sidx8", [128, 8], U32))
            pT1 = ph1.enter_context(nc.psum_tensor("pT1", [128, DM], F32))
            pG1 = ph1.enter_context(nc.psum_tensor("pG1", [128, NE], F32))

            NV, NS, NTN = 12, 4, 16
            v1b, s1b, t1b = V1.n, S1.n, T1.n

            @block.sync
            def _(s):
                for t in range(NTL + 1):
                    if t < NT:
                        if t >= 2:
                            s.wait_ge(V1.sem, v1b + NV * (t - 2) + 5)
                        dX.inc(s.dma_start(out=sx[t % 2][:], in_=bass.AP(xq, t * 128 * DM, [[DM, 128], [1, DM]])), t)
                    if t >= 1 and (ln_parts & 4):
                        u = t - 1
                        s.wait_ge(S1.sem, s1b + NS * u + 2)
                        dWb.inc(s.dma_start(out=bass.AP(xnbH, u * 128 * DM, [[DM, 128], [1, DM]]),
                                            in_=sxnb[u % 2][:]), u)
                        s.wait_ge(S1.sem, s1b + NS * u + 3)
                        dWT.inc(s.dma_start(out=bass.AP(xnTH, u * 128, [[TOK, 128], [128 * TOK, 8], [1, 128]]),
                                            in_=sxnTb[u % 2][:]), u)

            @block.vector
            def _(v):
                def vemit(thunk):
                    if ln_sk & 4:
                        v.nop().then_inc(V1.sem, 1)
                    else:
                        thunk().then_inc(V1.sem, 1)
                for t in range(NTL):
                    dX.wait_tile(v, t)
                    if t >= 1:
                        v.wait_ge(S1.sem, s1b + NS * (t - 1) + 1)
                    x_ = sx[t % 2]
                    vemit(lambda: v.bn_stats(sbn[:, 0:6], x_[:, 0:512]))
                    vemit(lambda: v.bn_stats(sbn[:, 6:12], x_[:, 512:1024]))
                    v.wait_ge(V1.sem, v1b + NV * t + 2)
                    vemit(lambda: v.bn_aggr(smv[:], sbn[:]))
                    v.wait_ge(S1.sem, s1b + NS * t + 1)
                    vemit(lambda: v.reciprocal(srstd[:], ssd[:]))
                    v.wait_ge(V1.sem, v1b + NV * t + 4)
                    vemit(lambda: v.tensor_scalar(sxn[:], x_[:], smv[:, 0:1], srstd[:], OP.subtract, OP.mult))
                    if ln_parts & 2:
                        v.wait_ge(T1.sem, t1b + NTN * t + 8)
                    vemit(lambda: v.tensor_copy(sxnTf[:], pT1[:]))
                    if ln_parts & 8:
                        v.wait_ge(T1.sem, t1b + NTN * t + 16)
                    vemit(lambda: v.tensor_copy(slg[:], pG1[:]))
                    v.wait_ge(V1.sem, v1b + NV * t + 7)
                    vemit(lambda: v.max(smax8[:], slg[:]))
                    v.wait_ge(V1.sem, v1b + NV * t + 8)
                    vemit(lambda: v.max_index(sidx8[:], smax8[:], slg[:]))
                    v.wait_ge(V1.sem, v1b + NV * t + 9)
                    vemit(lambda: v.tensor_copy(s_idxc[:, t:t + 1], sidx8[:, 0:1]))
                    vemit(lambda: v.tensor_scalar(sge[:], slg[:], smax8[:, 0:1], None, OP.subtract))
                    v.wait_ge(S1.sem, s1b + NS * t + 4)
                    if ln_sk & 1:
                        vemit(lambda: v.reciprocal(s_gatec[:, t:t + 1], smv[:, 1:2]))
                    else:
                        vemit(lambda: v.reciprocal(s_gatec[:, t:t + 1], sgsum[:]))

            @block.scalar
            def _(sc):
                def semit(thunk):
                    if ln_sk & 2:
                        sc.nop().then_inc(S1.sem, 1)
                    else:
                        thunk().then_inc(S1.sem, 1)
                sc.wait_ge(V0.sem, 3)
                for t in range(NTL):
                    sc.wait_ge(V1.sem, v1b + NV * t + 3)
                    semit(lambda: sc.activation(ssd[:], smv[:, 1:2], AF.Sqrt, bias=s_eps[:]))
                    sc.wait_ge(V1.sem, v1b + NV * t + 5)
                    if t >= 2:
                        dWb.wait_tile(sc, t - 2)
                    semit(lambda: sc.activation(sxnb[t % 2][:], sxn[:], AF.Copy))
                    sc.wait_ge(T1.sem, t1b + NTN * t + 8)
                    sc.wait_ge(V1.sem, v1b + NV * t + 6)
                    if t >= 2:
                        dWT.wait_tile(sc, t - 2)
                    semit(lambda: sc.activation(sxnTb[t % 2][:], pT1[:], AF.Copy))
                    sc.wait_ge(V1.sem, v1b + NV * t + 11)
                    if ln_sk & 1:
                        semit(lambda: sc.activation(sgex[:], sge[:], AF.Exp))
                    else:
                        semit(lambda: sc.activation(sgex[:], sge[:], AF.Exp, accum_out=sgsum[:]))

            @block.tensor
            def _(tn):
                tn.wait_ge(dS0.sem, 7 * 16)
                for t in range(NTL):
                    tn.wait_ge(V1.sem, v1b + NV * t + 5)
                    if t >= 1:
                        tn.wait_ge(S1.sem, s1b + NS * (t - 1) + 3)
                    for c in range(8):
                        tn.transpose(pT1[:, c * 128:(c + 1) * 128], sxn[:, c * 128:(c + 1) * 128],
                                     s_identf[:]).then_inc(T1.sem, 1)
                    tn.wait_ge(V1.sem, v1b + NV * t + 6)
                    for fc in range(8):
                        tn.matmul(pG1[:], sxnTf[:, fc * 128:(fc + 1) * 128], s_wg[:, fc, :],
                                  start=(fc == 0), stop=(fc == 7)).then_inc(T1.sem, 1)
                T1.n += NT * NTN

            V1.n = v1b + NV * NTL
            S1.n = s1b + NS * NTL

        PH = [None]
        V2 = Sem(es, nc, "V2")
        S2 = Sem(es, nc, "S2")
        T2 = Sem(es, nc, "T2")
        G2 = Sem(es, nc, "G2")
        dA = Sem(es, nc, "dA")

        if max_phase >= 2:
         with contextlib.ExitStack() as ph2:
            sidxT = ph2.enter_context(nc.sbuf_tensor("sidxT", [64, 128], F32))
            soh = ph2.enter_context(nc.sbuf_tensor("soh", [64, 128], F32))
            scum = ph2.enter_context(nc.sbuf_tensor("scum", [64, 128], F32))
            stmp = ph2.enter_context(nc.sbuf_tensor("stmp", [64, 128], F32))
            stmp2 = ph2.enter_context(nc.sbuf_tensor("stmp2", [64, 128], F32))
            slocT = ph2.enter_context(nc.sbuf_tensor("slocT", [64, 128], F32))
            sones64 = ph2.enter_context(nc.sbuf_tensor("sones64", [64, 128], F32))
            spart = ph2.enter_context(nc.sbuf_tensor("spart", [64, NE], F32))
            stileoff = ph2.enter_context(nc.sbuf_tensor("stileoff", [64, NE], F32))
            scnt = ph2.enter_context(nc.sbuf_tensor("scnt", [1, NE], F32))
            sarin = ph2.enter_context(nc.sbuf_tensor("sarin", [8, NE], F32))
            sdT = ph2.enter_context(nc.sbuf_tensor("sdT", [64, 128], F32))
            strashi = ph2.enter_context(nc.sbuf_tensor("strashi", [64, 128], mybir.dt.int32))
            strash = ph2.enter_context(nc.sbuf_tensor("strash", [64, 128], F32))
            slv64 = ph2.enter_context(nc.sbuf_tensor("slv64", [64, 128], F32))
            sdTc = ph2.enter_context(nc.sbuf_tensor("sdTc", [128, 64], F32))
            p2a = ph2.enter_context(nc.psum_tensor("p2a", [128, 128], F32))
            p2b = ph2.enter_context(nc.psum_tensor("p2b", [64, NE], F32))
            p2c = ph2.enter_context(nc.psum_tensor("p2c", [8, NE], F32))

            @block.tensor
            def _(tn):
                tn.wait_ge(V1.sem, V1.n)
                tn.transpose(p2a[0:64, :], s_idxc[:], s_identf[:]).then_inc(T2.sem, 1)
                T2.n += 1

            @block.gpsimd
            def _(g):
                g.wait_ge(V1.sem, V1.n)
                g.iota(strashi[:], [[1, 128]], base=DISP_ROWS, channel_multiplier=0).then_inc(G2.sem, 1)
                G2.n += 1

            @block.vector
            def _(v):
                v.wait_ge(S1.sem, S1.n)
                dWb.wait_all(v)
                dWT.wait_all(v)
                v.memset(sones64[:], 1.0).then_inc(V2.sem, 1)
                v.memset(slocT[:], 0.0).then_inc(V2.sem, 1)
                v.wait_ge(T2.sem, 1)
                v.tensor_copy(sidxT[:], p2a[0:64, :]).then_inc(V2.sem, 1)
                V2.n += 3

            def _vop(v, thunk):
                v.wait_ge(V2.sem, V2.n)
                V2.n += 1
                thunk().then_inc(V2.sem, 1)

            @block.vector
            def _(v):
                for e in range(NE):
                    _vop(v, lambda: v.tensor_single_scalar(soh[:], sidxT[:], float(e), OP.is_equal))
                    _vop(v, lambda: v.tensor_tensor_scan(scum[:], soh[:], sones64[:], 0.0, OP.add, OP.mult))
                    _vop(v, lambda: v.tensor_copy(spart[:, e:e + 1], scum[:, 127:128]))

            @block.tensor
            def _(tn):
                tn.wait_ge(V2.sem, V2.n)
                tn.matmul(p2b[:], s_triu[:], spart[:], start=True, stop=True).then_inc(T2.sem, 1)
                tn.matmul(p2c[0:1, :], s_onesf[0:64, 0:1], spart[:], start=True, stop=True).then_inc(T2.sem, 1)
                T2.n += 2

            @block.vector
            def _(v):
                v.wait_ge(T2.sem, T2.n)
                _vop(v, lambda: v.tensor_copy(stileoff[:], p2b[:]))
                _vop(v, lambda: v.tensor_copy(scnt[:], p2c[0:1, :]))
                for e in range(NE):
                    _vop(v, lambda: v.tensor_single_scalar(soh[:], sidxT[:], float(e), OP.is_equal))
                    _vop(v, lambda: v.tensor_tensor_scan(scum[:], soh[:], sones64[:], 0.0, OP.add, OP.mult))
                    _vop(v, lambda: v.tensor_scalar(stmp[:], scum[:], stileoff[:, e:e + 1], 1.0, OP.add, OP.subtract))
                    _vop(v, lambda: v.tensor_tensor(stmp2[:], stmp[:], soh[:], OP.mult))
                    _vop(v, lambda: v.tensor_tensor(slocT[:], slocT[:], stmp2[:], OP.add))
                v.wait_ge(G2.sem, 1)
                _vop(v, lambda: v.tensor_copy(strash[:], strashi[:]))
                _vop(v, lambda: v.tensor_single_scalar(stmp[:], slocT[:], float(CAPL - 1), OP.min))
                _vop(v, lambda: v.tensor_single_scalar(stmp2[:], sidxT[:], float(CAPL), OP.mult))
                _vop(v, lambda: v.tensor_tensor(sdT[:], stmp[:], stmp2[:], OP.add))
                _vop(v, lambda: v.tensor_single_scalar(slv64[:], slocT[:], float(CAPL), OP.is_lt))
                _vop(v, lambda: v.tensor_tensor(sdT[:], sdT[:], slv64[:], OP.mult))
                _vop(v, lambda: v.tensor_single_scalar(stmp[:], slv64[:], 1.0, OP.subtract))
                _vop(v, lambda: v.tensor_single_scalar(stmp[:], stmp[:], -1.0, OP.mult))
                _vop(v, lambda: v.tensor_tensor(stmp[:], stmp[:], strash[:], OP.mult))
                _vop(v, lambda: v.tensor_tensor(sdT[:], sdT[:], stmp[:], OP.add))

            sgBf = ph2.enter_context(nc.sbuf_tensor("sgBf", [128, NT], F32))

            @block.tensor
            def _(tn):
                tn.wait_ge(V2.sem, V2.n)
                tn.matmul(p2c[:], s_corehot[:], scnt[:], start=True, stop=True).then_inc(T2.sem, 1)
                tn.transpose(p2a[:, 0:64], sdT[:], s_identf[0:64, 0:64]).then_inc(T2.sem, 1)
                T2.n += 2

            @block.vector
            def _(v):
                v.wait_ge(T2.sem, T2.n)
                _vop(v, lambda: v.tensor_copy(sarin[:], p2c[:]))
                _vop(v, lambda: v.tensor_copy(sdTc[:], p2a[:, 0:64]))
                _vop(v, lambda: v.tensor_copy(s_dc[:], sdTc[:]))

            @block.tensor
            def _(tn):
                tn.wait_ge(V2.sem, V2.n)
                tn.transpose(p2a[:, 0:64], slocT[:], s_identf[0:64, 0:64]).then_inc(T2.sem, 1)
                T2.n += 1

            @block.vector
            def _(v):
                v.wait_ge(T2.sem, T2.n)
                _vop(v, lambda: v.tensor_copy(s_locc[:], p2a[:, 0:64]))
                _vop(v, lambda: v.tensor_single_scalar(sdTc[:], s_locc[:], float(CAPL - 1), OP.min))
                _vop(v, lambda: v.tensor_single_scalar(sgBf[:], s_idxc[:], float(CAPL), OP.mult))
                _vop(v, lambda: v.tensor_tensor(sgBf[:], sgBf[:], sdTc[:], OP.add))
                _vop(v, lambda: v.tensor_copy(s_gB[:], sgBf[:]))

            @block.gpsimd
            def _(g):
                g.wait_ge(V2.sem, V2.n)
                g.dma_start(out=arinH.ap(), in_=sarin[:]).then_inc(dA.sem, 16)
                dA.n += 16
                g.wait_ge(dA.sem, dA.n)
                g.collective_compute(
                    "AllReduce", OP.add,
                    replica_groups=[list(range(NCORES))],
                    ins=[arinH.ap()], outs=[aroutH.ap()],
                ).then_inc(G2.sem, 1)
                G2.n += 1

        dZ = Sem(es, nc, "dZ")
        dR = DSem2(es, nc, "dR")
        dSc = DSem2(es, nc, "dSc")
        VZ = Sem(es, nc, "VZ")

        if max_phase >= 3:
         with contextlib.ExitStack() as ph3:
            szero = ph3.enter_context(nc.sbuf_tensor("szero", [128, 8192], BF16))
            sxr = [ph3.enter_context(nc.sbuf_tensor(f"sxr{i}", [128, DM], BF16)) for i in range(2)]

            @block.vector
            def _(v):
                v.wait_ge(dA.sem, dA.n)
                v.memset(szero[:], 0.0).then_inc(VZ.sem, 1)
                VZ.n += 1

            @block.sync
            def _(s):
                s.wait_ge(VZ.sem, 1)
                for k in range(16):
                    s.dma_start(out=bass.AP(dispH, k * 128 * 8192, [[8192, 128], [1, 8192]]),
                                in_=szero[:]).then_inc(dZ.sem, 16)
                s.dma_start(out=bass.AP(dispH, 16 * 128 * 8192, [[DM, 128], [1, DM]]),
                            in_=szero[:, 0:DM]).then_inc(dZ.sem, 16)
                dZ.n += 17 * 16
                dWb.wait_all(s)
                for t in range(NT):
                    if t >= 2:
                        dSc.wait_tile(s, t - 2)
                    dR.inc(s.dma_start(out=sxr[t % 2][:], in_=bass.AP(xnbH, t * 128 * DM, [[DM, 128], [1, DM]])), t)

            @block.gpsimd
            def _(g):
                g.wait_ge(V2.sem, V2.n)
                g.wait_ge(dZ.sem, 17 * 16)
                for t in range(NT):
                    dR.wait_tile(g, t)
                    dSc.inc(g.indirect_dma_start(
                        out=dispH.ap(), out_offset=bass.IndirectOffsetOnAxis(ap=s_dc[:, t:t + 1], axis=0),
                        in_=sxr[t % 2][:], in_offset=None,
                    ), t)

        dAt = Sem(es, nc, "dAt")
        TA = Sem(es, nc, "TA")
        VA = Sem(es, nc, "VA")
        SA = Sem(es, nc, "SA")
        dXT = DSem2(es, nc, "dXT")
        dQKw = DSem2(es, nc, "dQKw")
        dQKr = DSem2(es, nc, "dQKr")
        dPW2 = DSem2(es, nc, "dPW2")
        dSH = DSem2(es, nc, "dSH")
        dOTw = DSem2(es, nc, "dOTw")
        dOTr = DSem2(es, nc, "dOTr")
        dAW = DSem2(es, nc, "dAW")
        dVW = DSem2(es, nc, "dVW")
        dVr = DSem2(es, nc, "dVr")

        def _segs(qt, half):
            lo = max(0, 128 * qt - 64)
            hi = min(T, 128 * qt + 192)
            h0, h1 = 1024 * half, 1024 * half + 1024
            out = []
            for (a, b2, kind) in [(0, lo, "L"), (lo, hi, "B"), (hi, T, "R")]:
                aa, bb = max(a, h0), min(b2, h1)
                if aa < bb:
                    out.append((aa, bb, kind))
            return out

        if max_phase >= 4 and True:
         with contextlib.ExitStack() as pha:
            sxnT = pha.enter_context(nc.sbuf_tensor("sxnT", [128, 8, T], BF16))
            svst = [pha.enter_context(nc.sbuf_tensor(f"svst{i}", [128, DM], BF16)) for i in range(2)]
            sVr = [pha.enter_context(nc.sbuf_tensor(f"sVr{i}", [128, 16, 128], BF16)) for i in range(2)]
            sqkst = [pha.enter_context(nc.sbuf_tensor(f"sqkst{i}", [128, T], BF16)) for i in range(2)]
            sq = [pha.enter_context(nc.sbuf_tensor(f"sq{i}", [128, T], BF16)) for i in range(2)]
            sk = [pha.enter_context(nc.sbuf_tensor(f"sk{i}", [128, T], BF16)) for i in range(2)]
            spad = [pha.enter_context(nc.sbuf_tensor(f"spad{i}", [128, PADW + 1], BF16)) for i in range(2)]
            sshear = [pha.enter_context(nc.sbuf_tensor(f"sshear{i}", [128, 256], BF16)) for i in range(2)]
            sA = pha.enter_context(nc.sbuf_tensor("sA", [128, T], BF16))
            sAT = [pha.enter_context(nc.sbuf_tensor(f"sAT{i}", [128, 16, 512], BF16)) for i in range(2)]
            sOTst = [pha.enter_context(nc.sbuf_tensor(f"sOTst{i}", [128, 512], BF16)) for i in range(2)]
            sOTr = [pha.enter_context(nc.sbuf_tensor(f"sOTr{i}", [128, 8, 128], BF16)) for i in range(2)]
            sattst = [pha.enter_context(nc.sbuf_tensor(f"sattst{i}", [128, 512], F32)) for i in range(2)]
            sB0 = pha.enter_context(nc.sbuf_tensor("sB0", [128, 2, NQT], BF16))
            sB128 = pha.enter_context(nc.sbuf_tensor("sB128", [128, 2, NQT], BF16))
            smx4 = pha.enter_context(nc.sbuf_tensor("smx4", [128, 4], F32))
            sMc = pha.enter_context(nc.sbuf_tensor("sMc", [128, 1], F32))
            sMnc = pha.enter_context(nc.sbuf_tensor("sMnc", [128, 1], F32))
            smxB = pha.enter_context(nc.sbuf_tensor("smxB", [128, 1], F32))
            sbL = pha.enter_context(nc.sbuf_tensor("sbL", [128, 1], F32))
            sbR = pha.enter_context(nc.sbuf_tensor("sbR", [128, 1], F32))
            sacc = pha.enter_context(nc.sbuf_tensor("sacc", [128, 8], F32))
            srow = pha.enter_context(nc.sbuf_tensor("srow", [128, 1], F32))
            srec = pha.enter_context(nc.sbuf_tensor("srec", [128, 1], F32))
            ps0 = pha.enter_context(nc.psum_tensor("ps0", [128, 1024], F32))
            ps1 = pha.enter_context(nc.psum_tensor("ps1", [128, 1024], F32))
            patp = pha.enter_context(nc.psum_tensor("patp", [128, 2048], BF16))
            pav = pha.enter_context(nc.psum_tensor("pav", [128, 512], F32))
            paux = pha.enter_context(nc.psum_tensor("paux", [128, 512], F32))

            R = {}

            @block.vector
            def _(v):
                v.wait_ge(dZ.sem, dZ.n)
                dR.wait_all(v)
                dSc.wait_all(v)
                v.nop().then_inc(VA.sem, 1)
                VA.n += 1

            @block.scalar
            def _(sc):
                sc.wait_ge(VA.sem, VA.n)
                sc.nop().then_inc(SA.sem, 1)
                SA.n += 1

            @block.tensor
            def _(tn):
                tn.wait_ge(VA.sem, VA.n)
                tn.wait_ge(dG0.sem, dG0.n)
                tn.nop().then_inc(TA.sem, 1)
                TA.n += 1

            qk_tiles = [ps0[:, 0:512], ps0[:, 512:1024], ps1[:, 0:512], ps1[:, 512:1024]]

            if NHPL < NHP or BPCL < BPC:
                dZO = Sem(es, nc, "dZO")
                szatt = pha.enter_context(nc.sbuf_tensor("szatt", [128, DM], F32))

                @block.vector
                def _(v):
                    v.wait_ge(VA.sem, VA.n)
                    v.memset(sA[:], 0.0).then_inc(VA.sem, 1)
                    v.memset(szatt[:], 0.0).then_inc(VA.sem, 1)
                    VA.n += 2

                @block.sync
                def _(s):
                    s.wait_ge(VA.sem, VA.n)
                    for k8 in range(8):
                        s.dma_start(out=bass.AP(OTH, k8 * 128 * T, [[T, 128], [1, T]]),
                                    in_=sA[:]).then_inc(dZO.sem, 16)
                    dZO.n += 8 * 16
                    for k8 in range(64):
                        s.dma_start(out=bass.AP(attnH, k8 * 128 * DM, [[DM, 128], [1, DM]]),
                                    in_=szatt[:]).then_inc(dZO.sem, 16)
                    dZO.n += 64 * 16

                @block.tensor
                def _(tn):
                    tn.wait_ge(dZO.sem, dZO.n)
                    tn.nop().then_inc(TA.sem, 1)
                    TA.n += 1

            for b in range(BPCL):
                @block.sync
                def _(s, b=b):
                    if b == 0:
                        s.wait_ge(dZ.sem, dZ.n)
                        dR.wait_all(s)
                        dSc.wait_all(s)
                        dWT.wait_all(s)
                    if b >= 1:
                        s.wait_ge(TA.sem, R[("xnT_free", b - 1)])
                    for fc in range(8):
                        dXT.inc(s.dma_start(
                            out=sxnT[:, fc, :],
                            in_=bass.AP(xnTH, fc * 128 * TOK + b * T, [[TOK, 128], [1, T]])), b)

                for oc in range(16):
                    @block.tensor
                    def _(tn, b=b, oc=oc):
                        if oc == 0:
                            dXT.wait_tile(tn, b, per=8)
                            if b >= 1:
                                tn.wait_ge(VA.sem, R[("opev", b - 1, 15)])
                        if oc >= 1:
                            tn.wait_ge(SA.sem, R[("qkevac", b, oc - 1)])
                        for fc in range(8):
                            for tm in range(4):
                                tn.matmul(qk_tiles[tm], swqkv[:, fc, oc * 128:(oc + 1) * 128],
                                          sxnT[:, fc, tm * 512:(tm + 1) * 512],
                                          start=(fc == 0), stop=False).then_inc(TA.sem, 1)
                                TA.n += 1
                        for tm in range(4):
                            tn.matmul(qk_tiles[tm], s_biasrow[0:1, oc * 128:(oc + 1) * 128],
                                      s_onesb[0:1, 0:512],
                                      start=False, stop=True).then_inc(TA.sem, 1)
                            TA.n += 1
                        R[("qkmm", b, oc)] = TA.n

                    @block.scalar
                    def _(sc, b=b, oc=oc):
                        sc.wait_ge(TA.sem, R[("qkmm", b, oc)])
                        gi = b * 16 + oc
                        if gi >= 2:
                            dQKw.wait_tile(sc, gi - 2)
                        scale = 0.125 if oc < 8 else 1.0
                        for tm in range(4):
                            sc.activation(sqkst[oc % 2][:, tm * 512:(tm + 1) * 512], qk_tiles[tm],
                                          AF.Copy, scale=scale).then_inc(SA.sem, 1)
                            SA.n += 1
                        R[("qkevac", b, oc)] = SA.n

                    @block.sync
                    def _(s, b=b, oc=oc):
                        s.wait_ge(SA.sem, R[("qkevac", b, oc)])
                        dQKw.inc(s.dma_start(out=bass.AP(qkH, oc * 128 * T, [[T, 128], [1, T]]),
                                             in_=sqkst[oc % 2][:]), b * 16 + oc)
                        R[("qkw_tot", b)] = (dQKw.cnt[0], dQKw.cnt[1])

                for tt in range(16):
                    @block.tensor
                    def _(tn, b=b, tt=tt):
                        if tt >= 1:
                            tn.wait_ge(SA.sem, R[("vevac", b, tt - 1)])
                        else:
                            tn.wait_ge(SA.sem, R[("qkevac", b, 15)])
                        for fc in range(8):
                            for om in range(2):
                                tn.matmul(qk_tiles[om], sxnT[:, fc, tt * 128:(tt + 1) * 128],
                                          swqkv[:, fc, 2048 + om * 512:2048 + (om + 1) * 512],
                                          start=(fc == 0), stop=False).then_inc(TA.sem, 1)
                                TA.n += 1
                        for om in range(2):
                            tn.matmul(qk_tiles[om], s_onesb[0:1, 0:128],
                                      s_biasrow[0:1, 2048 + om * 512:2048 + (om + 1) * 512],
                                      start=False, stop=True).then_inc(TA.sem, 1)
                            TA.n += 1
                        R[("vmm", b, tt)] = TA.n

                    @block.scalar
                    def _(sc, b=b, tt=tt):
                        sc.wait_ge(TA.sem, R[("vmm", b, tt)])
                        gv = b * 16 + tt
                        if gv >= 2:
                            dVW.wait_tile(sc, gv - 2)
                        for om in range(2):
                            sc.activation(svst[tt % 2][:, om * 512:(om + 1) * 512], qk_tiles[om],
                                          AF.Copy).then_inc(SA.sem, 1)
                            SA.n += 1
                        R[("vevac", b, tt)] = SA.n

                    @block.sync
                    def _(s, b=b, tt=tt):
                        s.wait_ge(SA.sem, R[("vevac", b, tt)])
                        if b >= 1 and tt == 0:
                            dVr.wait_all(s)
                        dVW.inc(s.dma_start(out=bass.AP(vH, tt * 128 * DM, [[DM, 128], [1, DM]]),
                                             in_=svst[tt % 2][:]), b * 16 + tt)
                        R[("vw_tot", b)] = (dVW.cnt[0], dVW.cnt[1])

                R[("xnT_free", b)] = R[("vmm", b, 15)]

                for hp in range(NHPL):
                    hpg = b * NHPL + hp

                    @block.sync
                    def _(s, b=b, hp=hp, hpg=hpg):
                        s.wait_ge(dQKw.sems[0], R[("qkw_tot", b)][0])
                        s.wait_ge(dQKw.sems[1], R[("qkw_tot", b)][1])
                        if hpg >= 2:
                            s.wait_ge(TA.sem, R[("pair_free_t", hpg - 2)])
                            s.wait_ge(SA.sem, R[("pair_free_s", hpg - 2)])
                        dQKr.inc(s.dma_start(out=sq[hpg % 2][:],
                                             in_=bass.AP(qkH, hp * 128 * T, [[T, 128], [1, T]])), hpg)
                        dQKr.inc(s.dma_start(out=sk[hpg % 2][:],
                                             in_=bass.AP(qkH, (8 + hp) * 128 * T, [[T, 128], [1, T]])), hpg)
                        s.wait_ge(dVW.sems[0], R[("vw_tot", b)][0])
                        s.wait_ge(dVW.sems[1], R[("vw_tot", b)][1])
                        if hpg >= 2:
                            s.wait_ge(TA.sem, R[("avmm", hpg - 2, 3)])
                        dVr.inc(s.dma_start(out=sVr[hpg % 2][:],
                                            in_=bass.AP(vH, hp * 128, [[DM, 128], [128 * DM, 16], [1, 128]])), hpg)

                    for hh in range(2):
                        for qt in range(NQT):
                            @block.tensor
                            def _(tn, hpg=hpg, hh=hh, qt=qt):
                                if hh == 0 and qt == 0:
                                    dQKr.wait_tile(tn, hpg, per=2)
                                    if hpg >= 1:
                                        tn.wait_ge(VA.sem, R[("aux_free", hpg - 1)])
                                        tn.wait_ge(VA.sem, R[("pair_free_v", hpg - 1)])
                                elif qt >= 1:
                                    tn.wait_ge(VA.sem, R[("padbuild", hpg, hh, qt - 1)])
                                else:
                                    tn.wait_ge(VA.sem, R[("padbuild", hpg, hh - 1, NQT - 1)])
                                tn.matmul(paux[:, 0:129], sq[hpg % 2][64 * hh:64 * hh + 64, qt * 128:(qt + 1) * 128],
                                          s_peT2[64 * hh:64 * hh + 64, :], start=True, stop=True,
                                          tile_position=(64 * hh, 0)).then_inc(TA.sem, 1)
                                TA.n += 1
                                R[("Bmm", hpg, hh, qt)] = TA.n

                            @block.vector
                            def _(v, hpg=hpg, hh=hh, qt=qt):
                                v.wait_ge(TA.sem, R[("Bmm", hpg, hh, qt)])
                                gpq = (hpg * 2 + hh) * NQT + qt
                                if gpq >= 2:
                                    dPW2.wait_tile(v, gpq - 2)
                                par = gpq % 2
                                v.tensor_copy(spad[par][:, PADL:PADL + PADM], paux[:, 0:129]).then_inc(VA.sem, 1)
                                VA.n += 1
                                v.tensor_scalar(spad[par][:, 0:PADL], s_onesb[:, 0:PADL],
                                                paux[:, 0:1], None, OP.mult).then_inc(VA.sem, 1)
                                VA.n += 1
                                v.tensor_scalar(spad[par][:, PADL + PADM:PADW],
                                                s_onesb[:, 0:PADR],
                                                paux[:, 128:129], None, OP.mult).then_inc(VA.sem, 1)
                                VA.n += 1
                                v.tensor_copy(sB0[:, hh, qt:qt + 1], paux[:, 0:1]).then_inc(VA.sem, 1)
                                v.tensor_copy(sB128[:, hh, qt:qt + 1], paux[:, 128:129]).then_inc(VA.sem, 1)
                                VA.n += 2
                                R[("padbuild", hpg, hh, qt)] = VA.n

                            @block.sync
                            def _(s, b=b, hp=hp, hpg=hpg, hh=hh, qt=qt):
                                s.wait_ge(VA.sem, R[("padbuild", hpg, hh, qt)])
                                gpq = (hpg * 2 + hh) * NQT + qt
                                dPW2.inc(s.dma_start(
                                    out=bass.AP(padH, (((hpg % 2) * 2 + hh) * NQT + qt) * 128 * (PADW + 1), [[PADW + 1, 128], [1, PADW]]),
                                    in_=spad[gpq % 2][:, 0:PADW]), gpq)

                    R[("aux_free", hpg)] = R[("padbuild", hpg, 1, NQT - 1)]

                    for qq in range(4):
                        for hh in range(2):
                          for qtl in range(4):
                            qt = qq * 4 + qtl
                            gtau = R.get("tau_ctr", 0)
                            R["tau_ctr"] = gtau + 1
                            lo = max(0, 128 * qt - 64)
                            hi = min(T, 128 * qt + 192)

                            chsegs = []
                            for c in range(4):
                                c0, c1 = c * 512, (c + 1) * 512
                                segs = []
                                for (a, b2, kind) in [(0, lo, "L"), (lo, hi, "B"), (hi, T, "R")]:
                                    aa, bb = max(a, c0), min(b2, c1)
                                    if aa < bb:
                                        segs.append((aa, bb, kind))
                                chsegs.append(segs)

                            @block.sync
                            def _(s, hpg=hpg, hh=hh, qt=qt, gtau=gtau, lo=lo, hi=hi):
                                dPW2.wait_tile(s, (hpg * 2 + hh) * NQT + qt)
                                if gtau >= 2:
                                    s.wait_ge(VA.sem, R[("banddone", gtau - 2)])
                                W = hi - lo
                                base = (((hpg % 2) * 2 + hh) * NQT + qt) * 128 * (PADW + 1) + (lo - 128 * qt + 191)
                                dSH.inc(s.dma_start(out=sshear[gtau % 2][:, 0:W],
                                                    in_=bass.AP(padH, base, [[PADW, 128], [1, W]])), gtau)

                            @block.tensor
                            def _(tn, hpg=hpg, hh=hh, qt=qt, gtau=gtau, b=b):
                                for c in range(4):
                                    if gtau >= 1:
                                        tn.wait_ge(SA.sem, R[("expc", gtau - 1, c)])
                                    if hpg == b * NHPL and hh == 0 and qt == 0 and c == 0:
                                        tn.wait_ge(SA.sem, R[("vevac", b, 15)])
                                    tn.matmul(qk_tiles[c],
                                              sq[hpg % 2][64 * hh:64 * hh + 64, qt * 128:(qt + 1) * 128],
                                              sk[hpg % 2][64 * hh:64 * hh + 64, c * 512:(c + 1) * 512],
                                              start=True, stop=True,
                                              tile_position=(64 * hh, 0)).then_inc(TA.sem, 1)
                                    TA.n += 1
                                    R[("smm", gtau, c)] = TA.n
                                if gtau >= 1:
                                    tn.wait_ge(VA.sem, R[("scaleA", gtau - 1)])
                                    if gtau >= 2:
                                        tn.wait_ge(SA.sem, R[("ATev", gtau - 2)])
                                    for jc in range(16):
                                        tn.transpose(patp[:, jc * 128:(jc + 1) * 128],
                                                     sA[:, jc * 128:(jc + 1) * 128],
                                                     s_identb[:]).then_inc(TA.sem, 1)
                                    TA.n += 16
                                    R[("ATmm", gtau - 1)] = TA.n

                            for c in range(4):
                                @block.vector
                                def _(v, hpg=hpg, gtau=gtau, c=c, chsegs=chsegs, lo=lo):
                                    v.wait_ge(TA.sem, R[("smm", gtau, c)])
                                    for (a, bb, kind) in chsegs[c]:
                                        if kind == "B":
                                            dSH.wait_tile(v, gtau)
                                            v.tensor_tensor(qk_tiles[c][:, a - c * 512:bb - c * 512],
                                                            qk_tiles[c][:, a - c * 512:bb - c * 512],
                                                            sshear[gtau % 2][:, a - lo:bb - lo],
                                                            OP.add).then_inc(VA.sem, 1)
                                            VA.n += 1
                                    v.wait_ge(VA.sem, VA.n)
                                    v.tensor_reduce(smx4[:, c:c + 1], qk_tiles[c], axis=mybir.AxisListType.X,
                                                    op=OP.max).then_inc(VA.sem, 1)
                                    VA.n += 1
                                    R[("chmax", gtau, c)] = VA.n

                            @block.vector
                            def _(v, hpg=hpg, hh=hh, qt=qt, gtau=gtau):
                                if gtau >= 1:
                                    v.wait_ge(SA.sem, R[("expdone", gtau - 1)])
                                v.wait_ge(VA.sem, VA.n)
                                v.tensor_reduce(sMc[:], smx4[:], axis=mybir.AxisListType.X,
                                                op=OP.max).then_inc(VA.sem, 1)
                                VA.n += 1
                                v.tensor_tensor(smxB[:], sB0[:, hh, qt:qt + 1], sB128[:, hh, qt:qt + 1],
                                                OP.max).then_inc(VA.sem, 1)
                                VA.n += 1
                                v.wait_ge(VA.sem, VA.n)
                                v.tensor_single_scalar(smxB[:], smxB[:], 0.0, OP.max).then_inc(VA.sem, 1)
                                VA.n += 1
                                v.wait_ge(VA.sem, VA.n)
                                v.tensor_tensor(sMc[:], sMc[:], smxB[:], OP.add).then_inc(VA.sem, 1)
                                VA.n += 1
                                v.wait_ge(VA.sem, VA.n)
                                v.tensor_single_scalar(sMnc[:], sMc[:], -1.0, OP.mult).then_inc(VA.sem, 1)
                                VA.n += 1
                                v.tensor_scalar(sbL[:], sB0[:, hh, qt:qt + 1], sMc[:], None,
                                                OP.subtract).then_inc(VA.sem, 1)
                                v.tensor_scalar(sbR[:], sB128[:, hh, qt:qt + 1], sMc[:], None,
                                                OP.subtract).then_inc(VA.sem, 1)
                                VA.n += 2
                                R[("Mdone", gtau)] = VA.n
                                R[("banddone", gtau)] = VA.n

                            @block.scalar
                            def _(sc, gtau=gtau, chsegs=chsegs):
                                sc.wait_ge(VA.sem, R[("Mdone", gtau)])
                                if gtau >= 1:
                                    sc.wait_ge(TA.sem, R[("ATmm", gtau - 1)])
                                slot = 0
                                for c in range(4):
                                    for (a, bb, kind) in chsegs[c]:
                                        bias = {"L": sbL[:], "B": sMnc[:], "R": sbR[:]}[kind]
                                        sc.activation(sA[:, a:bb], qk_tiles[c][:, a - c * 512:bb - c * 512],
                                                      AF.Exp, bias=bias,
                                                      accum_out=sacc[:, slot:slot + 1]).then_inc(SA.sem, 1)
                                        SA.n += 1
                                        slot += 1
                                    R[("expc", gtau, c)] = SA.n
                                R[("slot", gtau)] = slot
                                R[("expdone", gtau)] = SA.n
                                if gtau >= 1:
                                    hh_p, qtl_p = R[("tauinfo", gtau - 1)]
                                    if qtl_p == 0:
                                        if ("avmm", hpg, qq - 1) in R:
                                            pass
                                    sc.activation(
                                        bass.AP(sAT[hh_p], qtl_p * 128, [[16 * 512, 128], [512, 16], [1, 128]]),
                                        patp[:], AF.Copy).then_inc(SA.sem, 1)
                                    SA.n += 1
                                    R[("ATev", gtau - 1)] = SA.n

                            R[("tauinfo", gtau)] = (hh, qtl)

                            @block.vector
                            def _(v, gtau=gtau):
                                v.wait_ge(SA.sem, R[("expdone", gtau)])
                                ns = R[("slot", gtau)]
                                v.tensor_reduce(srow[:], sacc[:, 0:ns], axis=mybir.AxisListType.X,
                                                op=OP.add).then_inc(VA.sem, 1)
                                VA.n += 1
                                v.wait_ge(VA.sem, VA.n)
                                v.reciprocal(srec[:], srow[:]).then_inc(VA.sem, 1)
                                VA.n += 1
                                v.wait_ge(VA.sem, VA.n)
                                v.tensor_scalar(sA[:], sA[:], srec[:], None, OP.mult).then_inc(VA.sem, 1)
                                VA.n += 1
                                R[("scaleA", gtau)] = VA.n

                        lastt = R["tau_ctr"] - 1

                        @block.tensor
                        def _(tn, lastt=lastt):
                            tn.wait_ge(VA.sem, R[("scaleA", lastt)])
                            if lastt >= 1:
                                tn.wait_ge(SA.sem, R[("ATev", lastt - 1)])
                            for jc in range(16):
                                tn.transpose(patp[:, jc * 128:(jc + 1) * 128], sA[:, jc * 128:(jc + 1) * 128],
                                             s_identb[:]).then_inc(TA.sem, 1)
                            TA.n += 16
                            R[("ATmm", lastt)] = TA.n

                        @block.scalar
                        def _(sc, lastt=lastt):
                            sc.wait_ge(TA.sem, R[("ATmm", lastt)])
                            hh_p, qtl_p = R[("tauinfo", lastt)]
                            sc.activation(
                                bass.AP(sAT[hh_p], qtl_p * 128, [[16 * 512, 128], [512, 16], [1, 128]]),
                                patp[:], AF.Copy).then_inc(SA.sem, 1)
                            SA.n += 1
                            R[("ATev", lastt)] = SA.n

                        @block.tensor
                        def _(tn, hpg=hpg, hp=hp, qq=qq, b=b):
                            tn.wait_ge(SA.sem, R[("ATev", R["tau_ctr"] - 1)])
                            if qq >= 1:
                                tn.wait_ge(VA.sem, R[("otev", hpg, qq - 1)])
                            elif hpg >= 1:
                                tn.wait_ge(VA.sem, R[("otev", hpg - 1, 3)])
                            if qq == 0:
                                dVr.wait_tile(tn, hpg)
                            for jc in range(16):
                                tn.matmul(pav[0:64, :], sVr[hpg % 2][:, jc, 0:64],
                                          sAT[0][:, jc, :], start=(jc == 0), stop=(jc == 15),
                                          tile_position=(0, 0), skip_group_check=True).then_inc(TA.sem, 1)
                                tn.matmul(pav[64:128, :], sVr[hpg % 2][:, jc, 64:128],
                                          sAT[1][:, jc, :], start=(jc == 0), stop=(jc == 15),
                                          tile_position=(0, 64), skip_group_check=True).then_inc(TA.sem, 1)
                                TA.n += 2
                            R[("avmm", hpg, qq)] = TA.n

                        @block.vector
                        def _(v, hpg=hpg, qq=qq):
                            v.wait_ge(TA.sem, R[("avmm", hpg, qq)])
                            gq = hpg * 4 + qq
                            if gq >= 2:
                                dOTw.wait_tile(v, gq - 2)
                            v.tensor_copy(sOTst[gq % 2][:], pav[:]).then_inc(VA.sem, 1)
                            VA.n += 1
                            R[("otev", hpg, qq)] = VA.n

                        @block.sync
                        def _(s, hpg=hpg, hp=hp, qq=qq):
                            s.wait_ge(VA.sem, R[("otev", hpg, qq)])
                            gq = hpg * 4 + qq
                            dOTw.inc(s.dma_start(out=bass.AP(OTH, hp * 128 * T + qq * 512, [[T, 128], [1, 512]]),
                                                 in_=sOTst[gq % 2][:]), gq)
                            R[("otw_tot", b)] = (dOTw.cnt[0], dOTw.cnt[1])

                    R[("pair_free_t", hpg)] = R[("avmm", hpg, 3)]
                    R[("pair_free_v", hpg)] = R[("ATev", (hpg * 2 + 1) * NQT + NQT - 1)]
                    R[("pair_free_s", hpg)] = R[("ATev", R["tau_ctr"] - 1)]

                for tt in range(16):
                    @block.sync
                    def _(s, b=b, tt=tt):
                        s.wait_ge(dOTw.sems[0], R[("otw_tot", b)][0])
                        s.wait_ge(dOTw.sems[1], R[("otw_tot", b)][1])
                        gtt = b * 16 + tt
                        if gtt >= 2:
                            pb, ptt = divmod(gtt - 2, 16)
                            s.wait_ge(TA.sem, R[("opmm", pb, ptt)])
                        dOTr.inc(s.dma_start(out=sOTr[tt % 2][:],
                                             in_=bass.AP(OTH, tt * 128, [[T, 128], [T * 128, 8], [1, 128]])), b * 16 + tt)

                    @block.tensor
                    def _(tn, b=b, tt=tt):
                        dOTr.wait_tile(tn, b * 16 + tt)
                        if tt == 0:
                            tn.wait_ge(SA.sem, R[("pair_free_s", b * NHPL + NHPL - 1)])
                        gtt2 = b * 16 + tt
                        if gtt2 >= 1:
                            pb2, ptt2 = divmod(gtt2 - 1, 16)
                            tn.wait_ge(VA.sem, R[("opev", pb2, ptt2)])
                        for om in range(2):
                            for hp2 in range(8):
                                tn.matmul(ps0[:, om * 512:(om + 1) * 512], sOTr[tt % 2][:, hp2, :],
                                          soutw[:, hp2, om * 512:(om + 1) * 512],
                                          start=(hp2 == 0), stop=False).then_inc(TA.sem, 1)
                                TA.n += 1
                            tn.matmul(ps0[:, om * 512:(om + 1) * 512], s_onesb[0:1, 0:128],
                                      s_outbrow[0:1, om * 512:(om + 1) * 512],
                                      start=False, stop=True).then_inc(TA.sem, 1)
                            TA.n += 1
                        R[("opmm", b, tt)] = TA.n

                    @block.vector
                    def _(v, b=b, tt=tt):
                        v.wait_ge(TA.sem, R[("opmm", b, tt)])
                        for om in range(2):
                            gw = (b * 16 + tt) * 2 + om
                            if gw >= 2:
                                dAW.wait_tile(v, gw - 2)
                            v.tensor_copy(sattst[gw % 2][:], ps0[:, om * 512:(om + 1) * 512]).then_inc(VA.sem, 1)
                            VA.n += 1
                        R[("opev", b, tt)] = VA.n

                    @block.sync
                    def _(s, b=b, tt=tt):
                        s.wait_ge(VA.sem, R[("opev", b, tt)])
                        for om in range(2):
                            gw = (b * 16 + tt) * 2 + om
                            dAW.inc(s.dma_start(
                                out=bass.AP(attnH, (b * T + tt * 128) * DM + om * 512, [[DM, 128], [1, 512]]),
                                in_=sattst[gw % 2][:]), gw)

        TF = Sem(es, nc, "TF")
        VF = Sem(es, nc, "VF")
        SF = Sem(es, nc, "SF")
        dSL = DSem2(es, nc, "dSL")
        dW1 = DSem2(es, nc, "dW1")
        dW2 = DSem2(es, nc, "dW2")
        dO2 = DSem2(es, nc, "dO2")

        if max_phase >= 5:
         with contextlib.ExitStack() as phf:
            sslab = [phf.enter_context(nc.sbuf_tensor(f"sslab{i}", [128, DM], BF16)) for i in range(2)]
            sXeT = phf.enter_context(nc.sbuf_tensor("sXeT", [128, 8, CAPL], BF16))
            shT = phf.enter_context(nc.sbuf_tensor("shT", [128, 32, CAPL], BF16))
            sw1c = [phf.enter_context(nc.sbuf_tensor(f"sw1c{i}", [128, 8, 128], BF16)) for i in range(2)]
            sw2c = [phf.enter_context(nc.sbuf_tensor(f"sw2c{i}", [128, 512], BF16)) for i in range(2)]
            svo = [phf.enter_context(nc.sbuf_tensor(f"svo{i}", [128, 512], F32)) for i in range(2)]

            RF = {}

            @block.vector
            def _(v):
                dAW.wait_all(v)
                dXT.wait_all(v)
                dQKr.wait_all(v)
                dOTr.wait_all(v)
                dSH.wait_all(v)
                dQKw.wait_all(v)
                dOTw.wait_all(v)
                dVW.wait_all(v)
                dVr.wait_all(v)
                dPW2.wait_all(v)
                v.wait_ge(TA.sem, TA.n)
                v.wait_ge(SA.sem, SA.n)
                v.wait_ge(VA.sem, VA.n)
                v.wait_ge(dZ.sem, dZ.n)
                dR.wait_all(v)
                dSc.wait_all(v)
                v.nop().then_inc(VF.sem, 1)
                VF.n += 1

            @block.tensor
            def _(tn):
                tn.wait_ge(VF.sem, VF.n)
                tn.nop().then_inc(TF.sem, 1)
                TF.n += 1

            @block.scalar
            def _(sc):
                sc.wait_ge(VF.sem, VF.n)
                sc.nop().then_inc(SF.sem, 1)
                SF.n += 1

            @block.gpsimd
            def _(g):
                g.wait_ge(VF.sem, VF.n)

            if NEL < NE:
                dZF = Sem(es, nc, "dZF")
                szffn = phf.enter_context(nc.sbuf_tensor("szffn", [128, DM], F32))

                @block.vector
                def _(v):
                    v.wait_ge(VF.sem, VF.n)
                    v.memset(szffn[:], 0.0).then_inc(VF.sem, 1)
                    VF.n += 1

                @block.sync
                def _(s):
                    s.wait_ge(VF.sem, VF.n)
                    for k8 in range(DISP_ROWS // 128):
                        s.dma_start(out=bass.AP(out2H, k8 * 128 * DM, [[DM, 128], [1, DM]]),
                                    in_=szffn[:]).then_inc(dZF.sem, 16)
                    dZF.n += (DISP_ROWS // 128) * 16

                @block.tensor
                def _(tn):
                    tn.wait_ge(dZF.sem, dZF.n)
                    tn.nop().then_inc(TF.sem, 1)
                    TF.n += 1

            for e in range(NEL):
                with contextlib.ExitStack() as pe1:
                    pXT = [pe1.enter_context(nc.psum_tensor(f"pXT{e}_{i}", [128, DM], BF16)) for i in range(2)]
                    phT = [pe1.enter_context(nc.psum_tensor(f"phT{e}_{i}", [128, CAPL], F32)) for i in range(2)]

                    for t in range(8):
                        ge = e * 8 + t

                        @block.sync
                        def _(s, e=e, t=t, ge=ge):
                            if ge >= 2:
                                s.wait_ge(TF.sem, RF[("xtmm", ge - 2)])
                            dSL.inc(s.dma_start(out=sslab[t % 2][:],
                                                in_=bass.AP(dispH, (e * CAPL + t * 128) * DM, [[DM, 128], [1, DM]])), ge)

                        @block.tensor
                        def _(tn, e=e, t=t, ge=ge):
                            dSL.wait_tile(tn, ge)
                            if ge == 0:
                                pass
                            if ge >= 2:
                                tn.wait_ge(SF.sem, RF[("xtev", ge - 2)])
                            if t == 0 and e >= 1:
                                tn.wait_ge(TF.sem, RF[("hmm", e - 1)])
                            for kc in range(8):
                                tn.transpose(pXT[ge % 2][:, kc * 128:(kc + 1) * 128],
                                             sslab[t % 2][:, kc * 128:(kc + 1) * 128],
                                             s_identb[:]).then_inc(TF.sem, 1)
                            TF.n += 8
                            RF[("xtmm", ge)] = TF.n

                        @block.scalar
                        def _(sc, e=e, t=t, ge=ge):
                            sc.wait_ge(TF.sem, RF[("xtmm", ge)])
                            sc.activation(bass.AP(sXeT, t * 128, [[8 * CAPL, 128], [CAPL, 8], [1, 128]]),
                                          pXT[ge % 2][:], AF.Copy).then_inc(SF.sem, 1)
                            SF.n += 1
                            RF[("xtev", ge)] = SF.n

                    for Fc in range(32):
                        gf = e * 32 + Fc

                        @block.gpsimd
                        def _(g, e=e, Fc=Fc, gf=gf):
                            if gf >= 2:
                                g.wait_ge(TF.sem, RF[("hfc", gf - 2)])
                            dW1.inc(g.dma_start(
                                out=sw1c[Fc % 2][:],
                                in_=bass.AP(w1, e * DM * DFF + Fc * 128,
                                            [[DFF, 128], [128 * DFF, 8], [1, 128]])), gf)

                        @block.tensor
                        def _(tn, e=e, Fc=Fc, gf=gf):
                            dW1.wait_tile(tn, gf)
                            if Fc == 0:
                                tn.wait_ge(SF.sem, RF[("xtev", e * 8 + 7)])
                            if Fc >= 2:
                                tn.wait_ge(SF.sem, RF[("hev", gf - 2)])
                            for kc in range(8):
                                for sm in range(2):
                                    tn.matmul(phT[Fc % 2][:, sm * 512:(sm + 1) * 512],
                                              sw1c[Fc % 2][:, kc, :],
                                              sXeT[:, kc, sm * 512:(sm + 1) * 512],
                                              start=(kc == 0), stop=(kc == 7)).then_inc(TF.sem, 1)
                                    TF.n += 1
                            RF[("hfc", gf)] = TF.n

                        @block.scalar
                        def _(sc, e=e, Fc=Fc, gf=gf):
                            sc.wait_ge(TF.sem, RF[("hfc", gf)])
                            sc.activation(shT[:, Fc, :], phT[Fc % 2][:], AF.Relu).then_inc(SF.sem, 1)
                            SF.n += 1
                            RF[("hev", gf)] = SF.n

                    RF[("hmm", e)] = RF[("hfc", e * 32 + 31)]

                with contextlib.ExitStack() as pe2:
                    pO = [pe2.enter_context(nc.psum_tensor(f"pO{e}_{i}", [128, 512], F32)) for i in range(8)]

                    for om in range(2):
                        for Fc in range(32):
                            gw = e * 64 + om * 32 + Fc

                            @block.gpsimd
                            def _(g, e=e, om=om, Fc=Fc, gw=gw):
                                if gw >= 2:
                                    g.wait_ge(TF.sem, RF[("o2fc", gw - 2)])
                                dW2.inc(g.dma_start(
                                    out=sw2c[Fc % 2][:],
                                    in_=bass.AP(w2, e * DFF * DM + Fc * 128 * DM + om * 512,
                                                [[DM, 128], [1, 512]])), gw)

                            @block.tensor
                            def _(tn, e=e, om=om, Fc=Fc, gw=gw):
                                dW2.wait_tile(tn, gw)
                                if Fc == 0:
                                    tn.wait_ge(SF.sem, RF[("hev", e * 32 + 31)])
                                    if om == 1:
                                        tn.wait_ge(VF.sem, RF[("o2ev", e, 0)])
                                    elif e >= 1:
                                        tn.wait_ge(VF.sem, RF[("o2ev", e - 1, 1)])
                                for st in range(8):
                                    tn.matmul(pO[st][:], shT[:, Fc, st * 128:(st + 1) * 128],
                                              sw2c[Fc % 2][:], start=(Fc == 0), stop=(Fc == 31)).then_inc(TF.sem, 1)
                                    TF.n += 1
                                RF[("o2fc", gw)] = TF.n

                        @block.vector
                        def _(v, e=e, om=om):
                            v.wait_ge(TF.sem, RF[("o2fc", e * 64 + om * 32 + 31)])
                            for st in range(8):
                                go = (e * 2 + om) * 8 + st
                                if go >= 2:
                                    dO2.wait_tile(v, go - 2)
                                v.tensor_copy(svo[st % 2][:], pO[st][:]).then_inc(VF.sem, 1)
                                VF.n += 1
                                RF[("o2st", e, om, st)] = VF.n
                            RF[("o2ev", e, om)] = VF.n

                        @block.sync
                        def _(s, e=e, om=om):
                            for st in range(8):
                                go = (e * 2 + om) * 8 + st
                                s.wait_ge(VF.sem, RF[("o2st", e, om, st)])
                                dO2.inc(s.dma_start(
                                    out=bass.AP(out2H, (e * CAPL + st * 128) * DM + om * 512, [[DM, 128], [1, 512]]),
                                    in_=svo[st % 2][:]), go)

        dC = DSem2(es, nc, "dC")
        dGb = DSem2(es, nc, "dGb")
        dSaro = Sem(es, nc, "dSaro")
        dY = DSem2(es, nc, "dY")
        V4 = Sem(es, nc, "V4")
        T4 = Sem(es, nc, "T4")
        G4 = Sem(es, nc, "G4")

        if max_phase >= 6:
         with contextlib.ExitStack() as ph4:
            sxc = [ph4.enter_context(nc.sbuf_tensor(f"sxc{i}", [128, DM], F32)) for i in range(2)]
            sac = [ph4.enter_context(nc.sbuf_tensor(f"sac{i}", [128, DM], F32)) for i in range(2)]
            sfc = [ph4.enter_context(nc.sbuf_tensor(f"sfc{i}", [128, DM], F32)) for i in range(2)]
            syc = [ph4.enter_context(nc.sbuf_tensor(f"syc{i}", [128, DM], F32)) for i in range(2)]
            saro = ph4.enter_context(nc.sbuf_tensor("saro", [8, NE], F32))
            sgeff = ph4.enter_context(nc.sbuf_tensor("sgeff", [128, NT], F32))
            soffb = ph4.enter_context(nc.sbuf_tensor("soffb", [128, NE], F32))
            siota16 = ph4.enter_context(nc.sbuf_tensor("siota16", [128, NE], mybir.dt.int32))
            siota16f = ph4.enter_context(nc.sbuf_tensor("siota16f", [128, NE], F32))
            soh4 = ph4.enter_context(nc.sbuf_tensor("soh4", [128, NE], F32))
            scol = ph4.enter_context(nc.sbuf_tensor("scol", [128, 1], F32))
            scol2 = ph4.enter_context(nc.sbuf_tensor("scol2", [128, 1], F32))
            slvc = ph4.enter_context(nc.sbuf_tensor("slvc", [128, NT], F32))
            p4 = ph4.enter_context(nc.psum_tensor("p4", [1, NE], F32))

            @block.gpsimd
            def _(g):
                dO2.wait_all(g)
                g.wait_ge(TF.sem, TF.n)
                g.wait_ge(VF.sem, VF.n)
                g.wait_ge(SF.sem, SF.n)
                dW1.wait_all(g)
                dW2.wait_all(g)
                dSL.wait_all(g)
                g.wait_ge(G2.sem, G2.n)
                g.dma_start(out=saro[:], in_=aroutH.ap()).then_inc(dSaro.sem, 16)
                dSaro.n += 16
                g.iota(siota16[:], [[1, NE]], base=0, channel_multiplier=0).then_inc(G4.sem, 1)
                G4.n += 1

            @block.tensor
            def _(tn):
                tn.wait_ge(dSaro.sem, 16)
                tn.matmul(p4[:], s_prefmask[:], saro[:], start=True, stop=True).then_inc(T4.sem, 1)
                T4.n += 1

            @block.vector
            def _(v):
                dO2.wait_all(v)
                v.wait_ge(TF.sem, TF.n)
                v.wait_ge(SF.sem, SF.n)
                v.wait_ge(VF.sem, VF.n)
                v.wait_ge(T4.sem, 1)
                v.tensor_copy(s_offs[0:1, :], p4[:]).then_inc(V4.sem, 1)
                v.wait_ge(G4.sem, 1)
                v.tensor_copy(siota16f[:], siota16[:]).then_inc(V4.sem, 1)
                v.tensor_single_scalar(slvc[:], s_locc[:], float(CAPL), OP.is_lt).then_inc(V4.sem, 1)
                V4.n += 3
                v.wait_ge(V4.sem, V4.n)

            p4b = ph4.enter_context(nc.psum_tensor("p4b", [128, NE], F32))

            @block.tensor
            def _(tn):
                tn.wait_ge(V4.sem, V4.n)
                tn.matmul(p4b[:], s_onesfr[:], s_offs[0:1, :], start=True, stop=True).then_inc(T4.sem, 1)
                T4.n += 1

            @block.vector
            def _(v):
                v.wait_ge(T4.sem, T4.n)
                v.tensor_copy(soffb[:], p4b[:]).then_inc(V4.sem, 1)
                V4.n += 1

            NV4 = 7

            @block.vector
            def _(v):
                v.wait_ge(G4.sem, G4.n)
                for t in range(NT):
                    base_ = V4.n + NV4 * t
                    v.tensor_scalar(soh4[:], siota16f[:], s_idxc[:, t:t + 1], None, OP.is_equal).then_inc(V4.sem, 1)
                    v.wait_ge(V4.sem, base_ + 1)
                    v.tensor_tensor(soh4[:], soh4[:], soffb[:], OP.mult).then_inc(V4.sem, 1)
                    v.wait_ge(V4.sem, base_ + 2)
                    v.reduce_sum(scol[:], soh4[:], axis=mybir.AxisListType.X).then_inc(V4.sem, 1)
                    v.wait_ge(V4.sem, base_ + 3)
                    v.tensor_scalar(scol[:], scol[:], s_locc[:, t:t + 1], None, OP.add).then_inc(V4.sem, 1)
                    v.wait_ge(V4.sem, base_ + 4)
                    v.tensor_single_scalar(scol[:], scol[:], float(CG), OP.is_lt).then_inc(V4.sem, 1)
                    v.tensor_scalar(scol2[:], slvc[:, t:t + 1], s_gatec[:, t:t + 1], None, OP.mult).then_inc(V4.sem, 1)
                    v.wait_ge(V4.sem, base_ + 6)
                    v.tensor_tensor(sgeff[:, t:t + 1], scol[:], scol2[:], OP.mult).then_inc(V4.sem, 1)
                V4.n += NV4 * NT

            v4base = V4.n

            @block.sync
            def _(s):
                dAW.wait_all(s)
                dO2.wait_all(s)
                for t in range(NT + 1):
                    if t < NT:
                        if t >= 2:
                            s.wait_ge(V4.sem, v4base + 2 * (t - 2) + 2)
                        dC.inc(s.dma_start(out=sxc[t % 2][:], in_=bass.AP(xq, t * 128 * DM, [[DM, 128], [1, DM]])), t)
                        dC.inc(s.dma_start(out=sac[t % 2][:], in_=bass.AP(attnH, t * 128 * DM, [[DM, 128], [1, DM]])), t)
                    if t >= 1:
                        u = t - 1
                        s.wait_ge(V4.sem, v4base + 2 * u + 2)
                        dY.inc(s.dma_start(out=bass.AP(y, u * 128 * DM, [[DM, 128], [1, DM]]),
                                           in_=syc[u % 2][:]), u)

            @block.gpsimd
            def _(g):
                g.wait_ge(V4.sem, v4base)
                for t in range(NT):
                    if t >= 2:
                        g.wait_ge(V4.sem, v4base + 2 * (t - 2) + 2)
                    dGb.inc(g.indirect_dma_start(
                        out=sfc[t % 2][:], out_offset=None,
                        in_=out2H.ap(), in_offset=bass.IndirectOffsetOnAxis(ap=s_gB[:, t:t + 1], axis=0),
                    ), t)

            @block.vector
            def _(v):
                for t in range(NT):
                    dC.wait_tile(v, t, per=2)
                    dGb.wait_tile(v, t)
                    if t >= 2:
                        dY.wait_tile(v, t - 2)
                    v.scalar_tensor_tensor(syc[t % 2][:], sfc[t % 2][:], sgeff[:, t:t + 1],
                                           sxc[t % 2][:], OP.mult, OP.add).then_inc(V4.sem, 1)
                    v.wait_ge(V4.sem, v4base + 2 * t + 1)
                    v.tensor_tensor(syc[t % 2][:], syc[t % 2][:], sac[t % 2][:], OP.add).then_inc(V4.sem, 1)
                V4.n += 2 * NT

            @block.sync
            def _(s):
                dY.wait_all(s)

        if debug:
            dDbg = Sem(es, nc, "dDbg")
            dDbg2 = Sem(es, nc, "dDbg2")

            @block.sync
            def _(s):
                if max_phase >= 2:
                    s.wait_ge(V2.sem, V2.n)
                    s.dma_start(out=dbg["dbg_idx"].ap(), in_=s_idxc[:]).then_inc(dDbg.sem, 16)
                    s.dma_start(out=dbg["dbg_gate"].ap(), in_=s_gatec[:]).then_inc(dDbg.sem, 16)
                    s.dma_start(out=dbg["dbg_loc"].ap(), in_=s_locc[:]).then_inc(dDbg.sem, 16)
                    dDbg.n += 48

            @block.gpsimd
            def _(g):
                if max_phase >= 1:
                    dWb.wait_all(g)
                    dWT.wait_all(g)
                    g.dma_start(out=dbg["dbg_xnb"].ap(), in_=xnbH.ap()).then_inc(dDbg2.sem, 16)
                    g.dma_start(out=dbg["dbg_xnT"].ap(),
                                in_=bass.AP(xnTH, 0, [[TOK, 128], [1, TOK]])).then_inc(dDbg2.sem, 16)
                    dDbg2.n += 32
                if max_phase >= 3:
                    dSc.wait_all(g)
                    g.dma_start(out=dbg["dbg_disp"].ap(),
                                in_=bass.AP(dispH, 0, [[DM, 512], [1, DM]])).then_inc(dDbg2.sem, 16)
                    dDbg2.n += 16
                if max_phase >= 4:
                    dAW.wait_all(g)
                    g.dma_start(out=dbg["dbg_attn"].ap(),
                                in_=bass.AP(attnH, 0, [[DM, T], [1, DM]])).then_inc(dDbg2.sem, 16)
                    dDbg2.n += 16

            @block.sync
            def _(s):
                s.wait_ge(dDbg.sem, dDbg.n)
                s.wait_ge(dDbg2.sem, dDbg2.n)

        if max_phase < 6:
            dYF = Sem(es, nc, "dYF")
            with contextlib.ExitStack() as phz:
                szy = phz.enter_context(nc.sbuf_tensor("szy", [128, 8192], F32))

                @block.vector
                def _(v):
                    v.memset(szy[:], 0.0).then_inc(VZ.sem, 1)
                    VZ.n += 1

                @block.sync
                def _(s):
                    s.wait_ge(VZ.sem, VZ.n)
                    if max_phase >= 3:
                        dR.wait_all(s)
                        dSc.wait_all(s)
                        s.wait_ge(dZ.sem, dZ.n)
                    if max_phase >= 4:
                        dAW.wait_all(s)
                    if max_phase >= 5:
                        dO2.wait_all(s)
                        s.wait_ge(TF.sem, TF.n)
                    dWb.wait_all(s)
                    dWT.wait_all(s)
                    for k in range(8):
                        s.dma_start(out=bass.AP(y, k * 128 * 8192, [[8192, 128], [1, 8192]]),
                                    in_=szy[:]).then_inc(dYF.sem, 16)
                    s.wait_ge(dYF.sem, 8 * 16)

    return nc


def _host_consts():
    identb = np.eye(128, dtype=ml_dtypes.bfloat16)
    identf = np.eye(128, dtype=np.float32)
    triu = np.triu(np.ones((64, 64), np.float32), 1)
    sel2 = np.zeros((128, 2), ml_dtypes.bfloat16)
    sel2[:64, 0] = 1
    sel2[64:, 1] = 1
    return identb, identf, triu, sel2


def kernel(L0_attn_q, L0_attn_in_proj_weight, L0_attn_in_proj_bias, L0_attn_out_w,
           L0_attn_out_b, L0_attn_rel_pe, L0_ffn_Wg, L0_ffn_w1, L0_ffn_w2,
           _debug=False):
    x = np.ascontiguousarray(np.asarray(L0_attn_q, dtype=np.float32))
    wqkvT = np.ascontiguousarray(np.asarray(L0_attn_in_proj_weight, np.float32).T)
    bqkv = np.ascontiguousarray(np.asarray(L0_attn_in_proj_bias, np.float32))
    outwT = np.ascontiguousarray(np.asarray(L0_attn_out_w, np.float32).T)
    outb = np.ascontiguousarray(np.asarray(L0_attn_out_b, np.float32))
    peT = np.ascontiguousarray(np.asarray(L0_attn_rel_pe, np.float32).T)
    wg = np.ascontiguousarray(np.asarray(L0_ffn_Wg, np.float32))
    w1 = np.ascontiguousarray(np.asarray(L0_ffn_w1, np.float32))
    w2 = np.ascontiguousarray(np.asarray(L0_ffn_w2, np.float32))
    identb, identf, triu, sel2 = _host_consts()

    nc = build_nc(debug=_debug)
    in_maps = []
    for c in range(NCORES):
        xs = np.ascontiguousarray(x[c * BPC:(c + 1) * BPC].reshape(TOK, DM))
        corehot = np.zeros((1, 8), np.float32); corehot[0, c] = 1.0
        prefmask = np.zeros((8, 1), np.float32); prefmask[:c, 0] = 1.0
        in_maps.append({
            "xq": xs, "wqkvT": wqkvT, "bqkv": bqkv, "outwT": outwT, "outb": outb,
            "peT": peT, "wg": wg, "w1": w1, "w2": w2,
            "identb": identb, "identf": identf, "triu": triu,
            "corehot": corehot, "prefmask": prefmask, "sel2": sel2,
        })
    res = run_bass_kernel_spmd(nc, in_maps, core_ids=list(range(NCORES)))
    out = np.concatenate([r["y"].reshape(BPC, T, DM) for r in res.results], axis=0)
    if _debug:
        return out, res
    return out


# version 03
# speedup vs baseline: 1.0067x; 1.0067x over previous
import contextlib
import numpy as np
import ml_dtypes
import concourse.bass as bass
import concourse.mybir as mybir
from concourse.bass_utils import run_bass_kernel_spmd

F32 = mybir.dt.float32
BF16 = mybir.dt.bfloat16
U32 = mybir.dt.uint32
OP = mybir.AluOpType
AF = mybir.ActivationFunctionType

NCORES = 8
BPC = 4
T = 2048
TOK = BPC * T
DM = 1024
NT = TOK // 128
H = 16
DH = 64
NHP = H // 2
NQT = T // 128
NE = 16
CAPL = 1024
CG = 4096
DFF = 4096
PADL, PADM, PADR = 127, 129, 191
PADW = PADL + PADM + PADR
DISP_ROWS = NE * CAPL
DISP_ROWS_PAD = DISP_ROWS + 128
EPS = 1e-5


class DSem2:
    def __init__(self, es, nc, name):
        self.sems = [es.enter_context(nc.semaphore(f"{name}{i}")) for i in range(2)]
        self.cnt = [0, 0]

    def inc(self, inst, t):
        inst.then_inc(self.sems[t % 2], 16)
        self.cnt[t % 2] += 16

    def wait_tile(self, eng, t, per=1):
        eng.wait_ge(self.sems[t % 2], 16 * per * (t // 2 + 1))

    def wait_all(self, eng):
        eng.wait_ge(self.sems[0], self.cnt[0])
        eng.wait_ge(self.sems[1], self.cnt[1])


class Sem:
    def __init__(self, es, nc, name):
        self.sem = es.enter_context(nc.semaphore(name))
        self.n = 0

    def inc(self, inst, k=1):
        inst.then_inc(self.sem, k)
        self.n += k
        return self.n

    def dinc(self, inst):
        inst.then_inc(self.sem, 16)
        self.n += 16
        return self.n

    def wait(self, eng, n=None):
        eng.wait_ge(self.sem, self.n if n is None else n)


def build_nc(debug=False, max_phase=9, nt_lim=None, ln_parts=15, ln_sk=0,
             bpc_lim=None, nhp_lim=None, ne_lim=None):
    NTL = NT if nt_lim is None else nt_lim
    BPCL = BPC if bpc_lim is None else bpc_lim
    NHPL = NHP if nhp_lim is None else nhp_lim
    NEL = NE if ne_lim is None else ne_lim
    nc = bass.Bass("TRN2", target_bir_lowering=False, debug=False)

    xq = nc.dram_tensor("xq", [TOK, DM], F32, kind="ExternalInput")
    wqkvT = nc.dram_tensor("wqkvT", [DM, 3 * DM], F32, kind="ExternalInput")
    bqkv = nc.dram_tensor("bqkv", [3 * DM], F32, kind="ExternalInput")
    outwT = nc.dram_tensor("outwT", [DM, DM], F32, kind="ExternalInput")
    outb = nc.dram_tensor("outb", [DM], F32, kind="ExternalInput")
    peT = nc.dram_tensor("peT", [DH, 129], F32, kind="ExternalInput")
    wg = nc.dram_tensor("wg", [DM, NE], F32, kind="ExternalInput")
    w1 = nc.dram_tensor("w1", [NE, DM, DFF], F32, kind="ExternalInput")
    w2 = nc.dram_tensor("w2", [NE, DFF, DM], F32, kind="ExternalInput")
    identb = nc.dram_tensor("identb", [128, 128], BF16, kind="ExternalInput")
    identf = nc.dram_tensor("identf", [128, 128], F32, kind="ExternalInput")
    triu = nc.dram_tensor("triu", [64, 64], F32, kind="ExternalInput")
    corehot = nc.dram_tensor("corehot", [1, 8], F32, kind="ExternalInput")
    prefmask = nc.dram_tensor("prefmask", [8, 1], F32, kind="ExternalInput")
    sel2 = nc.dram_tensor("sel2", [128, 2], BF16, kind="ExternalInput")

    y = nc.dram_tensor("y", [TOK, DM], F32, kind="ExternalOutput")

    xnbH = nc.dram_tensor("xnbH", [TOK, DM], BF16, kind="Internal")
    xnTH = nc.dram_tensor("xnTH", [8, 128, TOK], BF16, kind="Internal")
    qkH = nc.dram_tensor("qkH", [16, 128, T], BF16, kind="Internal")
    padH = nc.dram_tensor("padH", [2, 2, NQT, 128, PADW + 1], BF16, kind="Internal")
    dispH = nc.dram_tensor("dispH", [DISP_ROWS_PAD, DM], BF16, kind="Internal")
    out2H = nc.dram_tensor("out2H", [DISP_ROWS, DM], F32, kind="Internal")
    attnH = nc.dram_tensor("attnH", [TOK, DM], F32, kind="Internal")
    OTH = nc.dram_tensor("OTH", [8, 128, T], BF16, kind="Internal")
    vH = nc.dram_tensor("vH", [16, 128, DM], BF16, kind="Internal")
    arinH = nc.dram_tensor("arinH", [8, NE], F32, kind="Internal")
    aroutH = nc.dram_tensor("aroutH", [8, NE], F32, kind="Internal")

    dbg = {}
    if debug:
        for name, shape, dt in [
            ("dbg_xnb", [TOK, DM], BF16),
            ("dbg_idx", [128, NT], F32),
            ("dbg_gate", [128, NT], F32),
            ("dbg_loc", [128, NT], F32),
            ("dbg_disp", [512, DM], BF16),
            ("dbg_xnT", [128, TOK], BF16),
            ("dbg_attn", [T, DM], F32),
        ]:
            dbg[name] = nc.dram_tensor(name, shape, dt, kind="ExternalOutput")

    es = contextlib.ExitStack()
    with es:
        s_identb = es.enter_context(nc.sbuf_tensor("s_identb", [128, 128], BF16))
        s_identf = es.enter_context(nc.sbuf_tensor("s_identf", [128, 128], F32))
        s_sel2 = es.enter_context(nc.sbuf_tensor("s_sel2", [128, 2], BF16))
        s_triu = es.enter_context(nc.sbuf_tensor("s_triu", [64, 64], F32))
        s_corehot = es.enter_context(nc.sbuf_tensor("s_corehot", [1, 8], F32))
        s_prefmask = es.enter_context(nc.sbuf_tensor("s_prefmask", [8, 1], F32))
        s_wg = es.enter_context(nc.sbuf_tensor("s_wg", [128, 8, NE], F32))
        s_onesb = es.enter_context(nc.sbuf_tensor("s_onesb", [128, 512], BF16))
        s_onesf = es.enter_context(nc.sbuf_tensor("s_onesf", [128, 8], F32))
        s_onesfr = es.enter_context(nc.sbuf_tensor("s_onesfr", [1, 128], F32))
        s_biasrow = es.enter_context(nc.sbuf_tensor("s_biasrow", [1, 3 * DM], BF16))
        s_outbrow = es.enter_context(nc.sbuf_tensor("s_outbrow", [1, DM], BF16))
        s_peT2 = es.enter_context(nc.sbuf_tensor("s_peT2", [128, 129], BF16))
        swqkv = es.enter_context(nc.sbuf_tensor("swqkv", [128, 8, 3 * DM], BF16))
        soutw = es.enter_context(nc.sbuf_tensor("soutw", [128, 8, DM], BF16))
        s_eps = es.enter_context(nc.sbuf_tensor("s_eps", [128, 1], F32))
        s_idxc = es.enter_context(nc.sbuf_tensor("s_idxc", [128, NT], F32))
        s_gatec = es.enter_context(nc.sbuf_tensor("s_gatec", [128, NT], F32))
        s_locc = es.enter_context(nc.sbuf_tensor("s_locc", [128, NT], F32))
        s_dc = es.enter_context(nc.sbuf_tensor("s_dc", [128, NT], U32))
        s_gB = es.enter_context(nc.sbuf_tensor("s_gB", [128, NT], U32))
        s_offs = es.enter_context(nc.sbuf_tensor("s_offs", [1, NE], F32))

        block = es.enter_context(nc.Block())

        dG0 = Sem(es, nc, "dG0")
        dS0 = Sem(es, nc, "dS0")
        T0 = Sem(es, nc, "T0")
        V0 = Sem(es, nc, "V0")
        S0 = Sem(es, nc, "S0")
        G0 = Sem(es, nc, "G0")

        @block.gpsimd
        def _(g):
            g.dma_start(out=s_biasrow[:], in_=bass.AP(bqkv, 0, [[3 * DM, 1], [1, 3 * DM]])).then_inc(dG0.sem, 16)
            g.dma_start(out=s_outbrow[:], in_=bass.AP(outb, 0, [[DM, 1], [1, DM]])).then_inc(dG0.sem, 16)
            g.dma_start(out=s_peT2[0:64, :], in_=peT.ap()).then_inc(dG0.sem, 16)
            g.dma_start(out=s_peT2[64:128, :], in_=peT.ap()).then_inc(dG0.sem, 16)
            g.dma_start(out=swqkv[:], in_=bass.AP(wqkvT, 0, [[3 * DM, 128], [128 * 3 * DM, 8], [1, 3 * DM]])).then_inc(dG0.sem, 16)
            g.dma_start(out=soutw[:], in_=bass.AP(outwT, 0, [[DM, 128], [128 * DM, 8], [1, DM]])).then_inc(dG0.sem, 16)
            dG0.n += 96

        @block.sync
        def _(s):
            s.dma_start(out=s_identb[:], in_=identb.ap()).then_inc(dS0.sem, 16)
            s.dma_start(out=s_identf[:], in_=identf.ap()).then_inc(dS0.sem, 16)
            s.dma_start(out=s_sel2[:], in_=sel2.ap()).then_inc(dS0.sem, 16)
            s.dma_start(out=s_triu[:], in_=triu.ap()).then_inc(dS0.sem, 16)
            s.dma_start(out=s_corehot[:], in_=corehot.ap()).then_inc(dS0.sem, 16)
            s.dma_start(out=s_prefmask[:], in_=prefmask.ap()).then_inc(dS0.sem, 16)
            s.dma_start(out=s_wg[:], in_=bass.AP(wg, 0, [[NE, 128], [128 * NE, 8], [1, NE]])).then_inc(dS0.sem, 16)
            dS0.n += 7 * 16


        @block.vector
        def _(v):
            v.memset(s_onesb[:], 1.0).then_inc(V0.sem, 1)
            v.memset(s_onesf[:], 1.0).then_inc(V0.sem, 1)
            v.memset(s_eps[:], EPS).then_inc(V0.sem, 1)
            v.memset(s_onesfr[:], 1.0).then_inc(V0.sem, 1)
            V0.n += 4

        @block.scalar
        def _(sc):
            pass




        dX = DSem2(es, nc, "dX")
        dWb = DSem2(es, nc, "dWb")
        dWT = DSem2(es, nc, "dWT")
        V1 = Sem(es, nc, "V1")
        S1 = Sem(es, nc, "S1")
        T1 = Sem(es, nc, "T1")

        if max_phase >= 1:
         with contextlib.ExitStack() as ph1:
            sx = [ph1.enter_context(nc.sbuf_tensor(f"sx{i}", [128, DM], F32)) for i in range(2)]
            sxn = ph1.enter_context(nc.sbuf_tensor("sxn", [128, DM], F32))
            sbn = ph1.enter_context(nc.sbuf_tensor("sbn", [128, 12], F32))
            smv = ph1.enter_context(nc.sbuf_tensor("smv", [128, 2], F32))
            ssd = ph1.enter_context(nc.sbuf_tensor("ssd", [128, 1], F32))
            srstd = ph1.enter_context(nc.sbuf_tensor("srstd", [128, 1], F32))
            sxnTf = ph1.enter_context(nc.sbuf_tensor("sxnTf", [128, DM], F32))
            sxnTb = [ph1.enter_context(nc.sbuf_tensor(f"sxnTb{i}", [128, DM], BF16)) for i in range(2)]
            sxnb = [ph1.enter_context(nc.sbuf_tensor(f"sxnb{i}", [128, DM], BF16)) for i in range(2)]
            slg = ph1.enter_context(nc.sbuf_tensor("slg", [128, NE], F32))
            sge = ph1.enter_context(nc.sbuf_tensor("sge", [128, NE], F32))
            sgex = ph1.enter_context(nc.sbuf_tensor("sgex", [128, NE], F32))
            sgsum = ph1.enter_context(nc.sbuf_tensor("sgsum", [128, 1], F32))
            smax8 = ph1.enter_context(nc.sbuf_tensor("smax8", [128, 8], F32))
            sidx8 = ph1.enter_context(nc.sbuf_tensor("sidx8", [128, 8], U32))
            pT1 = ph1.enter_context(nc.psum_tensor("pT1", [128, DM], F32))
            pG1 = ph1.enter_context(nc.psum_tensor("pG1", [128, NE], F32))

            NV, NS, NTN = 12, 4, 16
            v1b, s1b, t1b = V1.n, S1.n, T1.n

            @block.sync
            def _(s):
                for t in range(NTL + 1):
                    if t < NT:
                        if t >= 2:
                            s.wait_ge(V1.sem, v1b + NV * (t - 2) + 5)
                        dX.inc(s.dma_start(out=sx[t % 2][:], in_=bass.AP(xq, t * 128 * DM, [[DM, 128], [1, DM]])), t)
                    if t >= 1 and (ln_parts & 4):
                        u = t - 1
                        s.wait_ge(S1.sem, s1b + NS * u + 2)
                        dWb.inc(s.dma_start(out=bass.AP(xnbH, u * 128 * DM, [[DM, 128], [1, DM]]),
                                            in_=sxnb[u % 2][:]), u)
                        s.wait_ge(S1.sem, s1b + NS * u + 3)
                        dWT.inc(s.dma_start(out=bass.AP(xnTH, u * 128, [[TOK, 128], [128 * TOK, 8], [1, 128]]),
                                            in_=sxnTb[u % 2][:]), u)

            @block.vector
            def _(v):
                def vemit(thunk):
                    if ln_sk & 4:
                        v.nop().then_inc(V1.sem, 1)
                    else:
                        thunk().then_inc(V1.sem, 1)
                for t in range(NTL):
                    dX.wait_tile(v, t)
                    if t >= 1:
                        v.wait_ge(S1.sem, s1b + NS * (t - 1) + 1)
                    x_ = sx[t % 2]
                    vemit(lambda: v.bn_stats(sbn[:, 0:6], x_[:, 0:512]))
                    vemit(lambda: v.bn_stats(sbn[:, 6:12], x_[:, 512:1024]))
                    v.wait_ge(V1.sem, v1b + NV * t + 2)
                    vemit(lambda: v.bn_aggr(smv[:], sbn[:]))
                    v.wait_ge(S1.sem, s1b + NS * t + 1)
                    vemit(lambda: v.reciprocal(srstd[:], ssd[:]))
                    v.wait_ge(V1.sem, v1b + NV * t + 4)
                    vemit(lambda: v.tensor_scalar(sxn[:], x_[:], smv[:, 0:1], srstd[:], OP.subtract, OP.mult))
                    if ln_parts & 2:
                        v.wait_ge(T1.sem, t1b + NTN * t + 8)
                    vemit(lambda: v.tensor_copy(sxnTf[:], pT1[:]))
                    if ln_parts & 8:
                        v.wait_ge(T1.sem, t1b + NTN * t + 16)
                    vemit(lambda: v.tensor_copy(slg[:], pG1[:]))
                    v.wait_ge(V1.sem, v1b + NV * t + 7)
                    vemit(lambda: v.max(smax8[:], slg[:]))
                    v.wait_ge(V1.sem, v1b + NV * t + 8)
                    vemit(lambda: v.max_index(sidx8[:], smax8[:], slg[:]))
                    v.wait_ge(V1.sem, v1b + NV * t + 9)
                    vemit(lambda: v.tensor_copy(s_idxc[:, t:t + 1], sidx8[:, 0:1]))
                    vemit(lambda: v.tensor_scalar(sge[:], slg[:], smax8[:, 0:1], None, OP.subtract))
                    v.wait_ge(S1.sem, s1b + NS * t + 4)
                    if ln_sk & 1:
                        vemit(lambda: v.reciprocal(s_gatec[:, t:t + 1], smv[:, 1:2]))
                    else:
                        vemit(lambda: v.reciprocal(s_gatec[:, t:t + 1], sgsum[:]))

            @block.scalar
            def _(sc):
                def semit(thunk):
                    if ln_sk & 2:
                        sc.nop().then_inc(S1.sem, 1)
                    else:
                        thunk().then_inc(S1.sem, 1)
                sc.wait_ge(V0.sem, 3)
                for t in range(NTL):
                    sc.wait_ge(V1.sem, v1b + NV * t + 3)
                    semit(lambda: sc.activation(ssd[:], smv[:, 1:2], AF.Sqrt, bias=s_eps[:]))
                    sc.wait_ge(V1.sem, v1b + NV * t + 5)
                    if t >= 2:
                        dWb.wait_tile(sc, t - 2)
                    semit(lambda: sc.activation(sxnb[t % 2][:], sxn[:], AF.Copy))
                    sc.wait_ge(T1.sem, t1b + NTN * t + 8)
                    sc.wait_ge(V1.sem, v1b + NV * t + 6)
                    if t >= 2:
                        dWT.wait_tile(sc, t - 2)
                    semit(lambda: sc.activation(sxnTb[t % 2][:], pT1[:], AF.Copy))
                    sc.wait_ge(V1.sem, v1b + NV * t + 11)
                    if ln_sk & 1:
                        semit(lambda: sc.activation(sgex[:], sge[:], AF.Exp))
                    else:
                        semit(lambda: sc.activation(sgex[:], sge[:], AF.Exp, accum_out=sgsum[:]))

            @block.tensor
            def _(tn):
                tn.wait_ge(dS0.sem, 7 * 16)
                for t in range(NTL):
                    tn.wait_ge(V1.sem, v1b + NV * t + 5)
                    if t >= 1:
                        tn.wait_ge(S1.sem, s1b + NS * (t - 1) + 3)
                    for c in range(8):
                        tn.transpose(pT1[:, c * 128:(c + 1) * 128], sxn[:, c * 128:(c + 1) * 128],
                                     s_identf[:]).then_inc(T1.sem, 1)
                    tn.wait_ge(V1.sem, v1b + NV * t + 6)
                    for fc in range(8):
                        tn.matmul(pG1[:], sxnTf[:, fc * 128:(fc + 1) * 128], s_wg[:, fc, :],
                                  start=(fc == 0), stop=(fc == 7)).then_inc(T1.sem, 1)
                T1.n += NT * NTN

            V1.n = v1b + NV * NTL
            S1.n = s1b + NS * NTL

        PH = [None]
        V2 = Sem(es, nc, "V2")
        S2 = Sem(es, nc, "S2")
        T2 = Sem(es, nc, "T2")
        G2 = Sem(es, nc, "G2")
        dA = Sem(es, nc, "dA")

        if max_phase >= 2:
         with contextlib.ExitStack() as ph2:
            sidxT = ph2.enter_context(nc.sbuf_tensor("sidxT", [64, 128], F32))
            soh = ph2.enter_context(nc.sbuf_tensor("soh", [64, 128], F32))
            scum = ph2.enter_context(nc.sbuf_tensor("scum", [64, 128], F32))
            stmp = ph2.enter_context(nc.sbuf_tensor("stmp", [64, 128], F32))
            stmp2 = ph2.enter_context(nc.sbuf_tensor("stmp2", [64, 128], F32))
            slocT = ph2.enter_context(nc.sbuf_tensor("slocT", [64, 128], F32))
            sones64 = ph2.enter_context(nc.sbuf_tensor("sones64", [64, 128], F32))
            spart = ph2.enter_context(nc.sbuf_tensor("spart", [64, NE], F32))
            stileoff = ph2.enter_context(nc.sbuf_tensor("stileoff", [64, NE], F32))
            scnt = ph2.enter_context(nc.sbuf_tensor("scnt", [1, NE], F32))
            sarin = ph2.enter_context(nc.sbuf_tensor("sarin", [8, NE], F32))
            sdT = ph2.enter_context(nc.sbuf_tensor("sdT", [64, 128], F32))
            strashi = ph2.enter_context(nc.sbuf_tensor("strashi", [64, 128], mybir.dt.int32))
            strash = ph2.enter_context(nc.sbuf_tensor("strash", [64, 128], F32))
            slv64 = ph2.enter_context(nc.sbuf_tensor("slv64", [64, 128], F32))
            sdTc = ph2.enter_context(nc.sbuf_tensor("sdTc", [128, 64], F32))
            p2a = ph2.enter_context(nc.psum_tensor("p2a", [128, 128], F32))
            p2b = ph2.enter_context(nc.psum_tensor("p2b", [64, NE], F32))
            p2c = ph2.enter_context(nc.psum_tensor("p2c", [8, NE], F32))

            @block.tensor
            def _(tn):
                tn.wait_ge(V1.sem, V1.n)
                tn.transpose(p2a[0:64, :], s_idxc[:], s_identf[:]).then_inc(T2.sem, 1)
                T2.n += 1

            @block.gpsimd
            def _(g):
                g.wait_ge(V1.sem, V1.n)
                g.iota(strashi[:], [[1, 128]], base=DISP_ROWS, channel_multiplier=0).then_inc(G2.sem, 1)
                G2.n += 1

            @block.vector
            def _(v):
                v.wait_ge(S1.sem, S1.n)
                dWb.wait_all(v)
                dWT.wait_all(v)
                v.memset(sones64[:], 1.0).then_inc(V2.sem, 1)
                v.memset(slocT[:], 0.0).then_inc(V2.sem, 1)
                v.wait_ge(T2.sem, 1)
                v.tensor_copy(sidxT[:], p2a[0:64, :]).then_inc(V2.sem, 1)
                V2.n += 3

            def _vop(v, thunk):
                v.wait_ge(V2.sem, V2.n)
                V2.n += 1
                thunk().then_inc(V2.sem, 1)

            @block.vector
            def _(v):
                for e in range(NE):
                    _vop(v, lambda: v.tensor_single_scalar(soh[:], sidxT[:], float(e), OP.is_equal))
                    _vop(v, lambda: v.tensor_tensor_scan(scum[:], soh[:], sones64[:], 0.0, OP.add, OP.mult))
                    _vop(v, lambda: v.tensor_copy(spart[:, e:e + 1], scum[:, 127:128]))

            @block.tensor
            def _(tn):
                tn.wait_ge(V2.sem, V2.n)
                tn.matmul(p2b[:], s_triu[:], spart[:], start=True, stop=True).then_inc(T2.sem, 1)
                tn.matmul(p2c[0:1, :], s_onesf[0:64, 0:1], spart[:], start=True, stop=True).then_inc(T2.sem, 1)
                T2.n += 2

            @block.vector
            def _(v):
                v.wait_ge(T2.sem, T2.n)
                _vop(v, lambda: v.tensor_copy(stileoff[:], p2b[:]))
                _vop(v, lambda: v.tensor_copy(scnt[:], p2c[0:1, :]))
                for e in range(NE):
                    _vop(v, lambda: v.tensor_single_scalar(soh[:], sidxT[:], float(e), OP.is_equal))
                    _vop(v, lambda: v.tensor_tensor_scan(scum[:], soh[:], sones64[:], 0.0, OP.add, OP.mult))
                    _vop(v, lambda: v.tensor_scalar(stmp[:], scum[:], stileoff[:, e:e + 1], 1.0, OP.add, OP.subtract))
                    _vop(v, lambda: v.tensor_tensor(stmp2[:], stmp[:], soh[:], OP.mult))
                    _vop(v, lambda: v.tensor_tensor(slocT[:], slocT[:], stmp2[:], OP.add))
                v.wait_ge(G2.sem, 1)
                _vop(v, lambda: v.tensor_copy(strash[:], strashi[:]))
                _vop(v, lambda: v.tensor_single_scalar(stmp[:], slocT[:], float(CAPL - 1), OP.min))
                _vop(v, lambda: v.tensor_single_scalar(stmp2[:], sidxT[:], float(CAPL), OP.mult))
                _vop(v, lambda: v.tensor_tensor(sdT[:], stmp[:], stmp2[:], OP.add))
                _vop(v, lambda: v.tensor_single_scalar(slv64[:], slocT[:], float(CAPL), OP.is_lt))
                _vop(v, lambda: v.tensor_tensor(sdT[:], sdT[:], slv64[:], OP.mult))
                _vop(v, lambda: v.tensor_single_scalar(stmp[:], slv64[:], 1.0, OP.subtract))
                _vop(v, lambda: v.tensor_single_scalar(stmp[:], stmp[:], -1.0, OP.mult))
                _vop(v, lambda: v.tensor_tensor(stmp[:], stmp[:], strash[:], OP.mult))
                _vop(v, lambda: v.tensor_tensor(sdT[:], sdT[:], stmp[:], OP.add))

            sgBf = ph2.enter_context(nc.sbuf_tensor("sgBf", [128, NT], F32))

            @block.tensor
            def _(tn):
                tn.wait_ge(V2.sem, V2.n)
                tn.matmul(p2c[:], s_corehot[:], scnt[:], start=True, stop=True).then_inc(T2.sem, 1)
                tn.transpose(p2a[:, 0:64], sdT[:], s_identf[0:64, 0:64]).then_inc(T2.sem, 1)
                T2.n += 2

            @block.vector
            def _(v):
                v.wait_ge(T2.sem, T2.n)
                _vop(v, lambda: v.tensor_copy(sarin[:], p2c[:]))
                _vop(v, lambda: v.tensor_copy(sdTc[:], p2a[:, 0:64]))
                _vop(v, lambda: v.tensor_copy(s_dc[:], sdTc[:]))

            @block.tensor
            def _(tn):
                tn.wait_ge(V2.sem, V2.n)
                tn.transpose(p2a[:, 0:64], slocT[:], s_identf[0:64, 0:64]).then_inc(T2.sem, 1)
                T2.n += 1

            @block.vector
            def _(v):
                v.wait_ge(T2.sem, T2.n)
                _vop(v, lambda: v.tensor_copy(s_locc[:], p2a[:, 0:64]))
                _vop(v, lambda: v.tensor_single_scalar(sdTc[:], s_locc[:], float(CAPL - 1), OP.min))
                _vop(v, lambda: v.tensor_single_scalar(sgBf[:], s_idxc[:], float(CAPL), OP.mult))
                _vop(v, lambda: v.tensor_tensor(sgBf[:], sgBf[:], sdTc[:], OP.add))
                _vop(v, lambda: v.tensor_copy(s_gB[:], sgBf[:]))

            @block.gpsimd
            def _(g):
                g.wait_ge(V2.sem, V2.n)
                g.dma_start(out=arinH.ap(), in_=sarin[:]).then_inc(dA.sem, 16)
                dA.n += 16
                g.wait_ge(dA.sem, dA.n)
                g.collective_compute(
                    "AllReduce", OP.add,
                    replica_groups=[list(range(NCORES))],
                    ins=[arinH.ap()], outs=[aroutH.ap()],
                ).then_inc(G2.sem, 1)
                G2.n += 1

        dZ = Sem(es, nc, "dZ")
        dR = DSem2(es, nc, "dR")
        dSc = DSem2(es, nc, "dSc")
        VZ = Sem(es, nc, "VZ")

        if max_phase >= 3:
         with contextlib.ExitStack() as ph3:
            szero = ph3.enter_context(nc.sbuf_tensor("szero", [128, 8192], BF16))
            sxr = [ph3.enter_context(nc.sbuf_tensor(f"sxr{i}", [128, DM], BF16)) for i in range(2)]

            @block.vector
            def _(v):
                v.wait_ge(dA.sem, dA.n)
                v.memset(szero[:], 0.0).then_inc(VZ.sem, 1)
                VZ.n += 1

            @block.sync
            def _(s):
                s.wait_ge(VZ.sem, 1)
                for k in range(16):
                    s.dma_start(out=bass.AP(dispH, k * 128 * 8192, [[8192, 128], [1, 8192]]),
                                in_=szero[:]).then_inc(dZ.sem, 16)
                s.dma_start(out=bass.AP(dispH, 16 * 128 * 8192, [[DM, 128], [1, DM]]),
                            in_=szero[:, 0:DM]).then_inc(dZ.sem, 16)
                dZ.n += 17 * 16
                dWb.wait_all(s)
                for t in range(NT):
                    if t >= 2:
                        dSc.wait_tile(s, t - 2)
                    dR.inc(s.dma_start(out=sxr[t % 2][:], in_=bass.AP(xnbH, t * 128 * DM, [[DM, 128], [1, DM]])), t)

            @block.gpsimd
            def _(g):
                g.wait_ge(V2.sem, V2.n)
                g.wait_ge(dZ.sem, 17 * 16)
                for t in range(NT):
                    dR.wait_tile(g, t)
                    dSc.inc(g.indirect_dma_start(
                        out=dispH.ap(), out_offset=bass.IndirectOffsetOnAxis(ap=s_dc[:, t:t + 1], axis=0),
                        in_=sxr[t % 2][:], in_offset=None,
                    ), t)

        dAt = Sem(es, nc, "dAt")
        TA = Sem(es, nc, "TA")
        VA = Sem(es, nc, "VA")
        SA = Sem(es, nc, "SA")
        dXT = DSem2(es, nc, "dXT")
        dQKw = DSem2(es, nc, "dQKw")
        dQKr = DSem2(es, nc, "dQKr")
        dPW2 = DSem2(es, nc, "dPW2")
        dSH = DSem2(es, nc, "dSH")
        dOTw = DSem2(es, nc, "dOTw")
        dOTr = DSem2(es, nc, "dOTr")
        dAW = DSem2(es, nc, "dAW")
        dVW = DSem2(es, nc, "dVW")
        dVr = DSem2(es, nc, "dVr")

        def _segs(qt, half):
            lo = max(0, 128 * qt - 64)
            hi = min(T, 128 * qt + 192)
            h0, h1 = 1024 * half, 1024 * half + 1024
            out = []
            for (a, b2, kind) in [(0, lo, "L"), (lo, hi, "B"), (hi, T, "R")]:
                aa, bb = max(a, h0), min(b2, h1)
                if aa < bb:
                    out.append((aa, bb, kind))
            return out

        if max_phase >= 4 and True:
         with contextlib.ExitStack() as pha:
            sxnT = pha.enter_context(nc.sbuf_tensor("sxnT", [128, 8, T], BF16))
            svst = [pha.enter_context(nc.sbuf_tensor(f"svst{i}", [128, DM], BF16)) for i in range(2)]
            sVr = [pha.enter_context(nc.sbuf_tensor(f"sVr{i}", [128, 16, 128], BF16)) for i in range(2)]
            sqkst = [pha.enter_context(nc.sbuf_tensor(f"sqkst{i}", [128, T], BF16)) for i in range(2)]
            sq = [pha.enter_context(nc.sbuf_tensor(f"sq{i}", [128, T], BF16)) for i in range(2)]
            sk = [pha.enter_context(nc.sbuf_tensor(f"sk{i}", [128, T], BF16)) for i in range(2)]
            spad = [pha.enter_context(nc.sbuf_tensor(f"spad{i}", [128, PADW + 1], BF16)) for i in range(2)]
            sshear = [pha.enter_context(nc.sbuf_tensor(f"sshear{i}", [128, 256], BF16)) for i in range(2)]
            sA = pha.enter_context(nc.sbuf_tensor("sA", [128, T], BF16))
            sAT = [pha.enter_context(nc.sbuf_tensor(f"sAT{i}", [128, 16, 512], BF16)) for i in range(2)]
            sOTst = [pha.enter_context(nc.sbuf_tensor(f"sOTst{i}", [128, 512], BF16)) for i in range(2)]
            sOTr = [pha.enter_context(nc.sbuf_tensor(f"sOTr{i}", [128, 8, 128], BF16)) for i in range(2)]
            sattst = [pha.enter_context(nc.sbuf_tensor(f"sattst{i}", [128, 512], F32)) for i in range(2)]
            sB0 = pha.enter_context(nc.sbuf_tensor("sB0", [128, 2, NQT], BF16))
            sB128 = pha.enter_context(nc.sbuf_tensor("sB128", [128, 2, NQT], BF16))
            smx4 = pha.enter_context(nc.sbuf_tensor("smx4", [128, 4], F32))
            sMc = pha.enter_context(nc.sbuf_tensor("sMc", [128, 1], F32))
            sMnc = pha.enter_context(nc.sbuf_tensor("sMnc", [128, 1], F32))
            smxB = pha.enter_context(nc.sbuf_tensor("smxB", [128, 1], F32))
            sbL = pha.enter_context(nc.sbuf_tensor("sbL", [128, 1], F32))
            sbR = pha.enter_context(nc.sbuf_tensor("sbR", [128, 1], F32))
            sacc = pha.enter_context(nc.sbuf_tensor("sacc", [128, 8], F32))
            srow = pha.enter_context(nc.sbuf_tensor("srow", [128, 1], F32))
            srec = pha.enter_context(nc.sbuf_tensor("srec", [128, 1], F32))
            ps0 = pha.enter_context(nc.psum_tensor("ps0", [128, 1024], F32))
            ps1 = pha.enter_context(nc.psum_tensor("ps1", [128, 1024], F32))
            patp = pha.enter_context(nc.psum_tensor("patp", [128, 2048], BF16))
            pav = pha.enter_context(nc.psum_tensor("pav", [128, 512], F32))
            paux = pha.enter_context(nc.psum_tensor("paux", [128, 512], F32))

            R = {}

            @block.vector
            def _(v):
                v.wait_ge(dZ.sem, dZ.n)
                dR.wait_all(v)
                dSc.wait_all(v)
                v.nop().then_inc(VA.sem, 1)
                VA.n += 1

            @block.scalar
            def _(sc):
                sc.wait_ge(VA.sem, VA.n)
                sc.nop().then_inc(SA.sem, 1)
                SA.n += 1

            @block.tensor
            def _(tn):
                tn.wait_ge(VA.sem, VA.n)
                tn.wait_ge(dG0.sem, dG0.n)
                tn.nop().then_inc(TA.sem, 1)
                TA.n += 1

            qk_tiles = [ps0[:, 0:512], ps0[:, 512:1024], ps1[:, 0:512], ps1[:, 512:1024]]

            if NHPL < NHP or BPCL < BPC:
                dZO = Sem(es, nc, "dZO")
                szatt = pha.enter_context(nc.sbuf_tensor("szatt", [128, DM], F32))

                @block.vector
                def _(v):
                    v.wait_ge(VA.sem, VA.n)
                    v.memset(sA[:], 0.0).then_inc(VA.sem, 1)
                    v.memset(szatt[:], 0.0).then_inc(VA.sem, 1)
                    VA.n += 2

                @block.sync
                def _(s):
                    s.wait_ge(VA.sem, VA.n)
                    for k8 in range(8):
                        s.dma_start(out=bass.AP(OTH, k8 * 128 * T, [[T, 128], [1, T]]),
                                    in_=sA[:]).then_inc(dZO.sem, 16)
                    dZO.n += 8 * 16
                    for k8 in range(64):
                        s.dma_start(out=bass.AP(attnH, k8 * 128 * DM, [[DM, 128], [1, DM]]),
                                    in_=szatt[:]).then_inc(dZO.sem, 16)
                    dZO.n += 64 * 16

                @block.tensor
                def _(tn):
                    tn.wait_ge(dZO.sem, dZO.n)
                    tn.nop().then_inc(TA.sem, 1)
                    TA.n += 1

            for b in range(BPCL):
                @block.sync
                def _(s, b=b):
                    if b == 0:
                        s.wait_ge(dZ.sem, dZ.n)
                        dR.wait_all(s)
                        dSc.wait_all(s)
                        dWT.wait_all(s)
                    if b >= 1:
                        s.wait_ge(TA.sem, R[("xnT_free", b - 1)])
                    for fc in range(8):
                        dXT.inc(s.dma_start(
                            out=sxnT[:, fc, :],
                            in_=bass.AP(xnTH, fc * 128 * TOK + b * T, [[TOK, 128], [1, T]])), b)

                for oc in range(16):
                    @block.tensor
                    def _(tn, b=b, oc=oc):
                        if oc == 0:
                            dXT.wait_tile(tn, b, per=8)
                            if b >= 1:
                                tn.wait_ge(VA.sem, R[("opev", b - 1, 15)])
                        if oc >= 1:
                            tn.wait_ge(SA.sem, R[("qkevac", b, oc - 1)])
                        for fc in range(8):
                            for tm in range(4):
                                tn.matmul(qk_tiles[tm], swqkv[:, fc, oc * 128:(oc + 1) * 128],
                                          sxnT[:, fc, tm * 512:(tm + 1) * 512],
                                          start=(fc == 0), stop=False)
                        for tm in range(4):
                            mm = tn.matmul(qk_tiles[tm], s_biasrow[0:1, oc * 128:(oc + 1) * 128],
                                           s_onesb[0:1, 0:512],
                                           start=False, stop=True)
                            if tm == 3:
                                mm.then_inc(TA.sem, 1)
                                TA.n += 1
                        R[("qkmm", b, oc)] = TA.n

                    @block.scalar
                    def _(sc, b=b, oc=oc):
                        sc.wait_ge(TA.sem, R[("qkmm", b, oc)])
                        gi = b * 16 + oc
                        if gi >= 2:
                            dQKw.wait_tile(sc, gi - 2)
                        scale = 0.125 if oc < 8 else 1.0
                        for tm in range(4):
                            sc.activation(sqkst[oc % 2][:, tm * 512:(tm + 1) * 512], qk_tiles[tm],
                                          AF.Copy, scale=scale).then_inc(SA.sem, 1)
                            SA.n += 1
                        R[("qkevac", b, oc)] = SA.n

                    @block.sync
                    def _(s, b=b, oc=oc):
                        s.wait_ge(SA.sem, R[("qkevac", b, oc)])
                        dQKw.inc(s.dma_start(out=bass.AP(qkH, oc * 128 * T, [[T, 128], [1, T]]),
                                             in_=sqkst[oc % 2][:]), b * 16 + oc)
                        R[("qkw_tot", b)] = (dQKw.cnt[0], dQKw.cnt[1])

                for tt in range(16):
                    @block.tensor
                    def _(tn, b=b, tt=tt):
                        if tt >= 1:
                            tn.wait_ge(SA.sem, R[("vevac", b, tt - 1)])
                        else:
                            tn.wait_ge(SA.sem, R[("qkevac", b, 15)])
                        for fc in range(8):
                            for om in range(2):
                                tn.matmul(qk_tiles[om], sxnT[:, fc, tt * 128:(tt + 1) * 128],
                                          swqkv[:, fc, 2048 + om * 512:2048 + (om + 1) * 512],
                                          start=(fc == 0), stop=False)
                        for om in range(2):
                            mm = tn.matmul(qk_tiles[om], s_onesb[0:1, 0:128],
                                           s_biasrow[0:1, 2048 + om * 512:2048 + (om + 1) * 512],
                                           start=False, stop=True)
                            if om == 1:
                                mm.then_inc(TA.sem, 1)
                                TA.n += 1
                        R[("vmm", b, tt)] = TA.n

                    @block.scalar
                    def _(sc, b=b, tt=tt):
                        sc.wait_ge(TA.sem, R[("vmm", b, tt)])
                        gv = b * 16 + tt
                        if gv >= 2:
                            dVW.wait_tile(sc, gv - 2)
                        for om in range(2):
                            sc.activation(svst[tt % 2][:, om * 512:(om + 1) * 512], qk_tiles[om],
                                          AF.Copy).then_inc(SA.sem, 1)
                            SA.n += 1
                        R[("vevac", b, tt)] = SA.n

                    @block.sync
                    def _(s, b=b, tt=tt):
                        s.wait_ge(SA.sem, R[("vevac", b, tt)])
                        if b >= 1 and tt == 0:
                            dVr.wait_all(s)
                        dVW.inc(s.dma_start(out=bass.AP(vH, tt * 128 * DM, [[DM, 128], [1, DM]]),
                                             in_=svst[tt % 2][:]), b * 16 + tt)
                        R[("vw_tot", b)] = (dVW.cnt[0], dVW.cnt[1])

                R[("xnT_free", b)] = R[("vmm", b, 15)]

                for hp in range(NHPL):
                    hpg = b * NHPL + hp

                    @block.sync
                    def _(s, b=b, hp=hp, hpg=hpg):
                        s.wait_ge(dQKw.sems[0], R[("qkw_tot", b)][0])
                        s.wait_ge(dQKw.sems[1], R[("qkw_tot", b)][1])
                        if hpg >= 2:
                            s.wait_ge(TA.sem, R[("pair_free_t", hpg - 2)])
                            s.wait_ge(SA.sem, R[("pair_free_s", hpg - 2)])
                        dQKr.inc(s.dma_start(out=sq[hpg % 2][:],
                                             in_=bass.AP(qkH, hp * 128 * T, [[T, 128], [1, T]])), hpg)
                        dQKr.inc(s.dma_start(out=sk[hpg % 2][:],
                                             in_=bass.AP(qkH, (8 + hp) * 128 * T, [[T, 128], [1, T]])), hpg)
                        s.wait_ge(dVW.sems[0], R[("vw_tot", b)][0])
                        s.wait_ge(dVW.sems[1], R[("vw_tot", b)][1])
                        if hpg >= 2:
                            s.wait_ge(TA.sem, R[("avmm", hpg - 2, 3)])
                        dVr.inc(s.dma_start(out=sVr[hpg % 2][:],
                                            in_=bass.AP(vH, hp * 128, [[DM, 128], [128 * DM, 16], [1, 128]])), hpg)

                    for hh in range(2):
                        for qt in range(NQT):
                            @block.tensor
                            def _(tn, hpg=hpg, hh=hh, qt=qt):
                                if hh == 0 and qt == 0:
                                    dQKr.wait_tile(tn, hpg, per=2)
                                    if hpg >= 1:
                                        tn.wait_ge(VA.sem, R[("aux_free", hpg - 1)])
                                        tn.wait_ge(VA.sem, R[("pair_free_v", hpg - 1)])
                                elif qt >= 1:
                                    tn.wait_ge(VA.sem, R[("padbuild", hpg, hh, qt - 1)])
                                else:
                                    tn.wait_ge(VA.sem, R[("padbuild", hpg, hh - 1, NQT - 1)])
                                tn.matmul(paux[:, 0:129], sq[hpg % 2][64 * hh:64 * hh + 64, qt * 128:(qt + 1) * 128],
                                          s_peT2[64 * hh:64 * hh + 64, :], start=True, stop=True,
                                          tile_position=(64 * hh, 0)).then_inc(TA.sem, 1)
                                TA.n += 1
                                R[("Bmm", hpg, hh, qt)] = TA.n

                            @block.vector
                            def _(v, hpg=hpg, hh=hh, qt=qt):
                                v.wait_ge(TA.sem, R[("Bmm", hpg, hh, qt)])
                                gpq = (hpg * 2 + hh) * NQT + qt
                                if gpq >= 2:
                                    dPW2.wait_tile(v, gpq - 2)
                                par = gpq % 2
                                v.tensor_copy(spad[par][:, PADL:PADL + PADM], paux[:, 0:129]).then_inc(VA.sem, 1)
                                VA.n += 1
                                v.tensor_scalar(spad[par][:, 0:PADL], s_onesb[:, 0:PADL],
                                                paux[:, 0:1], None, OP.mult).then_inc(VA.sem, 1)
                                VA.n += 1
                                v.tensor_scalar(spad[par][:, PADL + PADM:PADW],
                                                s_onesb[:, 0:PADR],
                                                paux[:, 128:129], None, OP.mult).then_inc(VA.sem, 1)
                                VA.n += 1
                                v.tensor_copy(sB0[:, hh, qt:qt + 1], paux[:, 0:1]).then_inc(VA.sem, 1)
                                v.tensor_copy(sB128[:, hh, qt:qt + 1], paux[:, 128:129]).then_inc(VA.sem, 1)
                                VA.n += 2
                                R[("padbuild", hpg, hh, qt)] = VA.n

                            @block.sync
                            def _(s, b=b, hp=hp, hpg=hpg, hh=hh, qt=qt):
                                s.wait_ge(VA.sem, R[("padbuild", hpg, hh, qt)])
                                gpq = (hpg * 2 + hh) * NQT + qt
                                dPW2.inc(s.dma_start(
                                    out=bass.AP(padH, (((hpg % 2) * 2 + hh) * NQT + qt) * 128 * (PADW + 1), [[PADW + 1, 128], [1, PADW]]),
                                    in_=spad[gpq % 2][:, 0:PADW]), gpq)

                    R[("aux_free", hpg)] = R[("padbuild", hpg, 1, NQT - 1)]

                    for qq in range(4):
                        for hh in range(2):
                          for qtl in range(4):
                            qt = qq * 4 + qtl
                            gtau = R.get("tau_ctr", 0)
                            R["tau_ctr"] = gtau + 1
                            lo = max(0, 128 * qt - 64)
                            hi = min(T, 128 * qt + 192)

                            chsegs = []
                            for c in range(4):
                                c0, c1 = c * 512, (c + 1) * 512
                                segs = []
                                for (a, b2, kind) in [(0, lo, "L"), (lo, hi, "B"), (hi, T, "R")]:
                                    aa, bb = max(a, c0), min(b2, c1)
                                    if aa < bb:
                                        segs.append((aa, bb, kind))
                                chsegs.append(segs)

                            @block.sync
                            def _(s, hpg=hpg, hh=hh, qt=qt, gtau=gtau, lo=lo, hi=hi):
                                dPW2.wait_tile(s, (hpg * 2 + hh) * NQT + qt)
                                if gtau >= 2:
                                    s.wait_ge(VA.sem, R[("banddone", gtau - 2)])
                                W = hi - lo
                                base = (((hpg % 2) * 2 + hh) * NQT + qt) * 128 * (PADW + 1) + (lo - 128 * qt + 191)
                                dSH.inc(s.dma_start(out=sshear[gtau % 2][:, 0:W],
                                                    in_=bass.AP(padH, base, [[PADW, 128], [1, W]])), gtau)

                            @block.tensor
                            def _(tn, hpg=hpg, hh=hh, qt=qt, gtau=gtau, b=b):
                                for c in range(4):
                                    if gtau >= 1:
                                        tn.wait_ge(SA.sem, R[("expc", gtau - 1, c)])
                                    if hpg == b * NHPL and hh == 0 and qt == 0 and c == 0:
                                        tn.wait_ge(SA.sem, R[("vevac", b, 15)])
                                    tn.matmul(qk_tiles[c],
                                              sq[hpg % 2][64 * hh:64 * hh + 64, qt * 128:(qt + 1) * 128],
                                              sk[hpg % 2][64 * hh:64 * hh + 64, c * 512:(c + 1) * 512],
                                              start=True, stop=True,
                                              tile_position=(64 * hh, 0)).then_inc(TA.sem, 1)
                                    TA.n += 1
                                    R[("smm", gtau, c)] = TA.n
                                if gtau >= 1:
                                    tn.wait_ge(VA.sem, R[("scaleA", gtau - 1)])
                                    if gtau >= 2:
                                        tn.wait_ge(VA.sem, R[("ATev", gtau - 2)])
                                    for jc in range(16):
                                        mm = tn.transpose(patp[:, jc * 128:(jc + 1) * 128],
                                                          sA[:, jc * 128:(jc + 1) * 128],
                                                          s_identb[:])
                                        if jc == 15:
                                            mm.then_inc(TA.sem, 1)
                                            TA.n += 1
                                    R[("ATmm", gtau - 1)] = TA.n

                            for c in range(4):
                                @block.vector
                                def _(v, hpg=hpg, gtau=gtau, c=c, chsegs=chsegs, lo=lo):
                                    v.wait_ge(TA.sem, R[("smm", gtau, c)])
                                    for (a, bb, kind) in chsegs[c]:
                                        if kind == "B":
                                            dSH.wait_tile(v, gtau)
                                            v.tensor_tensor(qk_tiles[c][:, a - c * 512:bb - c * 512],
                                                            qk_tiles[c][:, a - c * 512:bb - c * 512],
                                                            sshear[gtau % 2][:, a - lo:bb - lo],
                                                            OP.add).then_inc(VA.sem, 1)
                                            VA.n += 1
                                    v.wait_ge(VA.sem, VA.n)
                                    v.tensor_reduce(smx4[:, c:c + 1], qk_tiles[c], axis=mybir.AxisListType.X,
                                                    op=OP.max).then_inc(VA.sem, 1)
                                    VA.n += 1
                                    R[("chmax", gtau, c)] = VA.n

                            @block.vector
                            def _(v, hpg=hpg, hh=hh, qt=qt, gtau=gtau):
                                if gtau >= 1:
                                    v.wait_ge(SA.sem, R[("expdone", gtau - 1)])
                                v.wait_ge(VA.sem, VA.n)
                                v.tensor_reduce(sMc[:], smx4[:], axis=mybir.AxisListType.X,
                                                op=OP.max).then_inc(VA.sem, 1)
                                VA.n += 1
                                v.tensor_tensor(smxB[:], sB0[:, hh, qt:qt + 1], sB128[:, hh, qt:qt + 1],
                                                OP.max).then_inc(VA.sem, 1)
                                VA.n += 1
                                v.wait_ge(VA.sem, VA.n)
                                v.tensor_single_scalar(smxB[:], smxB[:], 0.0, OP.max).then_inc(VA.sem, 1)
                                VA.n += 1
                                v.wait_ge(VA.sem, VA.n)
                                v.tensor_tensor(sMc[:], sMc[:], smxB[:], OP.add).then_inc(VA.sem, 1)
                                VA.n += 1
                                v.wait_ge(VA.sem, VA.n)
                                v.tensor_single_scalar(sMnc[:], sMc[:], -1.0, OP.mult).then_inc(VA.sem, 1)
                                VA.n += 1
                                v.tensor_scalar(sbL[:], sB0[:, hh, qt:qt + 1], sMc[:], None,
                                                OP.subtract).then_inc(VA.sem, 1)
                                v.tensor_scalar(sbR[:], sB128[:, hh, qt:qt + 1], sMc[:], None,
                                                OP.subtract).then_inc(VA.sem, 1)
                                VA.n += 2
                                R[("Mdone", gtau)] = VA.n
                                R[("banddone", gtau)] = VA.n

                            @block.scalar
                            def _(sc, gtau=gtau, chsegs=chsegs):
                                sc.wait_ge(VA.sem, R[("Mdone", gtau)])
                                if gtau >= 1:
                                    sc.wait_ge(TA.sem, R[("ATmm", gtau - 1)])
                                slot = 0
                                for c in range(4):
                                    for (a, bb, kind) in chsegs[c]:
                                        bias = {"L": sbL[:], "B": sMnc[:], "R": sbR[:]}[kind]
                                        sc.activation(sA[:, a:bb], qk_tiles[c][:, a - c * 512:bb - c * 512],
                                                      AF.Exp, bias=bias,
                                                      accum_out=sacc[:, slot:slot + 1]).then_inc(SA.sem, 1)
                                        SA.n += 1
                                        slot += 1
                                    R[("expc", gtau, c)] = SA.n
                                R[("slot", gtau)] = slot
                                R[("expdone", gtau)] = SA.n

                            R[("tauinfo", gtau)] = (hh, qtl)

                            @block.vector
                            def _(v, gtau=gtau):
                                if gtau >= 1:
                                    v.wait_ge(TA.sem, R[("ATmm", gtau - 1)])
                                    hh_p, qtl_p = R[("tauinfo", gtau - 1)]
                                    v.tensor_copy(
                                        bass.AP(sAT[hh_p], qtl_p * 128, [[16 * 512, 128], [512, 16], [1, 128]]),
                                        patp[:]).then_inc(VA.sem, 1)
                                    VA.n += 1
                                    R[("ATev", gtau - 1)] = VA.n
                                v.wait_ge(SA.sem, R[("expdone", gtau)])
                                ns = R[("slot", gtau)]
                                v.tensor_reduce(srow[:], sacc[:, 0:ns], axis=mybir.AxisListType.X,
                                                op=OP.add).then_inc(VA.sem, 1)
                                VA.n += 1
                                v.wait_ge(VA.sem, VA.n)
                                v.reciprocal(srec[:], srow[:]).then_inc(VA.sem, 1)
                                VA.n += 1
                                v.wait_ge(VA.sem, VA.n)
                                v.tensor_scalar(sA[:], sA[:], srec[:], None, OP.mult).then_inc(VA.sem, 1)
                                VA.n += 1
                                R[("scaleA", gtau)] = VA.n

                        lastt = R["tau_ctr"] - 1

                        @block.tensor
                        def _(tn, lastt=lastt):
                            tn.wait_ge(VA.sem, R[("scaleA", lastt)])
                            if lastt >= 1:
                                tn.wait_ge(VA.sem, R[("ATev", lastt - 1)])
                            for jc in range(16):
                                mm = tn.transpose(patp[:, jc * 128:(jc + 1) * 128], sA[:, jc * 128:(jc + 1) * 128],
                                                  s_identb[:])
                                if jc == 15:
                                    mm.then_inc(TA.sem, 1)
                                    TA.n += 1
                            R[("ATmm", lastt)] = TA.n

                        @block.vector
                        def _(v, lastt=lastt):
                            v.wait_ge(TA.sem, R[("ATmm", lastt)])
                            hh_p, qtl_p = R[("tauinfo", lastt)]
                            v.tensor_copy(
                                bass.AP(sAT[hh_p], qtl_p * 128, [[16 * 512, 128], [512, 16], [1, 128]]),
                                patp[:]).then_inc(VA.sem, 1)
                            VA.n += 1
                            R[("ATev", lastt)] = VA.n

                        @block.tensor
                        def _(tn, hpg=hpg, hp=hp, qq=qq, b=b):
                            tn.wait_ge(VA.sem, R[("ATev", R["tau_ctr"] - 1)])
                            if qq >= 1:
                                tn.wait_ge(VA.sem, R[("otev", hpg, qq - 1)])
                            elif hpg >= 1:
                                tn.wait_ge(VA.sem, R[("otev", hpg - 1, 3)])
                            if qq == 0:
                                dVr.wait_tile(tn, hpg)
                            for jc in range(16):
                                tn.matmul(pav[0:64, :], sVr[hpg % 2][:, jc, 0:64],
                                          sAT[0][:, jc, :], start=(jc == 0), stop=(jc == 15),
                                          tile_position=(0, 0), skip_group_check=True)
                                mm = tn.matmul(pav[64:128, :], sVr[hpg % 2][:, jc, 64:128],
                                               sAT[1][:, jc, :], start=(jc == 0), stop=(jc == 15),
                                               tile_position=(0, 64), skip_group_check=True)
                                if jc == 15:
                                    mm.then_inc(TA.sem, 1)
                                    TA.n += 1
                            R[("avmm", hpg, qq)] = TA.n

                        @block.vector
                        def _(v, hpg=hpg, qq=qq):
                            v.wait_ge(TA.sem, R[("avmm", hpg, qq)])
                            gq = hpg * 4 + qq
                            if gq >= 2:
                                dOTw.wait_tile(v, gq - 2)
                            v.tensor_copy(sOTst[gq % 2][:], pav[:]).then_inc(VA.sem, 1)
                            VA.n += 1
                            R[("otev", hpg, qq)] = VA.n

                        @block.sync
                        def _(s, hpg=hpg, hp=hp, qq=qq):
                            s.wait_ge(VA.sem, R[("otev", hpg, qq)])
                            gq = hpg * 4 + qq
                            dOTw.inc(s.dma_start(out=bass.AP(OTH, hp * 128 * T + qq * 512, [[T, 128], [1, 512]]),
                                                 in_=sOTst[gq % 2][:]), gq)
                            R[("otw_tot", b)] = (dOTw.cnt[0], dOTw.cnt[1])

                    R[("pair_free_t", hpg)] = R[("avmm", hpg, 3)]
                    R[("pair_free_v", hpg)] = R[("ATev", (hpg * 2 + 1) * NQT + NQT - 1)]
                    R[("pair_free_s", hpg)] = R[("expdone", R["tau_ctr"] - 1)]

                for tt in range(16):
                    @block.sync
                    def _(s, b=b, tt=tt):
                        s.wait_ge(dOTw.sems[0], R[("otw_tot", b)][0])
                        s.wait_ge(dOTw.sems[1], R[("otw_tot", b)][1])
                        gtt = b * 16 + tt
                        if gtt >= 2:
                            pb, ptt = divmod(gtt - 2, 16)
                            s.wait_ge(TA.sem, R[("opmm", pb, ptt)])
                        dOTr.inc(s.dma_start(out=sOTr[tt % 2][:],
                                             in_=bass.AP(OTH, tt * 128, [[T, 128], [T * 128, 8], [1, 128]])), b * 16 + tt)

                    @block.tensor
                    def _(tn, b=b, tt=tt):
                        dOTr.wait_tile(tn, b * 16 + tt)
                        if tt == 0:
                            tn.wait_ge(SA.sem, R[("pair_free_s", b * NHPL + NHPL - 1)])
                        gtt2 = b * 16 + tt
                        if gtt2 >= 1:
                            pb2, ptt2 = divmod(gtt2 - 1, 16)
                            tn.wait_ge(VA.sem, R[("opev", pb2, ptt2)])
                        for om in range(2):
                            for hp2 in range(8):
                                tn.matmul(ps0[:, om * 512:(om + 1) * 512], sOTr[tt % 2][:, hp2, :],
                                          soutw[:, hp2, om * 512:(om + 1) * 512],
                                          start=(hp2 == 0), stop=False)
                            mm = tn.matmul(ps0[:, om * 512:(om + 1) * 512], s_onesb[0:1, 0:128],
                                           s_outbrow[0:1, om * 512:(om + 1) * 512],
                                           start=False, stop=True)
                            if om == 1:
                                mm.then_inc(TA.sem, 1)
                                TA.n += 1
                        R[("opmm", b, tt)] = TA.n

                    @block.vector
                    def _(v, b=b, tt=tt):
                        v.wait_ge(TA.sem, R[("opmm", b, tt)])
                        for om in range(2):
                            gw = (b * 16 + tt) * 2 + om
                            if gw >= 2:
                                dAW.wait_tile(v, gw - 2)
                            v.tensor_copy(sattst[gw % 2][:], ps0[:, om * 512:(om + 1) * 512]).then_inc(VA.sem, 1)
                            VA.n += 1
                        R[("opev", b, tt)] = VA.n

                    @block.sync
                    def _(s, b=b, tt=tt):
                        s.wait_ge(VA.sem, R[("opev", b, tt)])
                        for om in range(2):
                            gw = (b * 16 + tt) * 2 + om
                            dAW.inc(s.dma_start(
                                out=bass.AP(attnH, (b * T + tt * 128) * DM + om * 512, [[DM, 128], [1, 512]]),
                                in_=sattst[gw % 2][:]), gw)

        TF = Sem(es, nc, "TF")
        VF = Sem(es, nc, "VF")
        SF = Sem(es, nc, "SF")
        dSL = DSem2(es, nc, "dSL")
        dW1 = DSem2(es, nc, "dW1")
        dW2 = DSem2(es, nc, "dW2")
        dO2 = DSem2(es, nc, "dO2")

        if max_phase >= 5:
         with contextlib.ExitStack() as phf:
            sslab = [phf.enter_context(nc.sbuf_tensor(f"sslab{i}", [128, DM], BF16)) for i in range(2)]
            sXeT = phf.enter_context(nc.sbuf_tensor("sXeT", [128, 8, CAPL], BF16))
            shT = phf.enter_context(nc.sbuf_tensor("shT", [128, 32, CAPL], BF16))
            sw1c = [phf.enter_context(nc.sbuf_tensor(f"sw1c{i}", [128, 8, 128], BF16)) for i in range(2)]
            sw2c = [phf.enter_context(nc.sbuf_tensor(f"sw2c{i}", [128, 512], BF16)) for i in range(2)]
            svo = [phf.enter_context(nc.sbuf_tensor(f"svo{i}", [128, 512], F32)) for i in range(2)]

            RF = {}

            @block.vector
            def _(v):
                dAW.wait_all(v)
                dXT.wait_all(v)
                dQKr.wait_all(v)
                dOTr.wait_all(v)
                dSH.wait_all(v)
                dQKw.wait_all(v)
                dOTw.wait_all(v)
                dVW.wait_all(v)
                dVr.wait_all(v)
                dPW2.wait_all(v)
                v.wait_ge(TA.sem, TA.n)
                v.wait_ge(SA.sem, SA.n)
                v.wait_ge(VA.sem, VA.n)
                v.wait_ge(dZ.sem, dZ.n)
                dR.wait_all(v)
                dSc.wait_all(v)
                v.nop().then_inc(VF.sem, 1)
                VF.n += 1

            @block.tensor
            def _(tn):
                tn.wait_ge(VF.sem, VF.n)
                tn.nop().then_inc(TF.sem, 1)
                TF.n += 1

            @block.scalar
            def _(sc):
                sc.wait_ge(VF.sem, VF.n)
                sc.nop().then_inc(SF.sem, 1)
                SF.n += 1

            @block.gpsimd
            def _(g):
                g.wait_ge(VF.sem, VF.n)

            if NEL < NE:
                dZF = Sem(es, nc, "dZF")
                szffn = phf.enter_context(nc.sbuf_tensor("szffn", [128, DM], F32))

                @block.vector
                def _(v):
                    v.wait_ge(VF.sem, VF.n)
                    v.memset(szffn[:], 0.0).then_inc(VF.sem, 1)
                    VF.n += 1

                @block.sync
                def _(s):
                    s.wait_ge(VF.sem, VF.n)
                    for k8 in range(DISP_ROWS // 128):
                        s.dma_start(out=bass.AP(out2H, k8 * 128 * DM, [[DM, 128], [1, DM]]),
                                    in_=szffn[:]).then_inc(dZF.sem, 16)
                    dZF.n += (DISP_ROWS // 128) * 16

                @block.tensor
                def _(tn):
                    tn.wait_ge(dZF.sem, dZF.n)
                    tn.nop().then_inc(TF.sem, 1)
                    TF.n += 1

            for e in range(NEL):
                with contextlib.ExitStack() as pe1:
                    pXT = [pe1.enter_context(nc.psum_tensor(f"pXT{e}_{i}", [128, DM], BF16)) for i in range(2)]
                    phT = [pe1.enter_context(nc.psum_tensor(f"phT{e}_{i}", [128, CAPL], F32)) for i in range(2)]

                    for t in range(8):
                        ge = e * 8 + t

                        @block.sync
                        def _(s, e=e, t=t, ge=ge):
                            if ge >= 2:
                                s.wait_ge(TF.sem, RF[("xtmm", ge - 2)])
                            dSL.inc(s.dma_start(out=sslab[t % 2][:],
                                                in_=bass.AP(dispH, (e * CAPL + t * 128) * DM, [[DM, 128], [1, DM]])), ge)

                        @block.tensor
                        def _(tn, e=e, t=t, ge=ge):
                            dSL.wait_tile(tn, ge)
                            if ge == 0:
                                pass
                            if ge >= 2:
                                tn.wait_ge(SF.sem, RF[("xtev", ge - 2)])
                            if t == 0 and e >= 1:
                                tn.wait_ge(TF.sem, RF[("hmm", e - 1)])
                            for kc in range(8):
                                mm = tn.transpose(pXT[ge % 2][:, kc * 128:(kc + 1) * 128],
                                                  sslab[t % 2][:, kc * 128:(kc + 1) * 128],
                                                  s_identb[:])
                                if kc == 7:
                                    mm.then_inc(TF.sem, 1)
                                    TF.n += 1
                            RF[("xtmm", ge)] = TF.n

                        @block.scalar
                        def _(sc, e=e, t=t, ge=ge):
                            sc.wait_ge(TF.sem, RF[("xtmm", ge)])
                            sc.activation(bass.AP(sXeT, t * 128, [[8 * CAPL, 128], [CAPL, 8], [1, 128]]),
                                          pXT[ge % 2][:], AF.Copy).then_inc(SF.sem, 1)
                            SF.n += 1
                            RF[("xtev", ge)] = SF.n

                    for Fc in range(32):
                        gf = e * 32 + Fc

                        @block.gpsimd
                        def _(g, e=e, Fc=Fc, gf=gf):
                            if gf >= 2:
                                g.wait_ge(TF.sem, RF[("hfc", gf - 2)])
                            dW1.inc(g.dma_start(
                                out=sw1c[Fc % 2][:],
                                in_=bass.AP(w1, e * DM * DFF + Fc * 128,
                                            [[DFF, 128], [128 * DFF, 8], [1, 128]])), gf)

                        @block.tensor
                        def _(tn, e=e, Fc=Fc, gf=gf):
                            dW1.wait_tile(tn, gf)
                            if Fc == 0:
                                tn.wait_ge(SF.sem, RF[("xtev", e * 8 + 7)])
                            if Fc >= 2:
                                tn.wait_ge(SF.sem, RF[("hev", gf - 2)])
                            for kc in range(8):
                                for sm in range(2):
                                    mm = tn.matmul(phT[Fc % 2][:, sm * 512:(sm + 1) * 512],
                                                   sw1c[Fc % 2][:, kc, :],
                                                   sXeT[:, kc, sm * 512:(sm + 1) * 512],
                                                   start=(kc == 0), stop=(kc == 7))
                                    if kc == 7 and sm == 1:
                                        mm.then_inc(TF.sem, 1)
                                        TF.n += 1
                            RF[("hfc", gf)] = TF.n

                        @block.scalar
                        def _(sc, e=e, Fc=Fc, gf=gf):
                            sc.wait_ge(TF.sem, RF[("hfc", gf)])
                            sc.activation(shT[:, Fc, :], phT[Fc % 2][:], AF.Relu).then_inc(SF.sem, 1)
                            SF.n += 1
                            RF[("hev", gf)] = SF.n

                    RF[("hmm", e)] = RF[("hfc", e * 32 + 31)]

                with contextlib.ExitStack() as pe2:
                    pO = [pe2.enter_context(nc.psum_tensor(f"pO{e}_{i}", [128, 512], F32)) for i in range(8)]

                    for om in range(2):
                        for Fc in range(32):
                            gw = e * 64 + om * 32 + Fc

                            @block.gpsimd
                            def _(g, e=e, om=om, Fc=Fc, gw=gw):
                                if gw >= 2:
                                    g.wait_ge(TF.sem, RF[("o2fc", gw - 2)])
                                dW2.inc(g.dma_start(
                                    out=sw2c[Fc % 2][:],
                                    in_=bass.AP(w2, e * DFF * DM + Fc * 128 * DM + om * 512,
                                                [[DM, 128], [1, 512]])), gw)

                            @block.tensor
                            def _(tn, e=e, om=om, Fc=Fc, gw=gw):
                                dW2.wait_tile(tn, gw)
                                if Fc == 0:
                                    tn.wait_ge(SF.sem, RF[("hev", e * 32 + 31)])
                                    if om == 1:
                                        tn.wait_ge(VF.sem, RF[("o2ev", e, 0)])
                                    elif e >= 1:
                                        tn.wait_ge(VF.sem, RF[("o2ev", e - 1, 1)])
                                for st in range(8):
                                    mm = tn.matmul(pO[st][:], shT[:, Fc, st * 128:(st + 1) * 128],
                                                   sw2c[Fc % 2][:], start=(Fc == 0), stop=(Fc == 31))
                                    if st == 7:
                                        mm.then_inc(TF.sem, 1)
                                        TF.n += 1
                                RF[("o2fc", gw)] = TF.n

                        @block.vector
                        def _(v, e=e, om=om):
                            v.wait_ge(TF.sem, RF[("o2fc", e * 64 + om * 32 + 31)])
                            for st in range(8):
                                go = (e * 2 + om) * 8 + st
                                if go >= 2:
                                    dO2.wait_tile(v, go - 2)
                                v.tensor_copy(svo[st % 2][:], pO[st][:]).then_inc(VF.sem, 1)
                                VF.n += 1
                                RF[("o2st", e, om, st)] = VF.n
                            RF[("o2ev", e, om)] = VF.n

                        @block.sync
                        def _(s, e=e, om=om):
                            for st in range(8):
                                go = (e * 2 + om) * 8 + st
                                s.wait_ge(VF.sem, RF[("o2st", e, om, st)])
                                dO2.inc(s.dma_start(
                                    out=bass.AP(out2H, (e * CAPL + st * 128) * DM + om * 512, [[DM, 128], [1, 512]]),
                                    in_=svo[st % 2][:]), go)

        dC = DSem2(es, nc, "dC")
        dGb = DSem2(es, nc, "dGb")
        dSaro = Sem(es, nc, "dSaro")
        dY = DSem2(es, nc, "dY")
        V4 = Sem(es, nc, "V4")
        T4 = Sem(es, nc, "T4")
        G4 = Sem(es, nc, "G4")

        if max_phase >= 6:
         with contextlib.ExitStack() as ph4:
            sxc = [ph4.enter_context(nc.sbuf_tensor(f"sxc{i}", [128, DM], F32)) for i in range(2)]
            sac = [ph4.enter_context(nc.sbuf_tensor(f"sac{i}", [128, DM], F32)) for i in range(2)]
            sfc = [ph4.enter_context(nc.sbuf_tensor(f"sfc{i}", [128, DM], F32)) for i in range(2)]
            syc = [ph4.enter_context(nc.sbuf_tensor(f"syc{i}", [128, DM], F32)) for i in range(2)]
            saro = ph4.enter_context(nc.sbuf_tensor("saro", [8, NE], F32))
            sgeff = ph4.enter_context(nc.sbuf_tensor("sgeff", [128, NT], F32))
            soffb = ph4.enter_context(nc.sbuf_tensor("soffb", [128, NE], F32))
            siota16 = ph4.enter_context(nc.sbuf_tensor("siota16", [128, NE], mybir.dt.int32))
            siota16f = ph4.enter_context(nc.sbuf_tensor("siota16f", [128, NE], F32))
            soh4 = ph4.enter_context(nc.sbuf_tensor("soh4", [128, NE], F32))
            scol = ph4.enter_context(nc.sbuf_tensor("scol", [128, 1], F32))
            scol2 = ph4.enter_context(nc.sbuf_tensor("scol2", [128, 1], F32))
            slvc = ph4.enter_context(nc.sbuf_tensor("slvc", [128, NT], F32))
            p4 = ph4.enter_context(nc.psum_tensor("p4", [1, NE], F32))

            @block.gpsimd
            def _(g):
                dO2.wait_all(g)
                g.wait_ge(TF.sem, TF.n)
                g.wait_ge(VF.sem, VF.n)
                g.wait_ge(SF.sem, SF.n)
                dW1.wait_all(g)
                dW2.wait_all(g)
                dSL.wait_all(g)
                g.wait_ge(G2.sem, G2.n)
                g.dma_start(out=saro[:], in_=aroutH.ap()).then_inc(dSaro.sem, 16)
                dSaro.n += 16
                g.iota(siota16[:], [[1, NE]], base=0, channel_multiplier=0).then_inc(G4.sem, 1)
                G4.n += 1

            @block.tensor
            def _(tn):
                tn.wait_ge(dSaro.sem, 16)
                tn.matmul(p4[:], s_prefmask[:], saro[:], start=True, stop=True).then_inc(T4.sem, 1)
                T4.n += 1

            @block.vector
            def _(v):
                dO2.wait_all(v)
                v.wait_ge(TF.sem, TF.n)
                v.wait_ge(SF.sem, SF.n)
                v.wait_ge(VF.sem, VF.n)
                v.wait_ge(T4.sem, 1)
                v.tensor_copy(s_offs[0:1, :], p4[:]).then_inc(V4.sem, 1)
                v.wait_ge(G4.sem, 1)
                v.tensor_copy(siota16f[:], siota16[:]).then_inc(V4.sem, 1)
                v.tensor_single_scalar(slvc[:], s_locc[:], float(CAPL), OP.is_lt).then_inc(V4.sem, 1)
                V4.n += 3
                v.wait_ge(V4.sem, V4.n)

            p4b = ph4.enter_context(nc.psum_tensor("p4b", [128, NE], F32))

            @block.tensor
            def _(tn):
                tn.wait_ge(V4.sem, V4.n)
                tn.matmul(p4b[:], s_onesfr[:], s_offs[0:1, :], start=True, stop=True).then_inc(T4.sem, 1)
                T4.n += 1

            @block.vector
            def _(v):
                v.wait_ge(T4.sem, T4.n)
                v.tensor_copy(soffb[:], p4b[:]).then_inc(V4.sem, 1)
                V4.n += 1

            NV4 = 7

            @block.vector
            def _(v):
                v.wait_ge(G4.sem, G4.n)
                for t in range(NT):
                    base_ = V4.n + NV4 * t
                    v.tensor_scalar(soh4[:], siota16f[:], s_idxc[:, t:t + 1], None, OP.is_equal).then_inc(V4.sem, 1)
                    v.wait_ge(V4.sem, base_ + 1)
                    v.tensor_tensor(soh4[:], soh4[:], soffb[:], OP.mult).then_inc(V4.sem, 1)
                    v.wait_ge(V4.sem, base_ + 2)
                    v.reduce_sum(scol[:], soh4[:], axis=mybir.AxisListType.X).then_inc(V4.sem, 1)
                    v.wait_ge(V4.sem, base_ + 3)
                    v.tensor_scalar(scol[:], scol[:], s_locc[:, t:t + 1], None, OP.add).then_inc(V4.sem, 1)
                    v.wait_ge(V4.sem, base_ + 4)
                    v.tensor_single_scalar(scol[:], scol[:], float(CG), OP.is_lt).then_inc(V4.sem, 1)
                    v.tensor_scalar(scol2[:], slvc[:, t:t + 1], s_gatec[:, t:t + 1], None, OP.mult).then_inc(V4.sem, 1)
                    v.wait_ge(V4.sem, base_ + 6)
                    v.tensor_tensor(sgeff[:, t:t + 1], scol[:], scol2[:], OP.mult).then_inc(V4.sem, 1)
                V4.n += NV4 * NT

            v4base = V4.n

            @block.sync
            def _(s):
                dAW.wait_all(s)
                dO2.wait_all(s)
                for t in range(NT + 1):
                    if t < NT:
                        if t >= 2:
                            s.wait_ge(V4.sem, v4base + 2 * (t - 2) + 2)
                        dC.inc(s.dma_start(out=sxc[t % 2][:], in_=bass.AP(xq, t * 128 * DM, [[DM, 128], [1, DM]])), t)
                        dC.inc(s.dma_start(out=sac[t % 2][:], in_=bass.AP(attnH, t * 128 * DM, [[DM, 128], [1, DM]])), t)
                    if t >= 1:
                        u = t - 1
                        s.wait_ge(V4.sem, v4base + 2 * u + 2)
                        dY.inc(s.dma_start(out=bass.AP(y, u * 128 * DM, [[DM, 128], [1, DM]]),
                                           in_=syc[u % 2][:]), u)

            @block.gpsimd
            def _(g):
                g.wait_ge(V4.sem, v4base)
                for t in range(NT):
                    if t >= 2:
                        g.wait_ge(V4.sem, v4base + 2 * (t - 2) + 2)
                    dGb.inc(g.indirect_dma_start(
                        out=sfc[t % 2][:], out_offset=None,
                        in_=out2H.ap(), in_offset=bass.IndirectOffsetOnAxis(ap=s_gB[:, t:t + 1], axis=0),
                    ), t)

            @block.vector
            def _(v):
                for t in range(NT):
                    dC.wait_tile(v, t, per=2)
                    dGb.wait_tile(v, t)
                    if t >= 2:
                        dY.wait_tile(v, t - 2)
                    v.scalar_tensor_tensor(syc[t % 2][:], sfc[t % 2][:], sgeff[:, t:t + 1],
                                           sxc[t % 2][:], OP.mult, OP.add).then_inc(V4.sem, 1)
                    v.wait_ge(V4.sem, v4base + 2 * t + 1)
                    v.tensor_tensor(syc[t % 2][:], syc[t % 2][:], sac[t % 2][:], OP.add).then_inc(V4.sem, 1)
                V4.n += 2 * NT

            @block.sync
            def _(s):
                dY.wait_all(s)

        if debug:
            dDbg = Sem(es, nc, "dDbg")
            dDbg2 = Sem(es, nc, "dDbg2")

            @block.sync
            def _(s):
                if max_phase >= 2:
                    s.wait_ge(V2.sem, V2.n)
                    s.dma_start(out=dbg["dbg_idx"].ap(), in_=s_idxc[:]).then_inc(dDbg.sem, 16)
                    s.dma_start(out=dbg["dbg_gate"].ap(), in_=s_gatec[:]).then_inc(dDbg.sem, 16)
                    s.dma_start(out=dbg["dbg_loc"].ap(), in_=s_locc[:]).then_inc(dDbg.sem, 16)
                    dDbg.n += 48

            @block.gpsimd
            def _(g):
                if max_phase >= 1:
                    dWb.wait_all(g)
                    dWT.wait_all(g)
                    g.dma_start(out=dbg["dbg_xnb"].ap(), in_=xnbH.ap()).then_inc(dDbg2.sem, 16)
                    g.dma_start(out=dbg["dbg_xnT"].ap(),
                                in_=bass.AP(xnTH, 0, [[TOK, 128], [1, TOK]])).then_inc(dDbg2.sem, 16)
                    dDbg2.n += 32
                if max_phase >= 3:
                    dSc.wait_all(g)
                    g.dma_start(out=dbg["dbg_disp"].ap(),
                                in_=bass.AP(dispH, 0, [[DM, 512], [1, DM]])).then_inc(dDbg2.sem, 16)
                    dDbg2.n += 16
                if max_phase >= 4:
                    dAW.wait_all(g)
                    g.dma_start(out=dbg["dbg_attn"].ap(),
                                in_=bass.AP(attnH, 0, [[DM, T], [1, DM]])).then_inc(dDbg2.sem, 16)
                    dDbg2.n += 16

            @block.sync
            def _(s):
                s.wait_ge(dDbg.sem, dDbg.n)
                s.wait_ge(dDbg2.sem, dDbg2.n)

        if max_phase < 6:
            dYF = Sem(es, nc, "dYF")
            with contextlib.ExitStack() as phz:
                szy = phz.enter_context(nc.sbuf_tensor("szy", [128, 8192], F32))

                @block.vector
                def _(v):
                    v.memset(szy[:], 0.0).then_inc(VZ.sem, 1)
                    VZ.n += 1

                @block.sync
                def _(s):
                    s.wait_ge(VZ.sem, VZ.n)
                    if max_phase >= 3:
                        dR.wait_all(s)
                        dSc.wait_all(s)
                        s.wait_ge(dZ.sem, dZ.n)
                    if max_phase >= 4:
                        dAW.wait_all(s)
                    if max_phase >= 5:
                        dO2.wait_all(s)
                        s.wait_ge(TF.sem, TF.n)
                    dWb.wait_all(s)
                    dWT.wait_all(s)
                    for k in range(8):
                        s.dma_start(out=bass.AP(y, k * 128 * 8192, [[8192, 128], [1, 8192]]),
                                    in_=szy[:]).then_inc(dYF.sem, 16)
                    s.wait_ge(dYF.sem, 8 * 16)

    return nc


def _host_consts():
    identb = np.eye(128, dtype=ml_dtypes.bfloat16)
    identf = np.eye(128, dtype=np.float32)
    triu = np.triu(np.ones((64, 64), np.float32), 1)
    sel2 = np.zeros((128, 2), ml_dtypes.bfloat16)
    sel2[:64, 0] = 1
    sel2[64:, 1] = 1
    return identb, identf, triu, sel2


def kernel(L0_attn_q, L0_attn_in_proj_weight, L0_attn_in_proj_bias, L0_attn_out_w,
           L0_attn_out_b, L0_attn_rel_pe, L0_ffn_Wg, L0_ffn_w1, L0_ffn_w2,
           _debug=False):
    x = np.ascontiguousarray(np.asarray(L0_attn_q, dtype=np.float32))
    wqkvT = np.ascontiguousarray(np.asarray(L0_attn_in_proj_weight, np.float32).T)
    bqkv = np.ascontiguousarray(np.asarray(L0_attn_in_proj_bias, np.float32))
    outwT = np.ascontiguousarray(np.asarray(L0_attn_out_w, np.float32).T)
    outb = np.ascontiguousarray(np.asarray(L0_attn_out_b, np.float32))
    peT = np.ascontiguousarray(np.asarray(L0_attn_rel_pe, np.float32).T)
    wg = np.ascontiguousarray(np.asarray(L0_ffn_Wg, np.float32))
    w1 = np.ascontiguousarray(np.asarray(L0_ffn_w1, np.float32))
    w2 = np.ascontiguousarray(np.asarray(L0_ffn_w2, np.float32))
    identb, identf, triu, sel2 = _host_consts()

    nc = build_nc(debug=_debug)
    in_maps = []
    for c in range(NCORES):
        xs = np.ascontiguousarray(x[c * BPC:(c + 1) * BPC].reshape(TOK, DM))
        corehot = np.zeros((1, 8), np.float32); corehot[0, c] = 1.0
        prefmask = np.zeros((8, 1), np.float32); prefmask[:c, 0] = 1.0
        in_maps.append({
            "xq": xs, "wqkvT": wqkvT, "bqkv": bqkv, "outwT": outwT, "outb": outb,
            "peT": peT, "wg": wg, "w1": w1, "w2": w2,
            "identb": identb, "identf": identf, "triu": triu,
            "corehot": corehot, "prefmask": prefmask, "sel2": sel2,
        })
    res = run_bass_kernel_spmd(nc, in_maps, core_ids=list(range(NCORES)))
    out = np.concatenate([r["y"].reshape(BPC, T, DM) for r in res.results], axis=0)
    if _debug:
        return out, res
    return out


# version 04 (final)
# speedup vs baseline: 1.0263x; 1.0195x over previous
import contextlib
import numpy as np
import ml_dtypes
import concourse.bass as bass
import concourse.mybir as mybir
from concourse.bass_utils import run_bass_kernel_spmd

F32 = mybir.dt.float32
BF16 = mybir.dt.bfloat16
U32 = mybir.dt.uint32
OP = mybir.AluOpType
AF = mybir.ActivationFunctionType

NCORES = 8
BPC = 4
T = 2048
TOK = BPC * T
DM = 1024
NT = TOK // 128
H = 16
DH = 64
NHP = H // 2
NQT = T // 128
NE = 16
CAPL = 768
CG = 4096
DFF = 4096
PADL, PADM, PADR = 127, 129, 191
PADW = PADL + PADM + PADR
DISP_ROWS = NE * CAPL
DISP_ROWS_PAD = DISP_ROWS + 128
EPS = 1e-5


class DSem2:
    def __init__(self, es, nc, name):
        self.sems = [es.enter_context(nc.semaphore(f"{name}{i}")) for i in range(2)]
        self.cnt = [0, 0]

    def inc(self, inst, t):
        inst.then_inc(self.sems[t % 2], 16)
        self.cnt[t % 2] += 16

    def wait_tile(self, eng, t, per=1):
        eng.wait_ge(self.sems[t % 2], 16 * per * (t // 2 + 1))

    def wait_all(self, eng):
        eng.wait_ge(self.sems[0], self.cnt[0])
        eng.wait_ge(self.sems[1], self.cnt[1])


class Sem:
    def __init__(self, es, nc, name):
        self.sem = es.enter_context(nc.semaphore(name))
        self.n = 0

    def inc(self, inst, k=1):
        inst.then_inc(self.sem, k)
        self.n += k
        return self.n

    def dinc(self, inst):
        inst.then_inc(self.sem, 16)
        self.n += 16
        return self.n

    def wait(self, eng, n=None):
        eng.wait_ge(self.sem, self.n if n is None else n)


def build_nc(debug=False, max_phase=9, nt_lim=None, ln_parts=15, ln_sk=0,
             bpc_lim=None, nhp_lim=None, ne_lim=None):
    NTL = NT if nt_lim is None else nt_lim
    BPCL = BPC if bpc_lim is None else bpc_lim
    NHPL = NHP if nhp_lim is None else nhp_lim
    NEL = NE if ne_lim is None else ne_lim
    nc = bass.Bass("TRN2", target_bir_lowering=False, debug=False)

    xq = nc.dram_tensor("xq", [TOK, DM], F32, kind="ExternalInput")
    wqkvT = nc.dram_tensor("wqkvT", [DM, 3 * DM], F32, kind="ExternalInput")
    bqkv = nc.dram_tensor("bqkv", [3 * DM], F32, kind="ExternalInput")
    outwT = nc.dram_tensor("outwT", [DM, DM], F32, kind="ExternalInput")
    outb = nc.dram_tensor("outb", [DM], F32, kind="ExternalInput")
    peT = nc.dram_tensor("peT", [DH, 129], F32, kind="ExternalInput")
    wg = nc.dram_tensor("wg", [DM, NE], F32, kind="ExternalInput")
    w1 = nc.dram_tensor("w1", [NE, DM, DFF], F32, kind="ExternalInput")
    w2 = nc.dram_tensor("w2", [NE, DFF, DM], F32, kind="ExternalInput")
    identb = nc.dram_tensor("identb", [128, 128], BF16, kind="ExternalInput")
    identf = nc.dram_tensor("identf", [128, 128], F32, kind="ExternalInput")
    triu = nc.dram_tensor("triu", [64, 64], F32, kind="ExternalInput")
    corehot = nc.dram_tensor("corehot", [1, 8], F32, kind="ExternalInput")
    prefmask = nc.dram_tensor("prefmask", [8, 1], F32, kind="ExternalInput")
    sel2 = nc.dram_tensor("sel2", [128, 2], BF16, kind="ExternalInput")

    y = nc.dram_tensor("y", [TOK, DM], F32, kind="ExternalOutput")

    xnbH = nc.dram_tensor("xnbH", [TOK, DM], BF16, kind="Internal")
    xnTH = nc.dram_tensor("xnTH", [8, 128, TOK], BF16, kind="Internal")
    qkH = nc.dram_tensor("qkH", [16, 128, T], BF16, kind="Internal")
    padH = nc.dram_tensor("padH", [2, 2, NQT, 128, PADW + 1], BF16, kind="Internal")
    dispH = nc.dram_tensor("dispH", [DISP_ROWS_PAD, DM], BF16, kind="Internal")
    out2H = nc.dram_tensor("out2H", [DISP_ROWS, DM], F32, kind="Internal")
    attnH = nc.dram_tensor("attnH", [TOK, DM], F32, kind="Internal")
    OTH = nc.dram_tensor("OTH", [8, 128, T], BF16, kind="Internal")
    vH = nc.dram_tensor("vH", [16, 128, DM], BF16, kind="Internal")
    arinH = nc.dram_tensor("arinH", [8, NE], F32, kind="Internal")
    aroutH = nc.dram_tensor("aroutH", [8, NE], F32, kind="Internal")

    dbg = {}
    if debug:
        for name, shape, dt in [
            ("dbg_xnb", [TOK, DM], BF16),
            ("dbg_idx", [128, NT], F32),
            ("dbg_gate", [128, NT], F32),
            ("dbg_loc", [128, NT], F32),
            ("dbg_disp", [512, DM], BF16),
            ("dbg_xnT", [128, TOK], BF16),
            ("dbg_attn", [T, DM], F32),
        ]:
            dbg[name] = nc.dram_tensor(name, shape, dt, kind="ExternalOutput")

    es = contextlib.ExitStack()
    with es:
        s_identb = es.enter_context(nc.sbuf_tensor("s_identb", [128, 128], BF16))
        s_identf = es.enter_context(nc.sbuf_tensor("s_identf", [128, 128], F32))
        s_sel2 = es.enter_context(nc.sbuf_tensor("s_sel2", [128, 2], BF16))
        s_triu = es.enter_context(nc.sbuf_tensor("s_triu", [64, 64], F32))
        s_corehot = es.enter_context(nc.sbuf_tensor("s_corehot", [1, 8], F32))
        s_prefmask = es.enter_context(nc.sbuf_tensor("s_prefmask", [8, 1], F32))
        s_wg = es.enter_context(nc.sbuf_tensor("s_wg", [128, 8, NE], F32))
        s_onesb = es.enter_context(nc.sbuf_tensor("s_onesb", [128, 512], BF16))
        s_onesf = es.enter_context(nc.sbuf_tensor("s_onesf", [128, 8], F32))
        s_onesfr = es.enter_context(nc.sbuf_tensor("s_onesfr", [1, 128], F32))
        s_biasrow = es.enter_context(nc.sbuf_tensor("s_biasrow", [1, 3 * DM], BF16))
        s_outbrow = es.enter_context(nc.sbuf_tensor("s_outbrow", [1, DM], BF16))
        s_peT2 = es.enter_context(nc.sbuf_tensor("s_peT2", [128, 129], BF16))
        swqkv = es.enter_context(nc.sbuf_tensor("swqkv", [128, 8, 3 * DM], BF16))
        soutw = es.enter_context(nc.sbuf_tensor("soutw", [128, 8, DM], BF16))
        s_eps = es.enter_context(nc.sbuf_tensor("s_eps", [128, 1], F32))
        s_idxc = es.enter_context(nc.sbuf_tensor("s_idxc", [128, NT], F32))
        s_gatec = es.enter_context(nc.sbuf_tensor("s_gatec", [128, NT], F32))
        s_locc = es.enter_context(nc.sbuf_tensor("s_locc", [128, NT], F32))
        s_dc = es.enter_context(nc.sbuf_tensor("s_dc", [128, NT], U32))
        s_gB = es.enter_context(nc.sbuf_tensor("s_gB", [128, NT], U32))
        s_offs = es.enter_context(nc.sbuf_tensor("s_offs", [1, NE], F32))

        block = es.enter_context(nc.Block())

        dG0 = Sem(es, nc, "dG0")
        dS0 = Sem(es, nc, "dS0")
        T0 = Sem(es, nc, "T0")
        V0 = Sem(es, nc, "V0")
        S0 = Sem(es, nc, "S0")
        G0 = Sem(es, nc, "G0")

        @block.gpsimd
        def _(g):
            g.dma_start(out=s_biasrow[:], in_=bass.AP(bqkv, 0, [[3 * DM, 1], [1, 3 * DM]])).then_inc(dG0.sem, 16)
            g.dma_start(out=s_outbrow[:], in_=bass.AP(outb, 0, [[DM, 1], [1, DM]])).then_inc(dG0.sem, 16)
            g.dma_start(out=s_peT2[0:64, :], in_=peT.ap()).then_inc(dG0.sem, 16)
            g.dma_start(out=s_peT2[64:128, :], in_=peT.ap()).then_inc(dG0.sem, 16)
            g.dma_start(out=swqkv[:], in_=bass.AP(wqkvT, 0, [[3 * DM, 128], [128 * 3 * DM, 8], [1, 3 * DM]])).then_inc(dG0.sem, 16)
            g.dma_start(out=soutw[:], in_=bass.AP(outwT, 0, [[DM, 128], [128 * DM, 8], [1, DM]])).then_inc(dG0.sem, 16)
            dG0.n += 96

        @block.sync
        def _(s):
            s.dma_start(out=s_identb[:], in_=identb.ap()).then_inc(dS0.sem, 16)
            s.dma_start(out=s_identf[:], in_=identf.ap()).then_inc(dS0.sem, 16)
            s.dma_start(out=s_sel2[:], in_=sel2.ap()).then_inc(dS0.sem, 16)
            s.dma_start(out=s_triu[:], in_=triu.ap()).then_inc(dS0.sem, 16)
            s.dma_start(out=s_corehot[:], in_=corehot.ap()).then_inc(dS0.sem, 16)
            s.dma_start(out=s_prefmask[:], in_=prefmask.ap()).then_inc(dS0.sem, 16)
            s.dma_start(out=s_wg[:], in_=bass.AP(wg, 0, [[NE, 128], [128 * NE, 8], [1, NE]])).then_inc(dS0.sem, 16)
            dS0.n += 7 * 16


        @block.vector
        def _(v):
            v.memset(s_onesb[:], 1.0).then_inc(V0.sem, 1)
            v.memset(s_onesf[:], 1.0).then_inc(V0.sem, 1)
            v.memset(s_eps[:], EPS).then_inc(V0.sem, 1)
            v.memset(s_onesfr[:], 1.0).then_inc(V0.sem, 1)
            V0.n += 4

        @block.scalar
        def _(sc):
            pass




        dX = DSem2(es, nc, "dX")
        dWb = DSem2(es, nc, "dWb")
        dWT = DSem2(es, nc, "dWT")
        V1 = Sem(es, nc, "V1")
        S1 = Sem(es, nc, "S1")
        T1 = Sem(es, nc, "T1")

        if max_phase >= 1:
         with contextlib.ExitStack() as ph1:
            sx = [ph1.enter_context(nc.sbuf_tensor(f"sx{i}", [128, DM], F32)) for i in range(2)]
            sxn = ph1.enter_context(nc.sbuf_tensor("sxn", [128, DM], F32))
            sbn = ph1.enter_context(nc.sbuf_tensor("sbn", [128, 12], F32))
            smv = ph1.enter_context(nc.sbuf_tensor("smv", [128, 2], F32))
            ssd = ph1.enter_context(nc.sbuf_tensor("ssd", [128, 1], F32))
            srstd = ph1.enter_context(nc.sbuf_tensor("srstd", [128, 1], F32))
            sxnTf = ph1.enter_context(nc.sbuf_tensor("sxnTf", [128, DM], F32))
            sxnTb = [ph1.enter_context(nc.sbuf_tensor(f"sxnTb{i}", [128, DM], BF16)) for i in range(2)]
            sxnb = [ph1.enter_context(nc.sbuf_tensor(f"sxnb{i}", [128, DM], BF16)) for i in range(2)]
            slg = ph1.enter_context(nc.sbuf_tensor("slg", [128, NE], F32))
            sge = ph1.enter_context(nc.sbuf_tensor("sge", [128, NE], F32))
            sgex = ph1.enter_context(nc.sbuf_tensor("sgex", [128, NE], F32))
            sgsum = ph1.enter_context(nc.sbuf_tensor("sgsum", [128, 1], F32))
            smax8 = ph1.enter_context(nc.sbuf_tensor("smax8", [128, 8], F32))
            sidx8 = ph1.enter_context(nc.sbuf_tensor("sidx8", [128, 8], U32))
            pT1 = ph1.enter_context(nc.psum_tensor("pT1", [128, DM], F32))
            pG1 = ph1.enter_context(nc.psum_tensor("pG1", [128, NE], F32))

            NV, NS, NTN = 12, 4, 16
            v1b, s1b, t1b = V1.n, S1.n, T1.n

            @block.sync
            def _(s):
                for t in range(NTL + 1):
                    if t < NT:
                        if t >= 2:
                            s.wait_ge(V1.sem, v1b + NV * (t - 2) + 5)
                        dX.inc(s.dma_start(out=sx[t % 2][:], in_=bass.AP(xq, t * 128 * DM, [[DM, 128], [1, DM]])), t)
                    if t >= 1 and (ln_parts & 4):
                        u = t - 1
                        s.wait_ge(S1.sem, s1b + NS * u + 2)
                        dWb.inc(s.dma_start(out=bass.AP(xnbH, u * 128 * DM, [[DM, 128], [1, DM]]),
                                            in_=sxnb[u % 2][:]), u)
                        s.wait_ge(S1.sem, s1b + NS * u + 3)
                        dWT.inc(s.dma_start(out=bass.AP(xnTH, u * 128, [[TOK, 128], [128 * TOK, 8], [1, 128]]),
                                            in_=sxnTb[u % 2][:]), u)

            @block.vector
            def _(v):
                def vemit(thunk):
                    if ln_sk & 4:
                        v.nop().then_inc(V1.sem, 1)
                    else:
                        thunk().then_inc(V1.sem, 1)
                for t in range(NTL):
                    dX.wait_tile(v, t)
                    if t >= 1:
                        v.wait_ge(S1.sem, s1b + NS * (t - 1) + 1)
                    x_ = sx[t % 2]
                    vemit(lambda: v.bn_stats(sbn[:, 0:6], x_[:, 0:512]))
                    vemit(lambda: v.bn_stats(sbn[:, 6:12], x_[:, 512:1024]))
                    v.wait_ge(V1.sem, v1b + NV * t + 2)
                    vemit(lambda: v.bn_aggr(smv[:], sbn[:]))
                    v.wait_ge(S1.sem, s1b + NS * t + 1)
                    vemit(lambda: v.reciprocal(srstd[:], ssd[:]))
                    v.wait_ge(V1.sem, v1b + NV * t + 4)
                    vemit(lambda: v.tensor_scalar(sxn[:], x_[:], smv[:, 0:1], srstd[:], OP.subtract, OP.mult))
                    if ln_parts & 2:
                        v.wait_ge(T1.sem, t1b + NTN * t + 8)
                    vemit(lambda: v.tensor_copy(sxnTf[:], pT1[:]))
                    if ln_parts & 8:
                        v.wait_ge(T1.sem, t1b + NTN * t + 16)
                    vemit(lambda: v.tensor_copy(slg[:], pG1[:]))
                    v.wait_ge(V1.sem, v1b + NV * t + 7)
                    vemit(lambda: v.max(smax8[:], slg[:]))
                    v.wait_ge(V1.sem, v1b + NV * t + 8)
                    vemit(lambda: v.max_index(sidx8[:], smax8[:], slg[:]))
                    v.wait_ge(V1.sem, v1b + NV * t + 9)
                    vemit(lambda: v.tensor_copy(s_idxc[:, t:t + 1], sidx8[:, 0:1]))
                    vemit(lambda: v.tensor_scalar(sge[:], slg[:], smax8[:, 0:1], None, OP.subtract))
                    v.wait_ge(S1.sem, s1b + NS * t + 4)
                    if ln_sk & 1:
                        vemit(lambda: v.reciprocal(s_gatec[:, t:t + 1], smv[:, 1:2]))
                    else:
                        vemit(lambda: v.reciprocal(s_gatec[:, t:t + 1], sgsum[:]))

            @block.scalar
            def _(sc):
                def semit(thunk):
                    if ln_sk & 2:
                        sc.nop().then_inc(S1.sem, 1)
                    else:
                        thunk().then_inc(S1.sem, 1)
                sc.wait_ge(V0.sem, 3)
                for t in range(NTL):
                    sc.wait_ge(V1.sem, v1b + NV * t + 3)
                    semit(lambda: sc.activation(ssd[:], smv[:, 1:2], AF.Sqrt, bias=s_eps[:]))
                    sc.wait_ge(V1.sem, v1b + NV * t + 5)
                    if t >= 2:
                        dWb.wait_tile(sc, t - 2)
                    semit(lambda: sc.activation(sxnb[t % 2][:], sxn[:], AF.Copy))
                    sc.wait_ge(T1.sem, t1b + NTN * t + 8)
                    sc.wait_ge(V1.sem, v1b + NV * t + 6)
                    if t >= 2:
                        dWT.wait_tile(sc, t - 2)
                    semit(lambda: sc.activation(sxnTb[t % 2][:], pT1[:], AF.Copy))
                    sc.wait_ge(V1.sem, v1b + NV * t + 11)
                    if ln_sk & 1:
                        semit(lambda: sc.activation(sgex[:], sge[:], AF.Exp))
                    else:
                        semit(lambda: sc.activation(sgex[:], sge[:], AF.Exp, accum_out=sgsum[:]))

            @block.tensor
            def _(tn):
                tn.wait_ge(dS0.sem, 7 * 16)
                for t in range(NTL):
                    tn.wait_ge(V1.sem, v1b + NV * t + 5)
                    if t >= 1:
                        tn.wait_ge(S1.sem, s1b + NS * (t - 1) + 3)
                    for c in range(8):
                        tn.transpose(pT1[:, c * 128:(c + 1) * 128], sxn[:, c * 128:(c + 1) * 128],
                                     s_identf[:]).then_inc(T1.sem, 1)
                    tn.wait_ge(V1.sem, v1b + NV * t + 6)
                    for fc in range(8):
                        tn.matmul(pG1[:], sxnTf[:, fc * 128:(fc + 1) * 128], s_wg[:, fc, :],
                                  start=(fc == 0), stop=(fc == 7)).then_inc(T1.sem, 1)
                T1.n += NT * NTN

            V1.n = v1b + NV * NTL
            S1.n = s1b + NS * NTL

        PH = [None]
        V2 = Sem(es, nc, "V2")
        S2 = Sem(es, nc, "S2")
        T2 = Sem(es, nc, "T2")
        G2 = Sem(es, nc, "G2")
        dA = Sem(es, nc, "dA")

        if max_phase >= 2:
         with contextlib.ExitStack() as ph2:
            sidxT = ph2.enter_context(nc.sbuf_tensor("sidxT", [64, 128], F32))
            soh = ph2.enter_context(nc.sbuf_tensor("soh", [64, 128], F32))
            scum = ph2.enter_context(nc.sbuf_tensor("scum", [64, 128], F32))
            stmp = ph2.enter_context(nc.sbuf_tensor("stmp", [64, 128], F32))
            stmp2 = ph2.enter_context(nc.sbuf_tensor("stmp2", [64, 128], F32))
            slocT = ph2.enter_context(nc.sbuf_tensor("slocT", [64, 128], F32))
            sones64 = ph2.enter_context(nc.sbuf_tensor("sones64", [64, 128], F32))
            spart = ph2.enter_context(nc.sbuf_tensor("spart", [64, NE], F32))
            stileoff = ph2.enter_context(nc.sbuf_tensor("stileoff", [64, NE], F32))
            scnt = ph2.enter_context(nc.sbuf_tensor("scnt", [1, NE], F32))
            sarin = ph2.enter_context(nc.sbuf_tensor("sarin", [8, NE], F32))
            sdT = ph2.enter_context(nc.sbuf_tensor("sdT", [64, 128], F32))
            strashi = ph2.enter_context(nc.sbuf_tensor("strashi", [64, 128], mybir.dt.int32))
            strash = ph2.enter_context(nc.sbuf_tensor("strash", [64, 128], F32))
            slv64 = ph2.enter_context(nc.sbuf_tensor("slv64", [64, 128], F32))
            sdTc = ph2.enter_context(nc.sbuf_tensor("sdTc", [128, 64], F32))
            p2a = ph2.enter_context(nc.psum_tensor("p2a", [128, 128], F32))
            p2b = ph2.enter_context(nc.psum_tensor("p2b", [64, NE], F32))
            p2c = ph2.enter_context(nc.psum_tensor("p2c", [8, NE], F32))

            @block.tensor
            def _(tn):
                tn.wait_ge(V1.sem, V1.n)
                tn.transpose(p2a[0:64, :], s_idxc[:], s_identf[:]).then_inc(T2.sem, 1)
                T2.n += 1

            @block.gpsimd
            def _(g):
                g.wait_ge(V1.sem, V1.n)
                g.iota(strashi[:], [[1, 128]], base=DISP_ROWS, channel_multiplier=0).then_inc(G2.sem, 1)
                G2.n += 1

            @block.vector
            def _(v):
                v.wait_ge(S1.sem, S1.n)
                dWb.wait_all(v)
                dWT.wait_all(v)
                v.memset(sones64[:], 1.0).then_inc(V2.sem, 1)
                v.memset(slocT[:], 0.0).then_inc(V2.sem, 1)
                v.wait_ge(T2.sem, 1)
                v.tensor_copy(sidxT[:], p2a[0:64, :]).then_inc(V2.sem, 1)
                V2.n += 3

            def _vop(v, thunk):
                v.wait_ge(V2.sem, V2.n)
                V2.n += 1
                thunk().then_inc(V2.sem, 1)

            @block.vector
            def _(v):
                for e in range(NE):
                    _vop(v, lambda: v.tensor_single_scalar(soh[:], sidxT[:], float(e), OP.is_equal))
                    _vop(v, lambda: v.tensor_tensor_scan(scum[:], soh[:], sones64[:], 0.0, OP.add, OP.mult))
                    _vop(v, lambda: v.tensor_copy(spart[:, e:e + 1], scum[:, 127:128]))

            @block.tensor
            def _(tn):
                tn.wait_ge(V2.sem, V2.n)
                tn.matmul(p2b[:], s_triu[:], spart[:], start=True, stop=True).then_inc(T2.sem, 1)
                tn.matmul(p2c[0:1, :], s_onesf[0:64, 0:1], spart[:], start=True, stop=True).then_inc(T2.sem, 1)
                T2.n += 2

            @block.vector
            def _(v):
                v.wait_ge(T2.sem, T2.n)
                _vop(v, lambda: v.tensor_copy(stileoff[:], p2b[:]))
                _vop(v, lambda: v.tensor_copy(scnt[:], p2c[0:1, :]))
                for e in range(NE):
                    _vop(v, lambda: v.tensor_single_scalar(soh[:], sidxT[:], float(e), OP.is_equal))
                    _vop(v, lambda: v.tensor_tensor_scan(scum[:], soh[:], sones64[:], 0.0, OP.add, OP.mult))
                    _vop(v, lambda: v.tensor_scalar(stmp[:], scum[:], stileoff[:, e:e + 1], 1.0, OP.add, OP.subtract))
                    _vop(v, lambda: v.tensor_tensor(stmp2[:], stmp[:], soh[:], OP.mult))
                    _vop(v, lambda: v.tensor_tensor(slocT[:], slocT[:], stmp2[:], OP.add))
                v.wait_ge(G2.sem, 1)
                _vop(v, lambda: v.tensor_copy(strash[:], strashi[:]))
                _vop(v, lambda: v.tensor_single_scalar(stmp[:], slocT[:], float(CAPL - 1), OP.min))
                _vop(v, lambda: v.tensor_single_scalar(stmp2[:], sidxT[:], float(CAPL), OP.mult))
                _vop(v, lambda: v.tensor_tensor(sdT[:], stmp[:], stmp2[:], OP.add))
                _vop(v, lambda: v.tensor_single_scalar(slv64[:], slocT[:], float(CAPL), OP.is_lt))
                _vop(v, lambda: v.tensor_tensor(sdT[:], sdT[:], slv64[:], OP.mult))
                _vop(v, lambda: v.tensor_single_scalar(stmp[:], slv64[:], 1.0, OP.subtract))
                _vop(v, lambda: v.tensor_single_scalar(stmp[:], stmp[:], -1.0, OP.mult))
                _vop(v, lambda: v.tensor_tensor(stmp[:], stmp[:], strash[:], OP.mult))
                _vop(v, lambda: v.tensor_tensor(sdT[:], sdT[:], stmp[:], OP.add))

            sgBf = ph2.enter_context(nc.sbuf_tensor("sgBf", [128, NT], F32))

            @block.tensor
            def _(tn):
                tn.wait_ge(V2.sem, V2.n)
                tn.matmul(p2c[:], s_corehot[:], scnt[:], start=True, stop=True).then_inc(T2.sem, 1)
                tn.transpose(p2a[:, 0:64], sdT[:], s_identf[0:64, 0:64]).then_inc(T2.sem, 1)
                T2.n += 2

            @block.vector
            def _(v):
                v.wait_ge(T2.sem, T2.n)
                _vop(v, lambda: v.tensor_copy(sarin[:], p2c[:]))
                _vop(v, lambda: v.tensor_copy(sdTc[:], p2a[:, 0:64]))
                _vop(v, lambda: v.tensor_copy(s_dc[:], sdTc[:]))

            @block.tensor
            def _(tn):
                tn.wait_ge(V2.sem, V2.n)
                tn.transpose(p2a[:, 0:64], slocT[:], s_identf[0:64, 0:64]).then_inc(T2.sem, 1)
                T2.n += 1

            @block.vector
            def _(v):
                v.wait_ge(T2.sem, T2.n)
                _vop(v, lambda: v.tensor_copy(s_locc[:], p2a[:, 0:64]))
                _vop(v, lambda: v.tensor_single_scalar(sdTc[:], s_locc[:], float(CAPL - 1), OP.min))
                _vop(v, lambda: v.tensor_single_scalar(sgBf[:], s_idxc[:], float(CAPL), OP.mult))
                _vop(v, lambda: v.tensor_tensor(sgBf[:], sgBf[:], sdTc[:], OP.add))
                _vop(v, lambda: v.tensor_copy(s_gB[:], sgBf[:]))

            @block.gpsimd
            def _(g):
                g.wait_ge(V2.sem, V2.n)
                g.dma_start(out=arinH.ap(), in_=sarin[:]).then_inc(dA.sem, 16)
                dA.n += 16
                g.wait_ge(dA.sem, dA.n)
                g.collective_compute(
                    "AllReduce", OP.add,
                    replica_groups=[list(range(NCORES))],
                    ins=[arinH.ap()], outs=[aroutH.ap()],
                ).then_inc(G2.sem, 1)
                G2.n += 1

        dZ = Sem(es, nc, "dZ")
        dR = DSem2(es, nc, "dR")
        dSc = DSem2(es, nc, "dSc")
        VZ = Sem(es, nc, "VZ")

        if max_phase >= 3:
         with contextlib.ExitStack() as ph3:
            szero = ph3.enter_context(nc.sbuf_tensor("szero", [128, 8192], BF16))
            sxr = [ph3.enter_context(nc.sbuf_tensor(f"sxr{i}", [128, DM], BF16)) for i in range(2)]

            @block.vector
            def _(v):
                v.wait_ge(dA.sem, dA.n)
                v.memset(szero[:], 0.0).then_inc(VZ.sem, 1)
                VZ.n += 1

            @block.sync
            def _(s):
                s.wait_ge(VZ.sem, 1)
                nzc = DISP_ROWS // 1024
                for k in range(nzc):
                    s.dma_start(out=bass.AP(dispH, k * 128 * 8192, [[8192, 128], [1, 8192]]),
                                in_=szero[:]).then_inc(dZ.sem, 16)
                s.dma_start(out=bass.AP(dispH, nzc * 128 * 8192, [[DM, 128], [1, DM]]),
                            in_=szero[:, 0:DM]).then_inc(dZ.sem, 16)
                dZ.n += (nzc + 1) * 16
                dWb.wait_all(s)
                for t in range(NT):
                    if t >= 2:
                        dSc.wait_tile(s, t - 2)
                    dR.inc(s.dma_start(out=sxr[t % 2][:], in_=bass.AP(xnbH, t * 128 * DM, [[DM, 128], [1, DM]])), t)

            @block.gpsimd
            def _(g):
                g.wait_ge(V2.sem, V2.n)
                g.wait_ge(dZ.sem, (DISP_ROWS // 1024 + 1) * 16)
                for t in range(NT):
                    dR.wait_tile(g, t)
                    dSc.inc(g.indirect_dma_start(
                        out=dispH.ap(), out_offset=bass.IndirectOffsetOnAxis(ap=s_dc[:, t:t + 1], axis=0),
                        in_=sxr[t % 2][:], in_offset=None,
                    ), t)

        dAt = Sem(es, nc, "dAt")
        TA = Sem(es, nc, "TA")
        VA = Sem(es, nc, "VA")
        SA = Sem(es, nc, "SA")
        dXT = DSem2(es, nc, "dXT")
        dQKw = DSem2(es, nc, "dQKw")
        dQKr = DSem2(es, nc, "dQKr")
        dPW2 = DSem2(es, nc, "dPW2")
        dSH = DSem2(es, nc, "dSH")
        dOTw = DSem2(es, nc, "dOTw")
        dOTr = DSem2(es, nc, "dOTr")
        dAW = DSem2(es, nc, "dAW")
        dVW = DSem2(es, nc, "dVW")
        dVr = DSem2(es, nc, "dVr")

        def _segs(qt, half):
            lo = max(0, 128 * qt - 64)
            hi = min(T, 128 * qt + 192)
            h0, h1 = 1024 * half, 1024 * half + 1024
            out = []
            for (a, b2, kind) in [(0, lo, "L"), (lo, hi, "B"), (hi, T, "R")]:
                aa, bb = max(a, h0), min(b2, h1)
                if aa < bb:
                    out.append((aa, bb, kind))
            return out

        if max_phase >= 4 and True:
         with contextlib.ExitStack() as pha:
            sxnT = pha.enter_context(nc.sbuf_tensor("sxnT", [128, 8, T], BF16))
            svst = [pha.enter_context(nc.sbuf_tensor(f"svst{i}", [128, DM], BF16)) for i in range(2)]
            sVr = [pha.enter_context(nc.sbuf_tensor(f"sVr{i}", [128, 16, 128], BF16)) for i in range(2)]
            sqkst = [pha.enter_context(nc.sbuf_tensor(f"sqkst{i}", [128, T], BF16)) for i in range(2)]
            sq = [pha.enter_context(nc.sbuf_tensor(f"sq{i}", [128, T], BF16)) for i in range(2)]
            sk = [pha.enter_context(nc.sbuf_tensor(f"sk{i}", [128, T], BF16)) for i in range(2)]
            spad = [pha.enter_context(nc.sbuf_tensor(f"spad{i}", [128, PADW + 1], BF16)) for i in range(2)]
            sshear = [pha.enter_context(nc.sbuf_tensor(f"sshear{i}", [128, 256], BF16)) for i in range(2)]
            sA = pha.enter_context(nc.sbuf_tensor("sA", [128, T], BF16))
            sAT = [pha.enter_context(nc.sbuf_tensor(f"sAT{i}", [128, 16, 512], BF16)) for i in range(2)]
            sOTst = [pha.enter_context(nc.sbuf_tensor(f"sOTst{i}", [128, 512], BF16)) for i in range(2)]
            sOTr = [pha.enter_context(nc.sbuf_tensor(f"sOTr{i}", [128, 8, 128], BF16)) for i in range(2)]
            sattst = [pha.enter_context(nc.sbuf_tensor(f"sattst{i}", [128, 512], F32)) for i in range(2)]
            sB0 = pha.enter_context(nc.sbuf_tensor("sB0", [128, 2, NQT], BF16))
            sB128 = pha.enter_context(nc.sbuf_tensor("sB128", [128, 2, NQT], BF16))
            smx4 = pha.enter_context(nc.sbuf_tensor("smx4", [128, 4], F32))
            sMc = pha.enter_context(nc.sbuf_tensor("sMc", [128, 1], F32))
            sMnc = pha.enter_context(nc.sbuf_tensor("sMnc", [128, 1], F32))
            smxB = pha.enter_context(nc.sbuf_tensor("smxB", [128, 1], F32))
            sbL = pha.enter_context(nc.sbuf_tensor("sbL", [128, 1], F32))
            sbR = pha.enter_context(nc.sbuf_tensor("sbR", [128, 1], F32))
            sacc = pha.enter_context(nc.sbuf_tensor("sacc", [128, 8], F32))
            srow = pha.enter_context(nc.sbuf_tensor("srow", [128, 1], F32))
            srec = pha.enter_context(nc.sbuf_tensor("srec", [128, 1], F32))
            ps0 = pha.enter_context(nc.psum_tensor("ps0", [128, 1024], F32))
            ps1 = pha.enter_context(nc.psum_tensor("ps1", [128, 1024], F32))
            patp = pha.enter_context(nc.psum_tensor("patp", [128, 2048], BF16))
            pav = pha.enter_context(nc.psum_tensor("pav", [128, 512], F32))
            paux = pha.enter_context(nc.psum_tensor("paux", [128, 512], F32))

            R = {}

            @block.vector
            def _(v):
                v.wait_ge(dZ.sem, dZ.n)
                dR.wait_all(v)
                dSc.wait_all(v)
                v.nop().then_inc(VA.sem, 1)
                VA.n += 1

            @block.scalar
            def _(sc):
                sc.wait_ge(VA.sem, VA.n)
                sc.nop().then_inc(SA.sem, 1)
                SA.n += 1

            @block.tensor
            def _(tn):
                tn.wait_ge(VA.sem, VA.n)
                tn.wait_ge(dG0.sem, dG0.n)
                tn.nop().then_inc(TA.sem, 1)
                TA.n += 1

            qk_tiles = [ps0[:, 0:512], ps0[:, 512:1024], ps1[:, 0:512], ps1[:, 512:1024]]

            if NHPL < NHP or BPCL < BPC:
                dZO = Sem(es, nc, "dZO")
                szatt = pha.enter_context(nc.sbuf_tensor("szatt", [128, DM], F32))

                @block.vector
                def _(v):
                    v.wait_ge(VA.sem, VA.n)
                    v.memset(sA[:], 0.0).then_inc(VA.sem, 1)
                    v.memset(szatt[:], 0.0).then_inc(VA.sem, 1)
                    VA.n += 2

                @block.sync
                def _(s):
                    s.wait_ge(VA.sem, VA.n)
                    for k8 in range(8):
                        s.dma_start(out=bass.AP(OTH, k8 * 128 * T, [[T, 128], [1, T]]),
                                    in_=sA[:]).then_inc(dZO.sem, 16)
                    dZO.n += 8 * 16
                    for k8 in range(64):
                        s.dma_start(out=bass.AP(attnH, k8 * 128 * DM, [[DM, 128], [1, DM]]),
                                    in_=szatt[:]).then_inc(dZO.sem, 16)
                    dZO.n += 64 * 16

                @block.tensor
                def _(tn):
                    tn.wait_ge(dZO.sem, dZO.n)
                    tn.nop().then_inc(TA.sem, 1)
                    TA.n += 1

            for b in range(BPCL):
                @block.sync
                def _(s, b=b):
                    if b == 0:
                        s.wait_ge(dZ.sem, dZ.n)
                        dR.wait_all(s)
                        dSc.wait_all(s)
                        dWT.wait_all(s)
                    if b >= 1:
                        s.wait_ge(TA.sem, R[("xnT_free", b - 1)])
                    for fc in range(8):
                        dXT.inc(s.dma_start(
                            out=sxnT[:, fc, :],
                            in_=bass.AP(xnTH, fc * 128 * TOK + b * T, [[TOK, 128], [1, T]])), b)

                for oc in range(16):
                    @block.tensor
                    def _(tn, b=b, oc=oc):
                        if oc == 0:
                            dXT.wait_tile(tn, b, per=8)
                            if b >= 1:
                                tn.wait_ge(VA.sem, R[("opev", b - 1, 15)])
                        if oc >= 1:
                            tn.wait_ge(SA.sem, R[("qkevac", b, oc - 1)])
                        for fc in range(8):
                            for tm in range(4):
                                tn.matmul(qk_tiles[tm], swqkv[:, fc, oc * 128:(oc + 1) * 128],
                                          sxnT[:, fc, tm * 512:(tm + 1) * 512],
                                          start=(fc == 0), stop=False)
                        for tm in range(4):
                            mm = tn.matmul(qk_tiles[tm], s_biasrow[0:1, oc * 128:(oc + 1) * 128],
                                           s_onesb[0:1, 0:512],
                                           start=False, stop=True)
                            if tm == 3:
                                mm.then_inc(TA.sem, 1)
                                TA.n += 1
                        R[("qkmm", b, oc)] = TA.n

                    @block.scalar
                    def _(sc, b=b, oc=oc):
                        sc.wait_ge(TA.sem, R[("qkmm", b, oc)])
                        gi = b * 16 + oc
                        if gi >= 2:
                            dQKw.wait_tile(sc, gi - 2)
                        scale = 0.125 if oc < 8 else 1.0
                        for tm in range(4):
                            sc.activation(sqkst[oc % 2][:, tm * 512:(tm + 1) * 512], qk_tiles[tm],
                                          AF.Copy, scale=scale).then_inc(SA.sem, 1)
                            SA.n += 1
                        R[("qkevac", b, oc)] = SA.n

                    @block.sync
                    def _(s, b=b, oc=oc):
                        s.wait_ge(SA.sem, R[("qkevac", b, oc)])
                        dQKw.inc(s.dma_start(out=bass.AP(qkH, oc * 128 * T, [[T, 128], [1, T]]),
                                             in_=sqkst[oc % 2][:]), b * 16 + oc)
                        R[("qkw_tot", b)] = (dQKw.cnt[0], dQKw.cnt[1])

                for tt in range(16):
                    @block.tensor
                    def _(tn, b=b, tt=tt):
                        if tt >= 1:
                            tn.wait_ge(SA.sem, R[("vevac", b, tt - 1)])
                        else:
                            tn.wait_ge(SA.sem, R[("qkevac", b, 15)])
                        for fc in range(8):
                            for om in range(2):
                                tn.matmul(qk_tiles[om], sxnT[:, fc, tt * 128:(tt + 1) * 128],
                                          swqkv[:, fc, 2048 + om * 512:2048 + (om + 1) * 512],
                                          start=(fc == 0), stop=False)
                        for om in range(2):
                            mm = tn.matmul(qk_tiles[om], s_onesb[0:1, 0:128],
                                           s_biasrow[0:1, 2048 + om * 512:2048 + (om + 1) * 512],
                                           start=False, stop=True)
                            if om == 1:
                                mm.then_inc(TA.sem, 1)
                                TA.n += 1
                        R[("vmm", b, tt)] = TA.n

                    @block.scalar
                    def _(sc, b=b, tt=tt):
                        sc.wait_ge(TA.sem, R[("vmm", b, tt)])
                        gv = b * 16 + tt
                        if gv >= 2:
                            dVW.wait_tile(sc, gv - 2)
                        for om in range(2):
                            sc.activation(svst[tt % 2][:, om * 512:(om + 1) * 512], qk_tiles[om],
                                          AF.Copy).then_inc(SA.sem, 1)
                            SA.n += 1
                        R[("vevac", b, tt)] = SA.n

                    @block.sync
                    def _(s, b=b, tt=tt):
                        s.wait_ge(SA.sem, R[("vevac", b, tt)])
                        if b >= 1 and tt == 0:
                            dVr.wait_all(s)
                        dVW.inc(s.dma_start(out=bass.AP(vH, tt * 128 * DM, [[DM, 128], [1, DM]]),
                                             in_=svst[tt % 2][:]), b * 16 + tt)
                        R[("vw_tot", b)] = (dVW.cnt[0], dVW.cnt[1])

                R[("xnT_free", b)] = R[("vmm", b, 15)]

                for hp in range(NHPL):
                    hpg = b * NHPL + hp

                    @block.sync
                    def _(s, b=b, hp=hp, hpg=hpg):
                        s.wait_ge(dQKw.sems[0], R[("qkw_tot", b)][0])
                        s.wait_ge(dQKw.sems[1], R[("qkw_tot", b)][1])
                        if hpg >= 2:
                            s.wait_ge(TA.sem, R[("pair_free_t", hpg - 2)])
                            s.wait_ge(SA.sem, R[("pair_free_s", hpg - 2)])
                        dQKr.inc(s.dma_start(out=sq[hpg % 2][:],
                                             in_=bass.AP(qkH, hp * 128 * T, [[T, 128], [1, T]])), hpg)
                        dQKr.inc(s.dma_start(out=sk[hpg % 2][:],
                                             in_=bass.AP(qkH, (8 + hp) * 128 * T, [[T, 128], [1, T]])), hpg)
                        s.wait_ge(dVW.sems[0], R[("vw_tot", b)][0])
                        s.wait_ge(dVW.sems[1], R[("vw_tot", b)][1])
                        if hpg >= 2:
                            s.wait_ge(TA.sem, R[("avmm", hpg - 2, 3)])
                        dVr.inc(s.dma_start(out=sVr[hpg % 2][:],
                                            in_=bass.AP(vH, hp * 128, [[DM, 128], [128 * DM, 16], [1, 128]])), hpg)

                    for hh in range(2):
                        for qt in range(NQT):
                            @block.tensor
                            def _(tn, hpg=hpg, hh=hh, qt=qt):
                                if hh == 0 and qt == 0:
                                    dQKr.wait_tile(tn, hpg, per=2)
                                    if hpg >= 1:
                                        tn.wait_ge(VA.sem, R[("aux_free", hpg - 1)])
                                        tn.wait_ge(VA.sem, R[("pair_free_v", hpg - 1)])
                                elif qt >= 1:
                                    tn.wait_ge(VA.sem, R[("padbuild", hpg, hh, qt - 1)])
                                else:
                                    tn.wait_ge(VA.sem, R[("padbuild", hpg, hh - 1, NQT - 1)])
                                tn.matmul(paux[:, 0:129], sq[hpg % 2][64 * hh:64 * hh + 64, qt * 128:(qt + 1) * 128],
                                          s_peT2[64 * hh:64 * hh + 64, :], start=True, stop=True,
                                          tile_position=(64 * hh, 0)).then_inc(TA.sem, 1)
                                TA.n += 1
                                R[("Bmm", hpg, hh, qt)] = TA.n

                            @block.vector
                            def _(v, hpg=hpg, hh=hh, qt=qt):
                                v.wait_ge(TA.sem, R[("Bmm", hpg, hh, qt)])
                                gpq = (hpg * 2 + hh) * NQT + qt
                                if gpq >= 2:
                                    dPW2.wait_tile(v, gpq - 2)
                                par = gpq % 2
                                v.tensor_copy(spad[par][:, PADL:PADL + PADM], paux[:, 0:129]).then_inc(VA.sem, 1)
                                VA.n += 1
                                v.tensor_scalar(spad[par][:, 0:PADL], s_onesb[:, 0:PADL],
                                                paux[:, 0:1], None, OP.mult).then_inc(VA.sem, 1)
                                VA.n += 1
                                v.tensor_scalar(spad[par][:, PADL + PADM:PADW],
                                                s_onesb[:, 0:PADR],
                                                paux[:, 128:129], None, OP.mult).then_inc(VA.sem, 1)
                                VA.n += 1
                                v.tensor_copy(sB0[:, hh, qt:qt + 1], paux[:, 0:1]).then_inc(VA.sem, 1)
                                v.tensor_copy(sB128[:, hh, qt:qt + 1], paux[:, 128:129]).then_inc(VA.sem, 1)
                                VA.n += 2
                                R[("padbuild", hpg, hh, qt)] = VA.n

                            @block.sync
                            def _(s, b=b, hp=hp, hpg=hpg, hh=hh, qt=qt):
                                s.wait_ge(VA.sem, R[("padbuild", hpg, hh, qt)])
                                gpq = (hpg * 2 + hh) * NQT + qt
                                dPW2.inc(s.dma_start(
                                    out=bass.AP(padH, (((hpg % 2) * 2 + hh) * NQT + qt) * 128 * (PADW + 1), [[PADW + 1, 128], [1, PADW]]),
                                    in_=spad[gpq % 2][:, 0:PADW]), gpq)

                    R[("aux_free", hpg)] = R[("padbuild", hpg, 1, NQT - 1)]

                    for qq in range(4):
                        for hh in range(2):
                          for qtl in range(4):
                            qt = qq * 4 + qtl
                            gtau = R.get("tau_ctr", 0)
                            R["tau_ctr"] = gtau + 1
                            lo = max(0, 128 * qt - 64)
                            hi = min(T, 128 * qt + 192)

                            chsegs = []
                            for c in range(4):
                                c0, c1 = c * 512, (c + 1) * 512
                                segs = []
                                for (a, b2, kind) in [(0, lo, "L"), (lo, hi, "B"), (hi, T, "R")]:
                                    aa, bb = max(a, c0), min(b2, c1)
                                    if aa < bb:
                                        segs.append((aa, bb, kind))
                                chsegs.append(segs)

                            @block.sync
                            def _(s, hpg=hpg, hh=hh, qt=qt, gtau=gtau, lo=lo, hi=hi):
                                dPW2.wait_tile(s, (hpg * 2 + hh) * NQT + qt)
                                if gtau >= 2:
                                    s.wait_ge(VA.sem, R[("banddone", gtau - 2)])
                                W = hi - lo
                                base = (((hpg % 2) * 2 + hh) * NQT + qt) * 128 * (PADW + 1) + (lo - 128 * qt + 191)
                                dSH.inc(s.dma_start(out=sshear[gtau % 2][:, 0:W],
                                                    in_=bass.AP(padH, base, [[PADW, 128], [1, W]])), gtau)

                            @block.tensor
                            def _(tn, hpg=hpg, hh=hh, qt=qt, gtau=gtau, b=b):
                                for c in range(4):
                                    if gtau >= 1:
                                        tn.wait_ge(SA.sem, R[("expc", gtau - 1, c)])
                                    if hpg == b * NHPL and hh == 0 and qt == 0 and c == 0:
                                        tn.wait_ge(SA.sem, R[("vevac", b, 15)])
                                    tn.matmul(qk_tiles[c],
                                              sq[hpg % 2][64 * hh:64 * hh + 64, qt * 128:(qt + 1) * 128],
                                              sk[hpg % 2][64 * hh:64 * hh + 64, c * 512:(c + 1) * 512],
                                              start=True, stop=True,
                                              tile_position=(64 * hh, 0)).then_inc(TA.sem, 1)
                                    TA.n += 1
                                    R[("smm", gtau, c)] = TA.n
                                if gtau >= 1:
                                    tn.wait_ge(VA.sem, R[("scaleA", gtau - 1)])
                                    if gtau >= 2:
                                        tn.wait_ge(VA.sem, R[("ATev", gtau - 2)])
                                    for jc in range(16):
                                        mm = tn.transpose(patp[:, jc * 128:(jc + 1) * 128],
                                                          sA[:, jc * 128:(jc + 1) * 128],
                                                          s_identb[:])
                                        if jc == 15:
                                            mm.then_inc(TA.sem, 1)
                                            TA.n += 1
                                    R[("ATmm", gtau - 1)] = TA.n

                            for c in range(4):
                                @block.vector
                                def _(v, hpg=hpg, gtau=gtau, c=c, chsegs=chsegs, lo=lo):
                                    v.wait_ge(TA.sem, R[("smm", gtau, c)])
                                    for (a, bb, kind) in chsegs[c]:
                                        if kind == "B":
                                            dSH.wait_tile(v, gtau)
                                            v.tensor_tensor(qk_tiles[c][:, a - c * 512:bb - c * 512],
                                                            qk_tiles[c][:, a - c * 512:bb - c * 512],
                                                            sshear[gtau % 2][:, a - lo:bb - lo],
                                                            OP.add).then_inc(VA.sem, 1)
                                            VA.n += 1
                                    v.wait_ge(VA.sem, VA.n)
                                    v.tensor_reduce(smx4[:, c:c + 1], qk_tiles[c], axis=mybir.AxisListType.X,
                                                    op=OP.max).then_inc(VA.sem, 1)
                                    VA.n += 1
                                    R[("chmax", gtau, c)] = VA.n

                            @block.vector
                            def _(v, hpg=hpg, hh=hh, qt=qt, gtau=gtau):
                                if gtau >= 1:
                                    v.wait_ge(SA.sem, R[("expdone", gtau - 1)])
                                v.wait_ge(VA.sem, VA.n)
                                v.tensor_reduce(sMc[:], smx4[:], axis=mybir.AxisListType.X,
                                                op=OP.max).then_inc(VA.sem, 1)
                                VA.n += 1
                                v.tensor_tensor(smxB[:], sB0[:, hh, qt:qt + 1], sB128[:, hh, qt:qt + 1],
                                                OP.max).then_inc(VA.sem, 1)
                                VA.n += 1
                                v.wait_ge(VA.sem, VA.n)
                                v.tensor_single_scalar(smxB[:], smxB[:], 0.0, OP.max).then_inc(VA.sem, 1)
                                VA.n += 1
                                v.wait_ge(VA.sem, VA.n)
                                v.tensor_tensor(sMc[:], sMc[:], smxB[:], OP.add).then_inc(VA.sem, 1)
                                VA.n += 1
                                v.wait_ge(VA.sem, VA.n)
                                v.tensor_single_scalar(sMnc[:], sMc[:], -1.0, OP.mult).then_inc(VA.sem, 1)
                                VA.n += 1
                                v.tensor_scalar(sbL[:], sB0[:, hh, qt:qt + 1], sMc[:], None,
                                                OP.subtract).then_inc(VA.sem, 1)
                                v.tensor_scalar(sbR[:], sB128[:, hh, qt:qt + 1], sMc[:], None,
                                                OP.subtract).then_inc(VA.sem, 1)
                                VA.n += 2
                                R[("Mdone", gtau)] = VA.n
                                R[("banddone", gtau)] = VA.n

                            @block.scalar
                            def _(sc, gtau=gtau, chsegs=chsegs):
                                sc.wait_ge(VA.sem, R[("Mdone", gtau)])
                                if gtau >= 1:
                                    sc.wait_ge(TA.sem, R[("ATmm", gtau - 1)])
                                slot = 0
                                for c in range(4):
                                    for (a, bb, kind) in chsegs[c]:
                                        bias = {"L": sbL[:], "B": sMnc[:], "R": sbR[:]}[kind]
                                        sc.activation(sA[:, a:bb], qk_tiles[c][:, a - c * 512:bb - c * 512],
                                                      AF.Exp, bias=bias,
                                                      accum_out=sacc[:, slot:slot + 1]).then_inc(SA.sem, 1)
                                        SA.n += 1
                                        slot += 1
                                    R[("expc", gtau, c)] = SA.n
                                R[("slot", gtau)] = slot
                                R[("expdone", gtau)] = SA.n

                            R[("tauinfo", gtau)] = (hh, qtl)

                            @block.vector
                            def _(v, gtau=gtau):
                                if gtau >= 1:
                                    v.wait_ge(TA.sem, R[("ATmm", gtau - 1)])
                                    hh_p, qtl_p = R[("tauinfo", gtau - 1)]
                                    v.tensor_copy(
                                        bass.AP(sAT[hh_p], qtl_p * 128, [[16 * 512, 128], [512, 16], [1, 128]]),
                                        patp[:]).then_inc(VA.sem, 1)
                                    VA.n += 1
                                    R[("ATev", gtau - 1)] = VA.n
                                v.wait_ge(SA.sem, R[("expdone", gtau)])
                                ns = R[("slot", gtau)]
                                v.tensor_reduce(srow[:], sacc[:, 0:ns], axis=mybir.AxisListType.X,
                                                op=OP.add).then_inc(VA.sem, 1)
                                VA.n += 1
                                v.wait_ge(VA.sem, VA.n)
                                v.reciprocal(srec[:], srow[:]).then_inc(VA.sem, 1)
                                VA.n += 1
                                v.wait_ge(VA.sem, VA.n)
                                v.tensor_scalar(sA[:], sA[:], srec[:], None, OP.mult).then_inc(VA.sem, 1)
                                VA.n += 1
                                R[("scaleA", gtau)] = VA.n

                        lastt = R["tau_ctr"] - 1

                        @block.tensor
                        def _(tn, lastt=lastt):
                            tn.wait_ge(VA.sem, R[("scaleA", lastt)])
                            if lastt >= 1:
                                tn.wait_ge(VA.sem, R[("ATev", lastt - 1)])
                            for jc in range(16):
                                mm = tn.transpose(patp[:, jc * 128:(jc + 1) * 128], sA[:, jc * 128:(jc + 1) * 128],
                                                  s_identb[:])
                                if jc == 15:
                                    mm.then_inc(TA.sem, 1)
                                    TA.n += 1
                            R[("ATmm", lastt)] = TA.n

                        @block.vector
                        def _(v, lastt=lastt):
                            v.wait_ge(TA.sem, R[("ATmm", lastt)])
                            hh_p, qtl_p = R[("tauinfo", lastt)]
                            v.tensor_copy(
                                bass.AP(sAT[hh_p], qtl_p * 128, [[16 * 512, 128], [512, 16], [1, 128]]),
                                patp[:]).then_inc(VA.sem, 1)
                            VA.n += 1
                            R[("ATev", lastt)] = VA.n

                        @block.tensor
                        def _(tn, hpg=hpg, hp=hp, qq=qq, b=b):
                            tn.wait_ge(VA.sem, R[("ATev", R["tau_ctr"] - 1)])
                            if qq >= 1:
                                tn.wait_ge(VA.sem, R[("otev", hpg, qq - 1)])
                            elif hpg >= 1:
                                tn.wait_ge(VA.sem, R[("otev", hpg - 1, 3)])
                            if qq == 0:
                                dVr.wait_tile(tn, hpg)
                            for jc in range(16):
                                tn.matmul(pav[0:64, :], sVr[hpg % 2][:, jc, 0:64],
                                          sAT[0][:, jc, :], start=(jc == 0), stop=(jc == 15),
                                          tile_position=(0, 0), skip_group_check=True)
                                mm = tn.matmul(pav[64:128, :], sVr[hpg % 2][:, jc, 64:128],
                                               sAT[1][:, jc, :], start=(jc == 0), stop=(jc == 15),
                                               tile_position=(0, 64), skip_group_check=True)
                                if jc == 15:
                                    mm.then_inc(TA.sem, 1)
                                    TA.n += 1
                            R[("avmm", hpg, qq)] = TA.n

                        @block.vector
                        def _(v, hpg=hpg, qq=qq):
                            v.wait_ge(TA.sem, R[("avmm", hpg, qq)])
                            gq = hpg * 4 + qq
                            if gq >= 2:
                                dOTw.wait_tile(v, gq - 2)
                            v.tensor_copy(sOTst[gq % 2][:], pav[:]).then_inc(VA.sem, 1)
                            VA.n += 1
                            R[("otev", hpg, qq)] = VA.n

                        @block.sync
                        def _(s, hpg=hpg, hp=hp, qq=qq):
                            s.wait_ge(VA.sem, R[("otev", hpg, qq)])
                            gq = hpg * 4 + qq
                            dOTw.inc(s.dma_start(out=bass.AP(OTH, hp * 128 * T + qq * 512, [[T, 128], [1, 512]]),
                                                 in_=sOTst[gq % 2][:]), gq)
                            R[("otw_tot", b)] = (dOTw.cnt[0], dOTw.cnt[1])

                    R[("pair_free_t", hpg)] = R[("avmm", hpg, 3)]
                    R[("pair_free_v", hpg)] = R[("ATev", (hpg * 2 + 1) * NQT + NQT - 1)]
                    R[("pair_free_s", hpg)] = R[("expdone", R["tau_ctr"] - 1)]

                for tt in range(16):
                    @block.sync
                    def _(s, b=b, tt=tt):
                        s.wait_ge(dOTw.sems[0], R[("otw_tot", b)][0])
                        s.wait_ge(dOTw.sems[1], R[("otw_tot", b)][1])
                        gtt = b * 16 + tt
                        if gtt >= 2:
                            pb, ptt = divmod(gtt - 2, 16)
                            s.wait_ge(TA.sem, R[("opmm", pb, ptt)])
                        dOTr.inc(s.dma_start(out=sOTr[tt % 2][:],
                                             in_=bass.AP(OTH, tt * 128, [[T, 128], [T * 128, 8], [1, 128]])), b * 16 + tt)

                    @block.tensor
                    def _(tn, b=b, tt=tt):
                        dOTr.wait_tile(tn, b * 16 + tt)
                        if tt == 0:
                            tn.wait_ge(SA.sem, R[("pair_free_s", b * NHPL + NHPL - 1)])
                        gtt2 = b * 16 + tt
                        if gtt2 >= 1:
                            pb2, ptt2 = divmod(gtt2 - 1, 16)
                            tn.wait_ge(VA.sem, R[("opev", pb2, ptt2)])
                        for om in range(2):
                            for hp2 in range(8):
                                tn.matmul(ps0[:, om * 512:(om + 1) * 512], sOTr[tt % 2][:, hp2, :],
                                          soutw[:, hp2, om * 512:(om + 1) * 512],
                                          start=(hp2 == 0), stop=False)
                            mm = tn.matmul(ps0[:, om * 512:(om + 1) * 512], s_onesb[0:1, 0:128],
                                           s_outbrow[0:1, om * 512:(om + 1) * 512],
                                           start=False, stop=True)
                            if om == 1:
                                mm.then_inc(TA.sem, 1)
                                TA.n += 1
                        R[("opmm", b, tt)] = TA.n

                    @block.vector
                    def _(v, b=b, tt=tt):
                        v.wait_ge(TA.sem, R[("opmm", b, tt)])
                        for om in range(2):
                            gw = (b * 16 + tt) * 2 + om
                            if gw >= 2:
                                dAW.wait_tile(v, gw - 2)
                            v.tensor_copy(sattst[gw % 2][:], ps0[:, om * 512:(om + 1) * 512]).then_inc(VA.sem, 1)
                            VA.n += 1
                        R[("opev", b, tt)] = VA.n

                    @block.sync
                    def _(s, b=b, tt=tt):
                        s.wait_ge(VA.sem, R[("opev", b, tt)])
                        for om in range(2):
                            gw = (b * 16 + tt) * 2 + om
                            dAW.inc(s.dma_start(
                                out=bass.AP(attnH, (b * T + tt * 128) * DM + om * 512, [[DM, 128], [1, 512]]),
                                in_=sattst[gw % 2][:]), gw)

        TF = Sem(es, nc, "TF")
        VF = Sem(es, nc, "VF")
        SF = Sem(es, nc, "SF")
        dSL = DSem2(es, nc, "dSL")
        dW1 = DSem2(es, nc, "dW1")
        dW2 = DSem2(es, nc, "dW2")
        dO2 = DSem2(es, nc, "dO2")

        if max_phase >= 5:
         with contextlib.ExitStack() as phf:
            sslab = [phf.enter_context(nc.sbuf_tensor(f"sslab{i}", [128, DM], BF16)) for i in range(2)]
            sXeT = phf.enter_context(nc.sbuf_tensor("sXeT", [128, 8, CAPL], BF16))
            shT = phf.enter_context(nc.sbuf_tensor("shT", [128, 32, CAPL], BF16))
            sw1c = [phf.enter_context(nc.sbuf_tensor(f"sw1c{i}", [128, 8, 128], BF16)) for i in range(2)]
            sw2c = [phf.enter_context(nc.sbuf_tensor(f"sw2c{i}", [128, 512], BF16)) for i in range(2)]
            svo = [phf.enter_context(nc.sbuf_tensor(f"svo{i}", [128, 512], F32)) for i in range(2)]

            RF = {}

            @block.vector
            def _(v):
                dAW.wait_all(v)
                dXT.wait_all(v)
                dQKr.wait_all(v)
                dOTr.wait_all(v)
                dSH.wait_all(v)
                dQKw.wait_all(v)
                dOTw.wait_all(v)
                dVW.wait_all(v)
                dVr.wait_all(v)
                dPW2.wait_all(v)
                v.wait_ge(TA.sem, TA.n)
                v.wait_ge(SA.sem, SA.n)
                v.wait_ge(VA.sem, VA.n)
                v.wait_ge(dZ.sem, dZ.n)
                dR.wait_all(v)
                dSc.wait_all(v)
                v.nop().then_inc(VF.sem, 1)
                VF.n += 1

            @block.tensor
            def _(tn):
                tn.wait_ge(VF.sem, VF.n)
                tn.nop().then_inc(TF.sem, 1)
                TF.n += 1

            @block.scalar
            def _(sc):
                sc.wait_ge(VF.sem, VF.n)
                sc.nop().then_inc(SF.sem, 1)
                SF.n += 1

            @block.gpsimd
            def _(g):
                g.wait_ge(VF.sem, VF.n)

            if NEL < NE:
                dZF = Sem(es, nc, "dZF")
                szffn = phf.enter_context(nc.sbuf_tensor("szffn", [128, DM], F32))

                @block.vector
                def _(v):
                    v.wait_ge(VF.sem, VF.n)
                    v.memset(szffn[:], 0.0).then_inc(VF.sem, 1)
                    VF.n += 1

                @block.sync
                def _(s):
                    s.wait_ge(VF.sem, VF.n)
                    for k8 in range(DISP_ROWS // 128):
                        s.dma_start(out=bass.AP(out2H, k8 * 128 * DM, [[DM, 128], [1, DM]]),
                                    in_=szffn[:]).then_inc(dZF.sem, 16)
                    dZF.n += (DISP_ROWS // 128) * 16

                @block.tensor
                def _(tn):
                    tn.wait_ge(dZF.sem, dZF.n)
                    tn.nop().then_inc(TF.sem, 1)
                    TF.n += 1

            for e in range(NEL):
                with contextlib.ExitStack() as pe1:
                    pXT = [pe1.enter_context(nc.psum_tensor(f"pXT{e}_{i}", [128, DM], BF16)) for i in range(2)]
                    phT = [pe1.enter_context(nc.psum_tensor(f"phT{e}_{i}", [128, CAPL], F32)) for i in range(2)]

                    NSLT = CAPL // 128
                    for t in range(NSLT):
                        ge = e * NSLT + t

                        @block.sync
                        def _(s, e=e, t=t, ge=ge):
                            if ge >= 2:
                                s.wait_ge(TF.sem, RF[("xtmm", ge - 2)])
                            dSL.inc(s.dma_start(out=sslab[t % 2][:],
                                                in_=bass.AP(dispH, (e * CAPL + t * 128) * DM, [[DM, 128], [1, DM]])), ge)

                        @block.tensor
                        def _(tn, e=e, t=t, ge=ge):
                            dSL.wait_tile(tn, ge)
                            if ge == 0:
                                pass
                            if ge >= 2:
                                tn.wait_ge(SF.sem, RF[("xtev", ge - 2)])
                            if t == 0 and e >= 1:
                                tn.wait_ge(TF.sem, RF[("hmm", e - 1)])
                            for kc in range(8):
                                mm = tn.transpose(pXT[ge % 2][:, kc * 128:(kc + 1) * 128],
                                                  sslab[t % 2][:, kc * 128:(kc + 1) * 128],
                                                  s_identb[:])
                                if kc == 7:
                                    mm.then_inc(TF.sem, 1)
                                    TF.n += 1
                            RF[("xtmm", ge)] = TF.n

                        @block.scalar
                        def _(sc, e=e, t=t, ge=ge):
                            sc.wait_ge(TF.sem, RF[("xtmm", ge)])
                            sc.activation(bass.AP(sXeT, t * 128, [[8 * CAPL, 128], [CAPL, 8], [1, 128]]),
                                          pXT[ge % 2][:], AF.Copy).then_inc(SF.sem, 1)
                            SF.n += 1
                            RF[("xtev", ge)] = SF.n

                    for Fc in range(32):
                        gf = e * 32 + Fc

                        @block.gpsimd
                        def _(g, e=e, Fc=Fc, gf=gf):
                            if gf >= 2:
                                g.wait_ge(TF.sem, RF[("hfc", gf - 2)])
                            dW1.inc(g.dma_start(
                                out=sw1c[Fc % 2][:],
                                in_=bass.AP(w1, e * DM * DFF + Fc * 128,
                                            [[DFF, 128], [128 * DFF, 8], [1, 128]])), gf)

                        @block.tensor
                        def _(tn, e=e, Fc=Fc, gf=gf):
                            dW1.wait_tile(tn, gf)
                            if Fc == 0:
                                tn.wait_ge(SF.sem, RF[("xtev", e * (CAPL // 128) + CAPL // 128 - 1)])
                            if Fc >= 2:
                                tn.wait_ge(SF.sem, RF[("hev", gf - 2)])
                            smb = [(i * 512, min((i + 1) * 512, CAPL)) for i in range((CAPL + 511) // 512)]
                            for kc in range(8):
                                for si, (sa_, sb_) in enumerate(smb):
                                    mm = tn.matmul(phT[Fc % 2][:, sa_:sb_],
                                                   sw1c[Fc % 2][:, kc, :],
                                                   sXeT[:, kc, sa_:sb_],
                                                   start=(kc == 0), stop=(kc == 7))
                                    if kc == 7 and si == len(smb) - 1:
                                        mm.then_inc(TF.sem, 1)
                                        TF.n += 1
                            RF[("hfc", gf)] = TF.n

                        @block.scalar
                        def _(sc, e=e, Fc=Fc, gf=gf):
                            sc.wait_ge(TF.sem, RF[("hfc", gf)])
                            sc.activation(shT[:, Fc, :], phT[Fc % 2][:], AF.Relu).then_inc(SF.sem, 1)
                            SF.n += 1
                            RF[("hev", gf)] = SF.n

                    RF[("hmm", e)] = RF[("hfc", e * 32 + 31)]

                with contextlib.ExitStack() as pe2:
                    pO = [pe2.enter_context(nc.psum_tensor(f"pO{e}_{i}", [128, 512], F32)) for i in range(CAPL // 128)]

                    for om in range(2):
                        for Fc in range(32):
                            gw = e * 64 + om * 32 + Fc

                            @block.gpsimd
                            def _(g, e=e, om=om, Fc=Fc, gw=gw):
                                if gw >= 2:
                                    g.wait_ge(TF.sem, RF[("o2fc", gw - 2)])
                                dW2.inc(g.dma_start(
                                    out=sw2c[Fc % 2][:],
                                    in_=bass.AP(w2, e * DFF * DM + Fc * 128 * DM + om * 512,
                                                [[DM, 128], [1, 512]])), gw)

                            @block.tensor
                            def _(tn, e=e, om=om, Fc=Fc, gw=gw):
                                dW2.wait_tile(tn, gw)
                                if Fc == 0:
                                    tn.wait_ge(SF.sem, RF[("hev", e * 32 + 31)])
                                    if om == 1:
                                        tn.wait_ge(VF.sem, RF[("o2ev", e, 0)])
                                    elif e >= 1:
                                        tn.wait_ge(VF.sem, RF[("o2ev", e - 1, 1)])
                                for st in range(CAPL // 128):
                                    mm = tn.matmul(pO[st][:], shT[:, Fc, st * 128:(st + 1) * 128],
                                                   sw2c[Fc % 2][:], start=(Fc == 0), stop=(Fc == 31))
                                    if st == CAPL // 128 - 1:
                                        mm.then_inc(TF.sem, 1)
                                        TF.n += 1
                                RF[("o2fc", gw)] = TF.n

                        @block.vector
                        def _(v, e=e, om=om):
                            v.wait_ge(TF.sem, RF[("o2fc", e * 64 + om * 32 + 31)])
                            for st in range(CAPL // 128):
                                go = (e * 2 + om) * (CAPL // 128) + st
                                if go >= 2:
                                    dO2.wait_tile(v, go - 2)
                                v.tensor_copy(svo[st % 2][:], pO[st][:]).then_inc(VF.sem, 1)
                                VF.n += 1
                                RF[("o2st", e, om, st)] = VF.n
                            RF[("o2ev", e, om)] = VF.n

                        @block.sync
                        def _(s, e=e, om=om):
                            for st in range(CAPL // 128):
                                go = (e * 2 + om) * (CAPL // 128) + st
                                s.wait_ge(VF.sem, RF[("o2st", e, om, st)])
                                dO2.inc(s.dma_start(
                                    out=bass.AP(out2H, (e * CAPL + st * 128) * DM + om * 512, [[DM, 128], [1, 512]]),
                                    in_=svo[st % 2][:]), go)

        dC = DSem2(es, nc, "dC")
        dGb = DSem2(es, nc, "dGb")
        dSaro = Sem(es, nc, "dSaro")
        dY = DSem2(es, nc, "dY")
        V4 = Sem(es, nc, "V4")
        T4 = Sem(es, nc, "T4")
        G4 = Sem(es, nc, "G4")

        if max_phase >= 6:
         with contextlib.ExitStack() as ph4:
            sxc = [ph4.enter_context(nc.sbuf_tensor(f"sxc{i}", [128, DM], F32)) for i in range(2)]
            sac = [ph4.enter_context(nc.sbuf_tensor(f"sac{i}", [128, DM], F32)) for i in range(2)]
            sfc = [ph4.enter_context(nc.sbuf_tensor(f"sfc{i}", [128, DM], F32)) for i in range(2)]
            syc = [ph4.enter_context(nc.sbuf_tensor(f"syc{i}", [128, DM], F32)) for i in range(2)]
            saro = ph4.enter_context(nc.sbuf_tensor("saro", [8, NE], F32))
            sgeff = ph4.enter_context(nc.sbuf_tensor("sgeff", [128, NT], F32))
            soffb = ph4.enter_context(nc.sbuf_tensor("soffb", [128, NE], F32))
            siota16 = ph4.enter_context(nc.sbuf_tensor("siota16", [128, NE], mybir.dt.int32))
            siota16f = ph4.enter_context(nc.sbuf_tensor("siota16f", [128, NE], F32))
            soh4 = ph4.enter_context(nc.sbuf_tensor("soh4", [128, NE], F32))
            scol = ph4.enter_context(nc.sbuf_tensor("scol", [128, 1], F32))
            scol2 = ph4.enter_context(nc.sbuf_tensor("scol2", [128, 1], F32))
            slvc = ph4.enter_context(nc.sbuf_tensor("slvc", [128, NT], F32))
            p4 = ph4.enter_context(nc.psum_tensor("p4", [1, NE], F32))

            @block.gpsimd
            def _(g):
                dO2.wait_all(g)
                g.wait_ge(TF.sem, TF.n)
                g.wait_ge(VF.sem, VF.n)
                g.wait_ge(SF.sem, SF.n)
                dW1.wait_all(g)
                dW2.wait_all(g)
                dSL.wait_all(g)
                g.wait_ge(G2.sem, G2.n)
                g.dma_start(out=saro[:], in_=aroutH.ap()).then_inc(dSaro.sem, 16)
                dSaro.n += 16
                g.iota(siota16[:], [[1, NE]], base=0, channel_multiplier=0).then_inc(G4.sem, 1)
                G4.n += 1

            @block.tensor
            def _(tn):
                tn.wait_ge(dSaro.sem, 16)
                tn.matmul(p4[:], s_prefmask[:], saro[:], start=True, stop=True).then_inc(T4.sem, 1)
                T4.n += 1

            @block.vector
            def _(v):
                dO2.wait_all(v)
                v.wait_ge(TF.sem, TF.n)
                v.wait_ge(SF.sem, SF.n)
                v.wait_ge(VF.sem, VF.n)
                v.wait_ge(T4.sem, 1)
                v.tensor_copy(s_offs[0:1, :], p4[:]).then_inc(V4.sem, 1)
                v.wait_ge(G4.sem, 1)
                v.tensor_copy(siota16f[:], siota16[:]).then_inc(V4.sem, 1)
                v.tensor_single_scalar(slvc[:], s_locc[:], float(CAPL), OP.is_lt).then_inc(V4.sem, 1)
                V4.n += 3
                v.wait_ge(V4.sem, V4.n)

            p4b = ph4.enter_context(nc.psum_tensor("p4b", [128, NE], F32))

            @block.tensor
            def _(tn):
                tn.wait_ge(V4.sem, V4.n)
                tn.matmul(p4b[:], s_onesfr[:], s_offs[0:1, :], start=True, stop=True).then_inc(T4.sem, 1)
                T4.n += 1

            @block.vector
            def _(v):
                v.wait_ge(T4.sem, T4.n)
                v.tensor_copy(soffb[:], p4b[:]).then_inc(V4.sem, 1)
                V4.n += 1

            NV4 = 7

            @block.vector
            def _(v):
                v.wait_ge(G4.sem, G4.n)
                for t in range(NT):
                    base_ = V4.n + NV4 * t
                    v.tensor_scalar(soh4[:], siota16f[:], s_idxc[:, t:t + 1], None, OP.is_equal).then_inc(V4.sem, 1)
                    v.wait_ge(V4.sem, base_ + 1)
                    v.tensor_tensor(soh4[:], soh4[:], soffb[:], OP.mult).then_inc(V4.sem, 1)
                    v.wait_ge(V4.sem, base_ + 2)
                    v.reduce_sum(scol[:], soh4[:], axis=mybir.AxisListType.X).then_inc(V4.sem, 1)
                    v.wait_ge(V4.sem, base_ + 3)
                    v.tensor_scalar(scol[:], scol[:], s_locc[:, t:t + 1], None, OP.add).then_inc(V4.sem, 1)
                    v.wait_ge(V4.sem, base_ + 4)
                    v.tensor_single_scalar(scol[:], scol[:], float(CG), OP.is_lt).then_inc(V4.sem, 1)
                    v.tensor_scalar(scol2[:], slvc[:, t:t + 1], s_gatec[:, t:t + 1], None, OP.mult).then_inc(V4.sem, 1)
                    v.wait_ge(V4.sem, base_ + 6)
                    v.tensor_tensor(sgeff[:, t:t + 1], scol[:], scol2[:], OP.mult).then_inc(V4.sem, 1)
                V4.n += NV4 * NT

            v4base = V4.n

            @block.sync
            def _(s):
                dAW.wait_all(s)
                dO2.wait_all(s)
                for t in range(NT + 1):
                    if t < NT:
                        if t >= 2:
                            s.wait_ge(V4.sem, v4base + 2 * (t - 2) + 2)
                        dC.inc(s.dma_start(out=sxc[t % 2][:], in_=bass.AP(xq, t * 128 * DM, [[DM, 128], [1, DM]])), t)
                        dC.inc(s.dma_start(out=sac[t % 2][:], in_=bass.AP(attnH, t * 128 * DM, [[DM, 128], [1, DM]])), t)
                    if t >= 1:
                        u = t - 1
                        s.wait_ge(V4.sem, v4base + 2 * u + 2)
                        dY.inc(s.dma_start(out=bass.AP(y, u * 128 * DM, [[DM, 128], [1, DM]]),
                                           in_=syc[u % 2][:]), u)

            @block.gpsimd
            def _(g):
                g.wait_ge(V4.sem, v4base)
                for t in range(NT):
                    if t >= 2:
                        g.wait_ge(V4.sem, v4base + 2 * (t - 2) + 2)
                    dGb.inc(g.indirect_dma_start(
                        out=sfc[t % 2][:], out_offset=None,
                        in_=out2H.ap(), in_offset=bass.IndirectOffsetOnAxis(ap=s_gB[:, t:t + 1], axis=0),
                    ), t)

            @block.vector
            def _(v):
                for t in range(NT):
                    dC.wait_tile(v, t, per=2)
                    dGb.wait_tile(v, t)
                    if t >= 2:
                        dY.wait_tile(v, t - 2)
                    v.scalar_tensor_tensor(syc[t % 2][:], sfc[t % 2][:], sgeff[:, t:t + 1],
                                           sxc[t % 2][:], OP.mult, OP.add).then_inc(V4.sem, 1)
                    v.wait_ge(V4.sem, v4base + 2 * t + 1)
                    v.tensor_tensor(syc[t % 2][:], syc[t % 2][:], sac[t % 2][:], OP.add).then_inc(V4.sem, 1)
                V4.n += 2 * NT

            @block.sync
            def _(s):
                dY.wait_all(s)

        if debug:
            dDbg = Sem(es, nc, "dDbg")
            dDbg2 = Sem(es, nc, "dDbg2")

            @block.sync
            def _(s):
                if max_phase >= 2:
                    s.wait_ge(V2.sem, V2.n)
                    s.dma_start(out=dbg["dbg_idx"].ap(), in_=s_idxc[:]).then_inc(dDbg.sem, 16)
                    s.dma_start(out=dbg["dbg_gate"].ap(), in_=s_gatec[:]).then_inc(dDbg.sem, 16)
                    s.dma_start(out=dbg["dbg_loc"].ap(), in_=s_locc[:]).then_inc(dDbg.sem, 16)
                    dDbg.n += 48

            @block.gpsimd
            def _(g):
                if max_phase >= 1:
                    dWb.wait_all(g)
                    dWT.wait_all(g)
                    g.dma_start(out=dbg["dbg_xnb"].ap(), in_=xnbH.ap()).then_inc(dDbg2.sem, 16)
                    g.dma_start(out=dbg["dbg_xnT"].ap(),
                                in_=bass.AP(xnTH, 0, [[TOK, 128], [1, TOK]])).then_inc(dDbg2.sem, 16)
                    dDbg2.n += 32
                if max_phase >= 3:
                    dSc.wait_all(g)
                    g.dma_start(out=dbg["dbg_disp"].ap(),
                                in_=bass.AP(dispH, 0, [[DM, 512], [1, DM]])).then_inc(dDbg2.sem, 16)
                    dDbg2.n += 16
                if max_phase >= 4:
                    dAW.wait_all(g)
                    g.dma_start(out=dbg["dbg_attn"].ap(),
                                in_=bass.AP(attnH, 0, [[DM, T], [1, DM]])).then_inc(dDbg2.sem, 16)
                    dDbg2.n += 16

            @block.sync
            def _(s):
                s.wait_ge(dDbg.sem, dDbg.n)
                s.wait_ge(dDbg2.sem, dDbg2.n)

        if max_phase < 6:
            dYF = Sem(es, nc, "dYF")
            with contextlib.ExitStack() as phz:
                szy = phz.enter_context(nc.sbuf_tensor("szy", [128, 8192], F32))

                @block.vector
                def _(v):
                    v.memset(szy[:], 0.0).then_inc(VZ.sem, 1)
                    VZ.n += 1

                @block.sync
                def _(s):
                    s.wait_ge(VZ.sem, VZ.n)
                    if max_phase >= 3:
                        dR.wait_all(s)
                        dSc.wait_all(s)
                        s.wait_ge(dZ.sem, dZ.n)
                    if max_phase >= 4:
                        dAW.wait_all(s)
                    if max_phase >= 5:
                        dO2.wait_all(s)
                        s.wait_ge(TF.sem, TF.n)
                    dWb.wait_all(s)
                    dWT.wait_all(s)
                    for k in range(8):
                        s.dma_start(out=bass.AP(y, k * 128 * 8192, [[8192, 128], [1, 8192]]),
                                    in_=szy[:]).then_inc(dYF.sem, 16)
                    s.wait_ge(dYF.sem, 8 * 16)

    return nc


def _host_consts():
    identb = np.eye(128, dtype=ml_dtypes.bfloat16)
    identf = np.eye(128, dtype=np.float32)
    triu = np.triu(np.ones((64, 64), np.float32), 1)
    sel2 = np.zeros((128, 2), ml_dtypes.bfloat16)
    sel2[:64, 0] = 1
    sel2[64:, 1] = 1
    return identb, identf, triu, sel2


def kernel(L0_attn_q, L0_attn_in_proj_weight, L0_attn_in_proj_bias, L0_attn_out_w,
           L0_attn_out_b, L0_attn_rel_pe, L0_ffn_Wg, L0_ffn_w1, L0_ffn_w2,
           _debug=False):
    x = np.ascontiguousarray(np.asarray(L0_attn_q, dtype=np.float32))
    wqkvT = np.ascontiguousarray(np.asarray(L0_attn_in_proj_weight, np.float32).T)
    bqkv = np.ascontiguousarray(np.asarray(L0_attn_in_proj_bias, np.float32))
    outwT = np.ascontiguousarray(np.asarray(L0_attn_out_w, np.float32).T)
    outb = np.ascontiguousarray(np.asarray(L0_attn_out_b, np.float32))
    peT = np.ascontiguousarray(np.asarray(L0_attn_rel_pe, np.float32).T)
    wg = np.ascontiguousarray(np.asarray(L0_ffn_Wg, np.float32))
    w1 = np.ascontiguousarray(np.asarray(L0_ffn_w1, np.float32))
    w2 = np.ascontiguousarray(np.asarray(L0_ffn_w2, np.float32))
    identb, identf, triu, sel2 = _host_consts()

    nc = build_nc(debug=_debug)
    in_maps = []
    for c in range(NCORES):
        xs = np.ascontiguousarray(x[c * BPC:(c + 1) * BPC].reshape(TOK, DM))
        corehot = np.zeros((1, 8), np.float32); corehot[0, c] = 1.0
        prefmask = np.zeros((8, 1), np.float32); prefmask[:c, 0] = 1.0
        in_maps.append({
            "xq": xs, "wqkvT": wqkvT, "bqkv": bqkv, "outwT": outwT, "outb": outb,
            "peT": peT, "wg": wg, "w1": w1, "w2": w2,
            "identb": identb, "identf": identf, "triu": triu,
            "corehot": corehot, "prefmask": prefmask, "sel2": sel2,
        })
    res = run_bass_kernel_spmd(nc, in_maps, core_ids=list(range(NCORES)))
    out = np.concatenate([r["y"].reshape(BPC, T, DM) for r in res.results], axis=0)
    if _debug:
        return out, res
    return out
